# version 01
# baseline (speedup 1.0000x reference)
import numpy as np
import ml_dtypes
from contextlib import ExitStack

import concourse.bass as bass
import concourse.mybir as mybir
from concourse.bass_utils import run_bass_kernel_spmd
from concourse.expressions import RuntimeValue

F32 = mybir.dt.float32
BF16 = mybir.dt.bfloat16
AF = mybir.ActivationFunctionType
ALU = mybir.AluOpType
AX = mybir.AxisListType.X
BF = ml_dtypes.bfloat16

NCORES = 8
B, T, C = 4, 4096, 1024
CHUNK = 512
HALF = T // 2
NCHOWN = HALF // CHUNK
LNEPS = 1e-5
LN0EPS = 1e-10


def f32c(x):
    return np.ascontiguousarray(np.asarray(x, dtype=np.float32))


class Ctr:
    def __init__(self, ctx, nc, name, step):
        self.sem = ctx.enter_context(nc.semaphore(name))
        self.n = 0
        self.step = step

    def mark(self, inst):
        self.n += self.step
        inst.then_inc(self.sem, self.step)
        return self.n


class Env:
    def __init__(self, nc, ctx):
        self.nc = nc
        self.ctx = ctx
        self.qd = Ctr(ctx, nc, "qd", 16)
        self.gd = Ctr(ctx, nc, "gd", 16)
        self.te = Ctr(ctx, nc, "te", 1)
        self.ve = Ctr(ctx, nc, "ve", 1)
        self.se = Ctr(ctx, nc, "se", 1)

    def sb(self, name, shape, dt=F32):
        return self.ctx.enter_context(self.nc.sbuf_tensor(name, shape, dt))

    def ps(self, name, shape, dt=F32):
        return self.ctx.enter_context(self.nc.psum_tensor(name, shape, dt))


def emit_postln(env, block, s_sb, st, junk_b, out_ap, eps_ap, sum_token, out2_ap=None,
                sum_eng="ve"):
    nc = env.nc

    @block.scalar
    def _(s):
        sem = env.ve.sem if sum_eng == "ve" else env.se.sem
        s.wait_ge(sem, sum_token)
        s.activation(junk_b[:, :], s_sb[:, :], AF.Square, accum_out=st[:, 1:2])
        env.se.mark(s.drain())
    sq_done = env.se.n

    return emit_ln_stats_apply(env, block, s_sb[:, :], st, out_ap, eps_ap,
                               sq_done, "se", out2_ap)


def emit_ln_stats_apply(env, block, src_ap, st, out_ap, eps_ap, sums_token,
                        sum_eng, out2_ap=None):
    @block.vector
    def _(v):
        v.wait_ge(env.se.sem if sum_eng == "se" else env.ve.sem, sums_token)
        v.tensor_scalar(st[:, 2:3], st[:, 0:1], 1.0 / C, None, ALU.mult)
        v.tensor_scalar(st[:, 3:4], st[:, 1:2], 1.0 / C, None, ALU.mult)
        v.drain()
        v.tensor_mul(st[:, 4:5], st[:, 2:3], st[:, 2:3])
        v.tensor_scalar(st[:, 8:9], st[:, 2:3], -1.0, None, ALU.mult)
        v.drain()
        v.tensor_sub(st[:, 5:6], st[:, 3:4], st[:, 4:5])
        env.ve.mark(v.drain())
    var_done = env.ve.n

    @block.scalar
    def _(s):
        s.wait_ge(env.ve.sem, var_done)
        inst = s.activation(st[:, 6:7], st[:, 5:6], AF.Sqrt, bias=eps_ap)
        env.se.mark(inst)
    sd_done = env.se.n

    @block.vector
    def _(v):
        v.wait_ge(env.se.sem, sd_done)
        v.reciprocal(st[:, 7:8], st[:, 6:7])
        v.drain()
        v.tensor_mul(st[:, 9:10], st[:, 8:9], st[:, 7:8])
        env.ve.mark(v.drain())
    nm_done = env.ve.n

    @block.scalar
    def _(s):
        s.wait_ge(env.ve.sem, nm_done)
        inst = s.activation(out_ap, src_ap, AF.Identity, bias=st[:, 9:10], scale=st[:, 7:8])
        if out2_ap is not None:
            inst = s.activation(out2_ap, src_ap, AF.Identity, bias=st[:, 9:10], scale=st[:, 7:8])
        env.se.mark(inst)
    return env.se.n


FFN_DEBUG = False


def build_ffn():
    nc = bass.Bass("TRN2", target_bir_lowering=False)
    x_d = nc.dram_tensor("x", [HALF, C], F32, kind="ExternalInput")
    hal_d = nc.dram_tensor("halos", [128, 8, NCHOWN], F32, kind="ExternalInput")
    wr_d = nc.dram_tensor("wr", [C, C], F32, kind="ExternalInput")
    wk_d = nc.dram_tensor("wk", [C, C], F32, kind="ExternalInput")
    wv_d = nc.dram_tensor("wv", [C, C], F32, kind="ExternalInput")
    mur_d = nc.dram_tensor("mur", [128, 8], F32, kind="ExternalInput")
    muk_d = nc.dram_tensor("muk", [128, 8], F32, kind="ExternalInput")
    eye_d = nc.dram_tensor("eye", [128, 128], F32, kind="ExternalInput")
    y_d = nc.dram_tensor("y", [HALF, C], F32, kind="ExternalOutput")
    if FFN_DEBUG:
        dbg = {n: nc.dram_tensor("dbg_" + n, sh, dt, kind="ExternalOutput")
               for n, sh, dt in [("xfm", [128, 8, CHUNK + 1], BF16),
                                 ("xr", [128, 8, CHUNK], BF16),
                                 ("xk", [128, 8, CHUNK], BF16),
                                 ("r", [128, C], F32),
                                 ("kact", [128, 8, CHUNK], BF16),
                                 ("s", [128, C], F32),
                                 ("st", [128, 16, 8], F32),
                                 ("sall", [128, 16, C], F32)]}

    with ExitStack() as ctx:
        env = Env(nc, ctx)
        sb, ps = env.sb, env.ps
        wr_b = sb("wr_b", [128, 8, C], BF16)
        wk_b = sb("wk_b", [128, 8, C], BF16)
        wv_b = sb("wv_b", [128, 8, C], BF16)
        mur_s = sb("mur_s", [128, 8])
        muk_s = sb("muk_s", [128, 8])
        eye_b = sb("eye_b", [128, 128], BF16)
        hal_s = sb("hal_s", [128, 8, NCHOWN])
        xres = sb("xres", [128, 4, C])
        xbf = sb("xbf", [128, 4, C], BF16)
        xfm = sb("xfm", [128, 8, CHUNK + 1], BF16)
        tmp_b = sb("tmp_b", [128, CHUNK], BF16)
        xr = sb("xr", [128, 8, CHUNK], BF16)
        xk = sb("xk", [128, 8, CHUNK], BF16)
        kact = sb("kact", [128, 8, CHUNK], BF16)
        krelu = sb("krelu", [128, 2, CHUNK])
        r_sb = sb("r_sb", [128, 4, C])
        o_sb = sb("o_sb", [128, C])
        s_sb = sb("s_sb", [128, C])
        junk_b = sb("junk_b", [128, C], BF16)
        st = sb("st", [128, 12])
        epsc = sb("epsc", [128, 1])
        ynew = sb("ynew", [128, 2, C])
        pT = [ps("pT0", [128, 1024], BF16), ps("pT1", [128, 1024], BF16)]
        pR = [ps("pR0", [128, 512]), ps("pR1", [128, 512])]
        pK = [ps("pK0", [128, 512]), ps("pK1", [128, 512])]
        pV = [ps("pV0", [128, 512]), ps("pV1", [128, 512])]
        block = ctx.enter_context(nc.Block())

        @block.gpsimd
        def _(g):
            for wsb, wd in ((wr_b, wr_d), (wk_b, wk_d), (wv_b, wv_d)):
                inst = g.dma_start(out=wsb[:, :, :],
                                   in_=wd.ap().rearrange("(k p) o -> p k o", p=128))
                env.gd.mark(inst)
            env.gd.mark(g.dma_start(out=eye_b[:, :], in_=eye_d[:, :]))
        wload_g = env.gd.n

        @block.vector
        def _(v):
            inst = v.memset(epsc[:, :], float(LNEPS))
            env.ve.mark(inst)

        @block.sync
        def _(q):
            env.qd.mark(q.dma_start(out=hal_s[:, :, :], in_=hal_d[:, :, :]))
            env.qd.mark(q.dma_start(out=mur_s[:, :], in_=mur_d[:, :]))
            env.qd.mark(q.dma_start(out=muk_s[:, :], in_=muk_d[:, :]))
        wload_q = env.qd.n

        evac_tok = [0, 0]
        prev_y_done = [0]
        sig_tok = {}
        kact_read = {}
        pv_tok = {}

        for ci in range(NCHOWN):
            t0 = ci * CHUNK

            @block.sync
            def _(q, t0=t0, ci=ci, ve_prev=env.ve.n):
                if ci > 0:
                    q.wait_ge(env.ve.sem, ve_prev)
                for tb in range(4):
                    env.qd.mark(q.dma_start(out=xres[:, tb, :],
                                            in_=x_d[t0 + tb * 128: t0 + (tb + 1) * 128, :]))
            loads_q = env.qd.n

            @block.vector
            def _(v, loads_q=loads_q, te_prev=env.te.n):
                v.wait_ge(env.qd.sem, loads_q)
                if te_prev:
                    v.wait_ge(env.te.sem, te_prev)
                for tb in range(4):
                    v.tensor_copy(xbf[:, tb, :], xres[:, tb, :])
                inst = v.tensor_copy(xfm[:, :, 0:1], hal_s[:, :, ci:ci + 1])
                env.ve.mark(inst)
            cast_done = env.ve.n

            for tb in range(4):
                @block.tensor
                def _(te, tb=tb, ci=ci):
                    if ci == 0 and tb == 0:
                        te.wait_ge(env.gd.sem, wload_g)
                    te.wait_ge(env.ve.sem, max(cast_done, evac_tok[tb % 2]))
                    for k in range(8):
                        inst = te.transpose(pT[tb % 2][:, k * 128:(k + 1) * 128],
                                            xbf[:, tb, k * 128:(k + 1) * 128], eye_b[:, :])
                    env.te.mark(inst)
                tr_done = env.te.n

                @block.vector
                def _(v, tb=tb, tr_done=tr_done):
                    v.wait_ge(env.te.sem, tr_done)
                    src = pT[tb % 2][:, :].rearrange("p (k t) -> p k t", k=8)
                    inst = v.tensor_copy(xfm[:, :, 1 + tb * 128: 1 + (tb + 1) * 128], src)
                    env.ve.mark(inst)
                evac_tok[tb % 2] = env.ve.n


            @block.vector
            def _(v, ci=ci):
                if ci == 0:
                    v.wait_ge(env.gd.sem, wload_g)
                    v.wait_ge(env.qd.sem, wload_q)
                for k in range(8):
                    v.tensor_sub(tmp_b[:, :], xfm[:, k, 1:513], xfm[:, k, 0:512])
                    v.scalar_tensor_tensor(xr[:, k, :], tmp_b[:, :], mur_s[:, k:k + 1],
                                           xfm[:, k, 0:512], ALU.mult, ALU.add)
                    inst = v.scalar_tensor_tensor(xk[:, k, :], tmp_b[:, :], muk_s[:, k:k + 1],
                                                  xfm[:, k, 0:512], ALU.mult, ALU.add)
                env.ve.mark(inst)
            mix_done = env.ve.n
            if FFN_DEBUG and ci == 0:
                @block.gpsimd
                def _(g):
                    g.wait_ge(env.ve.sem, mix_done)
                    env.gd.mark(g.dma_start(out=dbg["xfm"].ap(), in_=xfm[:, :, :]))
                    env.gd.mark(g.dma_start(out=dbg["xr"].ap(), in_=xr[:, :, :]))
                    env.gd.mark(g.dma_start(out=dbg["xk"].ap(), in_=xk[:, :, :]))

            for tb in range(4):
                for oh in range(2):
                    @block.tensor
                    def _(te, tb=tb, oh=oh):
                        te.wait_ge(env.ve.sem, mix_done)
                        prev = sig_tok.get(oh)
                        if prev:
                            te.wait_ge(env.se.sem, prev)
                        pbank = pR[oh]
                        for kf in range(8):
                            inst = te.matmul(pbank[:, :], xr[:, kf, tb * 128:(tb + 1) * 128],
                                             wr_b[:, kf, oh * 512:(oh + 1) * 512],
                                             start=(kf == 0), stop=(kf == 7))
                        env.te.mark(inst)
                    mm_done = env.te.n

                    @block.scalar
                    def _(s, tb=tb, oh=oh, mm_done=mm_done):
                        s.wait_ge(env.te.sem, mm_done)
                        inst = s.activation(r_sb[:, tb, oh * 512:(oh + 1) * 512],
                                            pR[oh][:, :], AF.Sigmoid)
                        env.se.mark(inst)
                    sig_tok[oh] = env.se.n

            for oc in range(8):
                @block.tensor
                def _(te, oc=oc):
                    if oc == 0:
                        te.wait_ge(env.ve.sem, mix_done)
                    prev = kact_read.get(oc % 2)
                    if prev:
                        te.wait_ge(env.se.sem, prev)
                    for kf in range(8):
                        inst = te.matmul(pK[oc % 2][:, :], wk_b[:, kf, oc * 128:(oc + 1) * 128],
                                         xk[:, kf, :], start=(kf == 0), stop=(kf == 7))
                    env.te.mark(inst)
                kk_done = env.te.n

                @block.vector
                def _(v, oc=oc, kk_done=kk_done):
                    v.wait_ge(env.te.sem, kk_done)
                    if oc >= 2:
                        v.wait_ge(env.se.sem, kact_read[oc % 2])
                    inst = v.tensor_scalar(krelu[:, oc % 2, :], pK[oc % 2][:, :],
                                           0.0, None, ALU.max)
                    env.ve.mark(inst)
                kr_done = env.ve.n

                @block.scalar
                def _(s, oc=oc, kr_done=kr_done):
                    s.wait_ge(env.ve.sem, kr_done)
                    inst = s.activation(kact[:, oc, :], krelu[:, oc % 2, :], AF.Square)
                    env.se.mark(inst)
                kact_read[oc % 2] = env.se.n
            kact_done = env.se.n
            if FFN_DEBUG and ci == 0:
                @block.gpsimd
                def _(g):
                    g.wait_ge(env.se.sem, kact_done)
                    env.gd.mark(g.dma_start(out=dbg["r"].ap(), in_=r_sb[:, 0, :]))
                    env.gd.mark(g.dma_start(out=dbg["kact"].ap(), in_=kact[:, :, :]))

            for tb in range(4):
                @block.tensor
                def _(te, tb=tb):
                    te.wait_ge(env.se.sem, kact_done)
                    for oh in range(2):
                        prev = pv_tok.get(oh)
                        if prev:
                            te.wait_ge(env.ve.sem, prev)
                        for oc in range(8):
                            inst = te.matmul(pV[oh][:, :], kact[:, oc, tb * 128:(tb + 1) * 128],
                                             wv_b[:, oc, oh * 512:(oh + 1) * 512],
                                             start=(oc == 0), stop=(oc == 7))
                        env.te.mark(inst)
                kv_done = env.te.n

                @block.vector
                def _(v, tb=tb, kv_done=kv_done, prev_y=prev_y_done[0]):
                    v.wait_ge(env.te.sem, kv_done)
                    if prev_y:
                        v.wait_ge(env.se.sem, prev_y)
                    for oh in range(2):
                        v.tensor_mul(o_sb[:, oh * 512:(oh + 1) * 512], pV[oh][:, :],
                                     r_sb[:, tb, oh * 512:(oh + 1) * 512])
                    v.scalar_tensor_tensor(s_sb[:, :], o_sb[:, :], 1.0, xres[:, tb, :],
                                             ALU.mult, ALU.add, accum_out=st[:, 0:1])
                    env.ve.mark(v.drain())
                pv_tok[0] = pv_tok[1] = env.ve.n
                sum_tok = env.ve.n
                if FFN_DEBUG and ci == 0 and tb == 0:
                    @block.gpsimd
                    def _(g, sum_tok=sum_tok):
                        g.wait_ge(env.ve.sem, sum_tok)
                        env.gd.mark(g.dma_start(out=dbg["s"].ap(), in_=s_sb[:, :]))

                y_done = emit_postln(env, block, s_sb, st, junk_b,
                                     ynew[:, tb % 2, :], epsc[:, 0:1], sum_tok)
                prev_y_done[0] = y_done
                if FFN_DEBUG:
                    @block.gpsimd
                    def _(g, y_done=y_done, idx=ci * 4 + tb):
                        g.wait_ge(env.se.sem, y_done)
                        env.gd.mark(g.dma_start(out=dbg["st"].ap()[:, idx, :],
                                                in_=st[:, :]))
                        env.gd.mark(g.dma_start(out=dbg["sall"].ap()[:, idx, :],
                                                in_=s_sb[:, :]))
                    @block.vector
                    def _(v):
                        v.wait_ge(env.gd.sem, env.gd.n)
                        env.ve.mark(v.engine_nop())

                @block.sync
                def _(q, ci=ci, tb=tb, y_done=y_done):
                    q.wait_ge(env.se.sem, y_done)
                    env.qd.mark(q.dma_start(
                        out=y_d[ci * CHUNK + tb * 128: ci * CHUNK + (tb + 1) * 128, :],
                        in_=ynew[:, tb % 2, :]))
                st_tok = env.qd.n

                @block.scalar
                def _(s, st_tok=st_tok):
                    if st_tok >= 32:
                        s.wait_ge(env.qd.sem, st_tok - 16)

        @block.sync
        def _(q):
            q.wait_ge(env.qd.sem, env.qd.n)
    return nc


_FFN_NC = None


def run_ffn(x_full, W_r, W_k, W_v, mu_r, mu_k):
    global _FFN_NC
    if _FFN_NC is None:
        _FFN_NC = build_ffn()
    eye = np.eye(128, dtype=np.float32)
    mur = f32c(np.asarray(mu_r).reshape(8, 128).T)
    muk = f32c(np.asarray(mu_k).reshape(8, 128).T)
    wr = f32c(np.asarray(W_r).T)
    wk = f32c(np.asarray(W_k).T)
    wv = f32c(np.asarray(W_v).T)
    in_maps = []
    for core in range(NCORES):
        b, h = core // 2, core % 2
        xo = f32c(x_full[b, h * HALF:(h + 1) * HALF])
        halos = np.zeros((128, 8, NCHOWN), np.float32)
        for ci in range(NCHOWN):
            tg = h * HALF + ci * CHUNK - 1
            if tg >= 0:
                halos[:, :, ci] = x_full[b, tg].reshape(8, 128).T
        in_maps.append(dict(x=xo, halos=halos, wr=wr, wk=wk, wv=wv,
                            mur=mur, muk=muk, eye=eye))
    res = run_bass_kernel_spmd(_FFN_NC, in_maps, core_ids=list(range(NCORES)))
    y = np.empty((B, T, C), np.float32)
    for core in range(NCORES):
        b, h = core // 2, core % 2
        y[b, h * HALF:(h + 1) * HALF] = res.results[core]["y"]
    return y


def kernel(**inputs):
    z = inputs
    x = f32c(z["L0_attn_x"])
    O = run_s1a(x, z["L0_attn_w_attn"], z["L0_attn_b_attn"])
    x = run_s1b(x, O, z["L0_attn_w_proj"], z["L0_attn_b_proj"],
                z["L0_attn_w_fc"], z["L0_attn_b_fc"],
                z["L0_attn_w_mlp_proj"], z["L0_attn_b_mlp_proj"])
    x = run_ffn(x, z["L0_ffn_W_r"], z["L0_ffn_W_k"], z["L0_ffn_W_v"],
                z["L0_ffn_mu_r"], z["L0_ffn_mu_k"])
    x = run_s3(x, z["L1_attn_Wk"], z["L1_attn_Wv"], z["L1_attn_Wq"], z["L1_attn_Wo"],
               z["L1_attn_ckw"], z["L1_attn_ckb"], z["L1_attn_cvw"], z["L1_attn_cvb"],
               z["L1_attn_cqw"], z["L1_attn_cqb"], z["L1_attn_mW1"], z["L1_attn_mW2"],
               z["L1_attn_aW"], z["L1_attn_ab"], z["L1_attn_eW"], z["L1_attn_eb"],
               z["L1_attn_tW"], z["L1_attn_tb"], z["L1_attn_norm_g"],
               z["L1_attn_gW"], z["L1_attn_gb"])
    x = run_ffn(x, z["L1_ffn_W_r"], z["L1_ffn_W_k"], z["L1_ffn_W_v"],
                z["L1_ffn_mu_r"], z["L1_ffn_mu_k"])
    return x


NH = 8
HD = 64
NSP = T // CHUNK


def build_s1a():
    nc = bass.Bass("TRN2", target_bir_lowering=False)
    x_d = nc.dram_tensor("x", [T, C], F32, kind="ExternalInput")
    wq_d = nc.dram_tensor("wq", [C, 512], F32, kind="ExternalInput")
    wk_d = nc.dram_tensor("wk", [C, 512], F32, kind="ExternalInput")
    wv_d = nc.dram_tensor("wv", [C, 512], F32, kind="ExternalInput")
    bq_d = nc.dram_tensor("bq", [128, 4], F32, kind="ExternalInput")
    bk_d = nc.dram_tensor("bk", [128, 4], F32, kind="ExternalInput")
    bv_d = nc.dram_tensor("bv", [1, 512], F32, kind="ExternalInput")
    msk_d = nc.dram_tensor("masks", [128, 4, 512], F32, kind="ExternalInput")
    eye_d = nc.dram_tensor("eye", [128, 128], F32, kind="ExternalInput")
    on2_d = nc.dram_tensor("ones2", [128, 2], F32, kind="ExternalInput")
    o_d = nc.dram_tensor("o", [NH, HD, T], BF16, kind="ExternalOutput")

    with ExitStack() as ctx:
        env = Env(nc, ctx)
        sb, ps = env.sb, env.ps
        wq_b = sb("wq_b", [128, 8, 512], BF16)
        wk_b = sb("wk_b", [128, 8, 512], BF16)
        wv_b = sb("wv_b", [128, 8, 512], BF16)
        bq_s = sb("bq_s", [128, 4])
        bk_s = sb("bk_s", [128, 4])
        bv_b = sb("bv_b", [1, 512], BF16)
        msk_s = sb("msk_s", [128, 4, 512])
        eye_b = sb("eye_b", [128, 128], BF16)
        eye_f = sb("eye_f", [128, 128])
        on2_b = sb("on2_b", [128, 2], BF16)
        onesc_f = sb("onesc_f", [1, 128])
        onesr_b = sb("onesr_b", [1, 128], BF16)
        xres = sb("xres", [128, 4, C])
        xbf = sb("xbf", [128, 4, C], BF16)
        junk_b = sb("junk_b", [128, C], BF16)
        st = sb("st", [128, 12])
        eps0 = sb("eps0", [128, 1])
        ln1_b = sb("ln1_b", [128, C], BF16)
        ln1fm = sb("ln1fm", [128, 8, CHUNK], BF16)
        q_all = sb("q_all", [128, 4, T], BF16)
        k_all = sb("k_all", [128, 4, T], BF16)
        v_all = sb("v_all", [128, 32, 4, 130], BF16)
        q2_b = sb("q2_b", [128, 512], BF16)
        ncoll = sb("ncoll", [128, 64])
        mhat = sb("mhat", [128, 1])
        m1x1 = sb("m1x1", [1, 4])
        p_sb = sb("p_sb", [128, 4, CHUNK], BF16)
        recl = sb("recl", [1, CHUNK])
        r_sb2 = sb("r_sb2", [64, CHUNK])
        onrm = sb("onrm", [64, CHUNK], BF16)
        pPool = ps("pPool", [128, 2048])
        pO = ps("pO", [128, 512])
        pR = ps("pR2", [64, 512])
        pN = ps("pN", [128, 512])
        pT = ps("pTa", [128, 1024], BF16)
        block = ctx.enter_context(nc.Block())

        @block.gpsimd
        def _(g):
            for wsb, wd in ((wq_b, wq_d), (wk_b, wk_d), (wv_b, wv_d)):
                env.gd.mark(g.dma_start(out=wsb[:, :, :],
                                        in_=wd.ap().rearrange("(f p) o -> p f o", p=128)))
            env.gd.mark(g.dma_start(out=eye_b[:, :], in_=eye_d[:, :]))
            env.gd.mark(g.dma_start(out=on2_b[:, :], in_=on2_d[:, :]))
            env.gd.mark(g.dma_start(out=bv_b[:, :], in_=bv_d[:, :]))
        wload_g = env.gd.n

        @block.sync
        def _(q):
            env.qd.mark(q.dma_start(out=bq_s[:, :], in_=bq_d[:, :]))
            env.qd.mark(q.dma_start(out=bk_s[:, :], in_=bk_d[:, :]))
            env.qd.mark(q.dma_start(out=msk_s[:, :, :], in_=msk_d[:, :, :]))
            env.qd.mark(q.dma_start(out=eye_f[:, :], in_=eye_d[:, :]))
        wload_q = env.qd.n

        @block.vector
        def _(v):
            v.wait_ge(env.gd.sem, wload_g)
            v.wait_ge(env.qd.sem, wload_q)
            v.memset(eps0[:, :], float(LN0EPS))
            v.memset(onesc_f[:, :], 1.0)
            v.memset(onesr_b[:, :], 1.0)
            v.memset(v_all[:, :, :, 64:65], 1.0)
            v.memset(v_all[:, :, :, 129:130], 1.0)
            env.ve.mark(v.drain())
        const_ve = env.ve.n

        evac_tok = [0]
        pn_free = [None]
        po_free = [0]
        q2_free = [0]
        nunit = [0]
        for sp in range(NSP):
            t0 = sp * CHUNK

            @block.sync
            def _(q, t0=t0, sp=sp, ve_prev=env.ve.n):
                if sp > 0:
                    q.wait_ge(env.ve.sem, ve_prev)
                for tb in range(4):
                    env.qd.mark(q.dma_start(out=xres[:, tb, :],
                                            in_=x_d[t0 + tb * 128: t0 + (tb + 1) * 128, :]))
            loads_q = env.qd.n

            for tb in range(4):
                @block.scalar
                def _(s, tb=tb, loads_q=loads_q, te_prev=env.te.n):
                    if tb == 0:
                        s.wait_ge(env.qd.sem, loads_q)
                        s.wait_ge(env.ve.sem, const_ve)
                        if te_prev:
                            s.wait_ge(env.te.sem, te_prev)
                    s.activation(xbf[:, tb, :], xres[:, tb, :], AF.Identity,
                                 accum_out=st[:, 0:1])
                    s.activation(junk_b[:, :], xres[:, tb, :], AF.Square,
                                 accum_out=st[:, 1:2])
                    env.se.mark(s.drain())
                sums_tok = env.se.n

                ln_done = emit_ln_stats_apply(env, block, xres[:, tb, :], st, ln1_b[:, :],
                                              eps0[:, 0:1], sums_tok, "se")

                @block.tensor
                def _(te, ln_done=ln_done, ev=evac_tok[0]):
                    te.wait_ge(env.se.sem, ln_done)
                    if ev:
                        te.wait_ge(env.ve.sem, ev)
                    for kf in range(8):
                        inst = te.transpose(pT[:, kf * 128:(kf + 1) * 128],
                                            ln1_b[:, kf * 128:(kf + 1) * 128], eye_b[:, :])
                    env.te.mark(inst)
                tr_done = env.te.n

                @block.vector
                def _(v, tb=tb, tr_done=tr_done):
                    v.wait_ge(env.te.sem, tr_done)
                    src = pT[:, :].rearrange("p (f t) -> p f t", f=8)
                    inst = v.tensor_copy(ln1fm[:, :, tb * 128:(tb + 1) * 128], src)
                    env.ve.mark(inst)
                evac_tok[0] = env.ve.n
            fm_done = env.ve.n

            for wsb, bias_s, dst in ((wq_b, bq_s, q_all), (wk_b, bk_s, k_all)):
                for oc in range(4):
                    @block.tensor
                    def _(te, oc=oc, fm_done=fm_done, pf=pn_free[0], wsb=wsb):
                        te.wait_ge(env.ve.sem, fm_done)
                        if pf:
                            te.wait_ge(env.se.sem if pf[0] == "se" else env.ve.sem, pf[1])
                        for kf in range(8):
                            inst = te.matmul(pN[:, :], wsb[:, kf, oc * 128:(oc + 1) * 128],
                                             ln1fm[:, kf, :], start=(kf == 0), stop=(kf == 7))
                        env.te.mark(inst)
                    mm_tok = env.te.n

                    @block.scalar
                    def _(s, oc=oc, sp=sp, mm_tok=mm_tok, bias_s=bias_s, dst=dst,
                          q2f=q2_free[0]):
                        s.wait_ge(env.te.sem, mm_tok)
                        inst = s.activation(dst[:, oc, sp * CHUNK:(sp + 1) * CHUNK],
                                            pN[:, :], AF.Identity, bias=bias_s[:, oc:oc + 1])
                        env.se.mark(inst)
                        if q2f:
                            s.wait_ge(env.te.sem, q2f)
                        inst = s.activation(q2_b[:, :], pN[:, :], AF.Square,
                                            bias=bias_s[:, oc:oc + 1])
                        env.se.mark(inst)
                    pn_free[0] = ("se", env.se.n)
                    sq_tok = env.se.n

                    @block.tensor
                    def _(te, sq_tok=sq_tok, pof=po_free[0]):
                        te.wait_ge(env.se.sem, sq_tok)
                        if pof:
                            te.wait_ge(env.ve.sem, pof)
                        for blk in range(4):
                            inst = te.matmul(pO[:, blk * 2: blk * 2 + 2],
                                             q2_b[:, blk * 128:(blk + 1) * 128], on2_b[:, :],
                                             start=True, stop=True)
                        env.te.mark(inst)
                    q2_free[0] = env.te.n
                    nm_tok = env.te.n

                    @block.vector
                    def _(v, nm_tok=nm_tok, u=nunit[0]):
                        v.wait_ge(env.te.sem, nm_tok)
                        inst = v.reduce_max(ncoll[:, u:u + 1], pO[:, 0:8], axis=AX)
                        env.ve.mark(inst)
                    po_free[0] = env.ve.n
                    nunit[0] += 1

            for blk in range(4):
                @block.tensor
                def _(te, blk=blk, pf=pn_free[0]):
                    if pf:
                        te.wait_ge(env.se.sem if pf[0] == "se" else env.ve.sem, pf[1])
                    for kf in range(8):
                        te.matmul(pN[:, :], ln1fm[:, kf, blk * 128:(blk + 1) * 128],
                                  wv_b[:, kf, :], start=(kf == 0), stop=False)
                    inst = te.matmul(pN[:, :], onesr_b[0:1, :], bv_b[:, :],
                                     start=False, stop=True)
                    env.te.mark(inst)
                mm_tok = env.te.n

                @block.vector
                def _(v, sp=sp, blk=blk, mm_tok=mm_tok):
                    v.wait_ge(env.te.sem, mm_tok)
                    gblk = sp * 4 + blk
                    vv = pN[:, :].rearrange("p (h two d) -> p h two d", two=2, d=64)
                    v.tensor_copy(v_all[:, gblk, :, 0:64], vv[:, :, 0, :])
                    inst = v.tensor_copy(v_all[:, gblk, :, 65:129], vv[:, :, 1, :])
                    env.ve.mark(inst)
                pn_free[0] = ("ve", env.ve.n)

        @block.vector
        def _(v, tok=env.ve.n):
            inst = v.reduce_max(st[:, 10:11], ncoll[:, :], axis=AX)
            env.ve.mark(inst)
        nmx_tok = env.ve.n

        @block.tensor
        def _(te, nmx_tok=nmx_tok):
            te.wait_ge(env.ve.sem, nmx_tok)
            inst = te.transpose(pN[0:1, 0:128], st[:, 10:11], eye_f[:, :])
            env.te.mark(inst)
        tr_tok = env.te.n

        @block.vector
        def _(v, tr_tok=tr_tok):
            v.wait_ge(env.te.sem, tr_tok)
            v.reduce_max(m1x1[0:1, 0:1], pN[0:1, 0:128], axis=AX)
            v.drain()
            inst = v.tensor_scalar(m1x1[0:1, 1:2], m1x1[0:1, 0:1], -0.125, None, ALU.mult)
            env.ve.mark(v.drain())
        neg_tok = env.ve.n

        @block.tensor
        def _(te, neg_tok=neg_tok):
            te.wait_ge(env.ve.sem, neg_tok)
            inst = te.matmul(pN[:, 0:1], onesc_f[0:1, :], m1x1[0:1, 1:2],
                             start=True, stop=True)
            env.te.mark(inst)
        bc_tok = env.te.n

        @block.vector
        def _(v, bc_tok=bc_tok):
            v.wait_ge(env.te.sem, bc_tok)
            inst = v.tensor_copy(mhat[:, :], pN[:, 0:1])
            env.ve.mark(inst)
        mhat_tok = env.ve.n

        ppool_free = [0]
        psb_free = [0]
        po_attn = [0]
        pr_free = [0]
        store_tok = [0]
        for h in range(NH):
            pair, w = h // 2, h % 2
            for sp2 in range(NSP):
                ngr = sp2 + 1
                for g in range(ngr):
                    @block.tensor
                    def _(te, h=h, sp2=sp2, g=g, pair=pair, w=w,
                          ppf=ppool_free[0], mh=mhat_tok):
                        if ppf:
                            te.wait_ge(env.se.sem, ppf)
                        if h == 0 and sp2 == 0 and g == 0:
                            te.wait_ge(env.ve.sem, mh)
                        for j in range(4):
                            kt = g * 4 + j
                            inst = te.matmul(pPool[:, j * 512:(j + 1) * 512],
                                             k_all[w * 64:(w + 1) * 64, pair,
                                                   kt * 128:(kt + 1) * 128],
                                             q_all[w * 64:(w + 1) * 64, pair,
                                                   sp2 * CHUNK:(sp2 + 1) * CHUNK],
                                             start=True, stop=True)
                        env.te.mark(inst)
                    sc_tok = env.te.n

                    if g == sp2:
                        @block.vector
                        def _(v, sc_tok=sc_tok):
                            v.wait_ge(env.te.sem, sc_tok)
                            for j in range(4):
                                inst = v.tensor_tensor(pPool[:, j * 512:(j + 1) * 512],
                                                       pPool[:, j * 512:(j + 1) * 512],
                                                       msk_s[:, j, :], ALU.add)
                            env.ve.mark(inst)
                        msk_tok = env.ve.n
                    else:
                        msk_tok = None

                    @block.scalar
                    def _(s, sc_tok=sc_tok, msk_tok=msk_tok, pvf=psb_free[0]):
                        s.wait_ge(env.te.sem, sc_tok)
                        if msk_tok:
                            s.wait_ge(env.ve.sem, msk_tok)
                        if pvf:
                            s.wait_ge(env.te.sem, pvf)
                        inst = s.activation(p_sb[:, :, :], pPool[:, :], AF.Exp,
                                            bias=mhat[:, 0:1], scale=0.125)
                        env.se.mark(inst)
                    ppool_free[0] = env.se.n
                    exp_tok = env.se.n

                    @block.tensor
                    def _(te, h=h, sp2=sp2, g=g, pair=pair, w=w, exp_tok=exp_tok,
                          poa=po_attn[0]):
                        te.wait_ge(env.se.sem, exp_tok)
                        if g == 0 and poa:
                            te.wait_ge(env.ve.sem, poa)
                        for j in range(4):
                            kt = g * 4 + j
                            inst = te.matmul(pO[0:65, :],
                                             v_all[:, kt, pair, w * 65:(w + 1) * 65],
                                             p_sb[:, j, :],
                                             start=(g == 0 and j == 0),
                                             stop=(g == sp2 and j == 3))
                        env.te.mark(inst)
                    psb_free[0] = env.te.n
                pv_tok = env.te.n

                @block.vector
                def _(v, pv_tok=pv_tok, prf=pr_free[0]):
                    v.wait_ge(env.te.sem, pv_tok)
                    inst = v.reciprocal(recl[:, :], pO[64:65, :])
                    env.ve.mark(v.drain())
                rec_tok = env.ve.n

                @block.tensor
                def _(te, rec_tok=rec_tok, prf=pr_free[0]):
                    te.wait_ge(env.ve.sem, rec_tok)
                    if prf:
                        te.wait_ge(env.ve.sem, prf)
                    inst = te.matmul(pR[:, :], onesc_f[0:1, 0:64], recl[0:1, :],
                                     start=True, stop=True)
                    env.te.mark(inst)
                rb_tok = env.te.n

                @block.vector
                def _(v, rb_tok=rb_tok, stok=store_tok[0]):
                    v.wait_ge(env.te.sem, rb_tok)
                    if stok:
                        v.wait_ge(env.qd.sem, stok)
                    v.tensor_copy(r_sb2[:, :], pR[:, :])
                    v.drain()
                    inst = v.tensor_mul(onrm[:, :], pO[0:64, :], r_sb2[:, :])
                    env.ve.mark(inst)
                pr_free[0] = env.ve.n
                po_attn[0] = env.ve.n
                on_tok = env.ve.n

                @block.sync
                def _(q, h=h, sp2=sp2, on_tok=on_tok):
                    q.wait_ge(env.ve.sem, on_tok)
                    env.qd.mark(q.dma_start(
                        out=o_d.ap()[h, :, sp2 * CHUNK:(sp2 + 1) * CHUNK],
                        in_=onrm[:, :]))
                store_tok[0] = env.qd.n

        @block.sync
        def _(q):
            q.wait_ge(env.qd.sem, env.qd.n)
    return nc


_S1A_NC = None


def s1a_inputs(x_full, w_attn, b_attn):
    w_attn = np.asarray(w_attn)
    b_attn = np.asarray(b_attn)
    eye = np.eye(128, dtype=np.float32)
    ones2 = np.zeros((128, 2), np.float32)
    ones2[0:64, 0] = 1.0
    ones2[64:128, 1] = 1.0
    masks = np.zeros((128, 4, 512), np.float32)
    r = np.arange(128)[:, None]
    cc = np.arange(512)[None, :]
    for m in range(4):
        masks[:, m, :] = np.where(m * 128 + r <= cc, 0.0, -1e9)
    in_maps = []
    for core in range(NCORES):
        b, hg = core // 2, core % 2
        rs = hg * 512
        wq = f32c(w_attn[rs:rs + 512, :].T)
        wk = f32c(w_attn[1024 + rs:1024 + rs + 512, :].T)
        wv = f32c(w_attn[2048 + rs:2048 + rs + 512, :].T)
        bq = f32c(b_attn[rs:rs + 512].reshape(4, 128).T)
        bk = f32c(b_attn[1024 + rs:1024 + rs + 512].reshape(4, 128).T)
        bv = f32c(b_attn[2048 + rs:2048 + rs + 512][None, :])
        in_maps.append(dict(x=f32c(x_full[b]), wq=wq, wk=wk, wv=wv,
                            bq=bq, bk=bk, bv=bv, masks=masks, eye=eye, ones2=ones2))
    return in_maps


def run_s1a(x_full, w_attn, b_attn):
    global _S1A_NC
    if _S1A_NC is None:
        _S1A_NC = build_s1a()
    res = run_bass_kernel_spmd(_S1A_NC, s1a_inputs(x_full, w_attn, b_attn),
                               core_ids=list(range(NCORES)))
    O = np.empty((B, 16, HD, T), BF)
    for core in range(NCORES):
        b, hg = core // 2, core % 2
        O[b, hg * 8:(hg + 1) * 8] = np.asarray(res.results[core]["o"], dtype=BF)
    return O


S1B_PHASES = 3
S1B_D1MASK = 15
S1B_DBGX1 = False
S1B_LOAD_WBIG = True
S1B_NSP = NCHOWN


def build_s1b():
    nc = bass.Bass("TRN2", target_bir_lowering=False)
    x_d = nc.dram_tensor("x", [HALF, C], F32, kind="ExternalInput")
    o_in = nc.dram_tensor("oin", [16, HD, HALF], BF16, kind="ExternalInput")
    wp_d = nc.dram_tensor("wp", [C, C], F32, kind="ExternalInput")
    bp_d = nc.dram_tensor("bp", [1, C], F32, kind="ExternalInput")
    wf_d = nc.dram_tensor("wf", [C, 4 * C], F32, kind="ExternalInput")
    bf_d = nc.dram_tensor("bfc", [128, 32], F32, kind="ExternalInput")
    wm_d = nc.dram_tensor("wm", [4 * C, C], F32, kind="ExternalInput")
    bm_d = nc.dram_tensor("bm", [1, C], F32, kind="ExternalInput")
    eye_d = nc.dram_tensor("eye", [128, 128], F32, kind="ExternalInput")
    y_d = nc.dram_tensor("y", [HALF, C], F32, kind="ExternalOutput")
    ln2sc = nc.dram_tensor("ln2sc", [NCHOWN, 128, 8, CHUNK], BF16)
    x1sc = nc.dram_tensor("x1sc", [NCHOWN, 4, 128, C], F32)
    hsc = nc.dram_tensor("hsc", [NCHOWN, 32, 128, CHUNK], BF16)

    with ExitStack() as ctx:
        env = Env(nc, ctx)
        sb, ps = env.sb, env.ps
        wp_b = sb("wp_b", [128, 8, C], BF16)
        bp_b = sb("bp_b", [1, C], BF16)
        bm_b = sb("bm_b", [1, C], BF16)
        bf_s = sb("bf_s", [128, 32])
        wbig = sb("wbig", [128, 32768], BF16)
        eye_b = sb("eye_b", [128, 128], BF16)
        onesr_b = sb("onesr_b", [1, 128], BF16)
        o_sb = sb("o_sb", [128, 8, CHUNK], BF16)
        xres = sb("xres", [128, C])
        x1_sb = sb("x1_sb", [128, C])
        t1_sb = sb("t1_sb", [128, C])
        junk_b = sb("junk_b", [128, C], BF16)
        st = sb("st", [128, 12])
        eps0 = sb("eps0", [128, 1])
        epsl = sb("epsl", [128, 1])
        ln2_b = sb("ln2_b", [128, C], BF16)
        ln2fm = sb("ln2fm", [128, 8, CHUNK], BF16)
        h_sb = sb("h_sb", [128, 32, CHUNK], BF16)
        hfc = sb("hfc", [128, 2, CHUNK], BF16)
        ynew = sb("ynew", [128, 2, C])
        pXt = [ps("pX0", [128, 512]), ps("pX1", [128, 512]),
               ps("pX2", [128, 512]), ps("pX3", [128, 512])]
        pX = [pXt[0:2], pXt[2:4]]
        pF = [ps("pF0", [128, 512]), ps("pF1", [128, 512])]
        pT = ps("pTb", [128, 1024], BF16)
        block = ctx.enter_context(nc.Block())
        wf_v = wbig[:, :].rearrange("p (f o) -> p f o", f=8)
        wm_v = wbig[:, :].rearrange("p (h o) -> p h o", h=32)

        @block.gpsimd
        def _(g):
            env.gd.mark(g.dma_start(out=wp_b[:, :, :],
                                    in_=wp_d.ap().rearrange("(f p) o -> p f o", p=128)))
            env.gd.mark(g.dma_start(out=bp_b[:, :], in_=bp_d[:, :]))
            env.gd.mark(g.dma_start(out=bm_b[:, :], in_=bm_d[:, :]))
            env.gd.mark(g.dma_start(out=eye_b[:, :], in_=eye_d[:, :]))
            if S1B_LOAD_WBIG:
                env.gd.mark(g.dma_start(out=wbig[:, :].rearrange("p (f o) -> p f o", f=8),
                                        in_=wf_d.ap().rearrange("(f p) o -> p f o", p=128)))
        wload_g = env.gd.n

        @block.sync
        def _(q):
            env.qd.mark(q.dma_start(out=bf_s[:, :], in_=bf_d[:, :]))
        wload_q = env.qd.n

        @block.vector
        def _(v):
            v.wait_ge(env.gd.sem, wload_g)
            v.wait_ge(env.qd.sem, wload_q)
            v.memset(eps0[:, :], float(LN0EPS))
            v.memset(epsl[:, :], float(LNEPS))
            v.memset(onesr_b[:, :], 1.0)
            env.ve.mark(v.drain())
        const_ve = env.ve.n

        evac_tok = [0]
        px_free = {}
        x1st_tok = [0]
        ln2st_tok = [0]
        for sp in range(S1B_NSP):
            @block.sync
            def _(q, sp=sp, ve_prev=env.ve.n):
                if sp > 0:
                    q.wait_ge(env.ve.sem, ve_prev)
                src = o_in.ap()[:, :, sp * CHUNK:(sp + 1) * CHUNK]
                env.qd.mark(q.dma_start(out=o_sb[:, :, :],
                                        in_=src.rearrange("(pr two) d t -> (two d) pr t", two=2)))
            oload_q = env.qd.n

            for tb in range(4):
                @block.sync
                def _(q, sp=sp, tb=tb, se_prev=env.se.n):
                    if se_prev:
                        q.wait_ge(env.se.sem, se_prev)
                    env.qd.mark(q.dma_start(
                        out=xres[:, :],
                        in_=x_d[sp * CHUNK + tb * 128: sp * CHUNK + (tb + 1) * 128, :]))
                xload_q = env.qd.n

                @block.tensor
                def _(te, sp=sp, tb=tb, oload_q=oload_q, cv=const_ve,
                      pxf=px_free.get(0), xl=xload_q):
                    if not (S1B_D1MASK & 1):
                        return
                    if tb == 0:
                        te.wait_ge(env.qd.sem, oload_q)
                        te.wait_ge(env.ve.sem, cv)
                    if pxf:
                        te.wait_ge(env.ve.sem, pxf)
                    for ot in range(2):
                        for p in range(8):
                            te.matmul(pX[0][ot][:, :],
                                      o_sb[:, p, tb * 128:(tb + 1) * 128],
                                      wp_b[:, p, ot * 512:(ot + 1) * 512],
                                      start=(p == 0), stop=False)
                        inst = te.matmul(pX[0][ot][:, :], onesr_b[0:1, :],
                                         bp_b[0:1, ot * 512:(ot + 1) * 512],
                                         start=False, stop=True)
                    env.te.mark(inst)
                mm_tok = env.te.n

                @block.vector
                def _(v, mm_tok=mm_tok, xl=xload_q, x1st=x1st_tok[0]):
                    if not (S1B_D1MASK & 2):
                        v.wait_ge(env.te.sem, mm_tok)
                        env.ve.mark(v.tensor_copy(x1_sb[:, :], xres[:, :]))
                        return
                    v.wait_ge(env.te.sem, mm_tok)
                    v.wait_ge(env.qd.sem, xl)
                    if x1st:
                        v.wait_ge(env.gd.sem, x1st)
                    v.tensor_add(x1_sb[:, 0:512], pX[0][0][:, :], xres[:, 0:512])
                    v.tensor_add(x1_sb[:, 512:1024], pX[0][1][:, :], xres[:, 512:1024])
                    env.ve.mark(v.drain())
                px_free[0] = env.ve.n
                sum_tok = env.ve.n

                if S1B_D1MASK & 2:
                    @block.scalar
                    def _(s, sum_tok=sum_tok):
                        s.wait_ge(env.ve.sem, sum_tok)
                        s.activation(junk_b[:, :], x1_sb[:, :], AF.Identity, accum_out=st[:, 0:1])
                        s.activation(junk_b[:, :], x1_sb[:, :], AF.Square, accum_out=st[:, 1:2])
                        env.se.mark(s.drain())
                    sq_tok = env.se.n

                    ln_done = emit_ln_stats_apply(env, block, x1_sb[:, :], st, ln2_b[:, :],
                                                  eps0[:, 0:1], sq_tok, "se")
                else:
                    @block.scalar
                    def _(s, sum_tok=sum_tok):
                        s.wait_ge(env.ve.sem, sum_tok)
                        env.se.mark(s.copy(ln2_b[:, :], x1_sb[:, :]))
                    ln_done = env.se.n

                if S1B_D1MASK & 4:
                    @block.gpsimd
                    def _(g, sp=sp, tb=tb, ln_done=ln_done):
                        g.wait_ge(env.se.sem, ln_done)
                        env.gd.mark(g.dma_start(out=x1sc.ap()[sp, tb, :, :], in_=x1_sb[:, :]))
                        if S1B_DBGX1:
                            env.gd.mark(g.dma_start(
                                out=y_d[sp * CHUNK + tb * 128: sp * CHUNK + (tb + 1) * 128, :],
                                in_=x1_sb[:, :]))
                    x1st_tok[0] = env.gd.n

                if not (S1B_D1MASK & 8):
                    continue
                @block.tensor
                def _(te, ln_done=ln_done, ev=evac_tok[0]):
                    te.wait_ge(env.se.sem, ln_done)
                    if ev:
                        te.wait_ge(env.ve.sem, ev)
                    for kf in range(8):
                        inst = te.transpose(pT[:, kf * 128:(kf + 1) * 128],
                                            ln2_b[:, kf * 128:(kf + 1) * 128], eye_b[:, :])
                    env.te.mark(inst)
                tr_done = env.te.n

                @block.vector
                def _(v, tb=tb, tr_done=tr_done, lst=ln2st_tok[0]):
                    v.wait_ge(env.te.sem, tr_done)
                    if tb == 0 and lst:
                        v.wait_ge(env.gd.sem, lst)
                    src = pT[:, :].rearrange("p (f t) -> p f t", f=8)
                    inst = v.tensor_copy(ln2fm[:, :, tb * 128:(tb + 1) * 128], src)
                    env.ve.mark(inst)
                evac_tok[0] = env.ve.n
            fm_done = env.ve.n

            if S1B_D1MASK & 8:
                @block.gpsimd
                def _(g, sp=sp, fm_done=fm_done):
                    g.wait_ge(env.ve.sem, fm_done)
                    env.gd.mark(g.dma_start(out=ln2sc.ap()[sp, :, :, :], in_=ln2fm[:, :, :]))
                ln2st_tok[0] = env.gd.n
        d1_g = env.gd.n
        d1_ve = env.ve.n

        if S1B_PHASES < 2:
            @block.sync
            def _(q):
                q.wait_ge(env.gd.sem, env.gd.n)
                q.wait_ge(env.qd.sem, env.qd.n)
            return nc
        pf_free = {}
        for sp in range(NCHOWN):
            @block.sync
            def _(q, sp=sp, d1_g=d1_g, te_prev=env.te.n):
                if sp == 0:
                    q.wait_ge(env.gd.sem, d1_g)
                else:
                    q.wait_ge(env.te.sem, te_prev)
                env.qd.mark(q.dma_start(out=ln2fm[:, :, :], in_=ln2sc.ap()[sp, :, :, :]))
            lload_q = env.qd.n

            for hc in range(32):
                @block.tensor
                def _(te, hc=hc, lload_q=lload_q, pff=pf_free.get(hc % 2)):
                    if hc == 0:
                        te.wait_ge(env.qd.sem, lload_q)
                    if pff:
                        te.wait_ge(env.se.sem, pff)
                    for kf in range(8):
                        inst = te.matmul(pF[hc % 2][:, :], wf_v[:, kf, hc * 128:(hc + 1) * 128],
                                         ln2fm[:, kf, :], start=(kf == 0), stop=(kf == 7))
                    env.te.mark(inst)
                mm_tok = env.te.n

                @block.scalar
                def _(s, hc=hc, mm_tok=mm_tok, gst=env.gd.n):
                    s.wait_ge(env.te.sem, mm_tok)
                    if gst >= 32:
                        s.wait_ge(env.gd.sem, gst - 16)
                    inst = s.activation(hfc[:, hc % 2, :], pF[hc % 2][:, :],
                                        AF.Gelu_apprx_tanh, bias=bf_s[:, hc:hc + 1])
                    env.se.mark(inst)
                pf_free[hc % 2] = env.se.n
                g_tok = env.se.n

                @block.gpsimd
                def _(g, sp=sp, hc=hc, g_tok=g_tok):
                    g.wait_ge(env.se.sem, g_tok)
                    env.gd.mark(g.dma_start(out=hsc.ap()[sp, hc, :, :],
                                            in_=hfc[:, hc % 2, :]))
        d2_g = env.gd.n

        if S1B_PHASES < 3:
            @block.sync
            def _(q):
                q.wait_ge(env.gd.sem, env.gd.n)
                q.wait_ge(env.qd.sem, env.qd.n)
            return nc
        @block.gpsimd
        def _(g, d2_g=d2_g, te_prev=env.te.n):
            g.wait_ge(env.gd.sem, d2_g)
            g.wait_ge(env.te.sem, te_prev)
            env.gd.mark(g.dma_start(out=wbig[:, :].rearrange("p (h o) -> p h o", h=32),
                                    in_=wm_d.ap().rearrange("(h p) o -> p h o", p=128)))
        wm_g = env.gd.n

        px_free2 = {}
        ylast = [0]
        for sp in range(NCHOWN):
            @block.sync
            def _(q, sp=sp, wm_g=wm_g, te_prev=env.te.n):
                if sp == 0:
                    q.wait_ge(env.gd.sem, wm_g)
                else:
                    q.wait_ge(env.te.sem, te_prev)
                env.qd.mark(q.dma_start(out=h_sb[:, :, :],
                                        in_=hsc.ap()[sp].rearrange("h p t -> p h t")))
            hload_q = env.qd.n

            for tb in range(4):
                @block.sync
                def _(q, sp=sp, tb=tb, ve_prev=env.ve.n):
                    if ve_prev:
                        q.wait_ge(env.ve.sem, ve_prev)
                    env.qd.mark(q.dma_start(
                        out=xres[:, :],
                        in_=x_d[sp * CHUNK + tb * 128: sp * CHUNK + (tb + 1) * 128, :]))
                    env.qd.mark(q.dma_start(out=x1_sb[:, :], in_=x1sc.ap()[sp, tb, :, :]))
                xl_q = env.qd.n

                @block.tensor
                def _(te, tb=tb, hload_q=hload_q, pxf=px_free2.get(tb % 2)):
                    if tb == 0:
                        te.wait_ge(env.qd.sem, hload_q)
                    if pxf:
                        te.wait_ge(env.ve.sem, pxf)
                    for ot in range(2):
                        for hc in range(32):
                            te.matmul(pX[tb % 2][ot][:, :],
                                      h_sb[:, hc, tb * 128:(tb + 1) * 128],
                                      wm_v[:, hc, ot * 512:(ot + 1) * 512],
                                      start=(hc == 0), stop=False)
                        inst = te.matmul(pX[tb % 2][ot][:, :], onesr_b[0:1, :],
                                         bm_b[0:1, ot * 512:(ot + 1) * 512],
                                         start=False, stop=True)
                    env.te.mark(inst)
                mm_tok = env.te.n

                @block.vector
                def _(v, tb=tb, mm_tok=mm_tok, xl=xl_q, yl=ylast[0]):
                    v.wait_ge(env.te.sem, mm_tok)
                    v.wait_ge(env.qd.sem, xl)
                    if yl:
                        v.wait_ge(env.se.sem, yl)
                    v.tensor_add(t1_sb[:, 0:512], pX[tb % 2][0][:, :], x1_sb[:, 0:512])
                    v.tensor_add(t1_sb[:, 512:1024], pX[tb % 2][1][:, :],
                                 x1_sb[:, 512:1024])
                    v.drain()
                    v.scalar_tensor_tensor(t1_sb[:, :], t1_sb[:, :], 1.0, xres[:, :],
                                           ALU.mult, ALU.add, accum_out=st[:, 0:1])
                    env.ve.mark(v.drain())
                px_free2[tb % 2] = env.ve.n
                sum_tok = env.ve.n

                @block.scalar
                def _(s, sum_tok=sum_tok):
                    s.wait_ge(env.ve.sem, sum_tok)
                    s.activation(junk_b[:, :], t1_sb[:, :], AF.Square, accum_out=st[:, 1:2])
                    env.se.mark(s.drain())
                sq_tok = env.se.n

                y_done = emit_ln_stats_apply(env, block, t1_sb[:, :], st,
                                             ynew[:, tb % 2, :], epsl[:, 0:1], sq_tok, "se")
                ylast[0] = y_done

                @block.sync
                def _(q, sp=sp, tb=tb, y_done=y_done):
                    q.wait_ge(env.se.sem, y_done)
                    env.qd.mark(q.dma_start(
                        out=y_d[sp * CHUNK + tb * 128: sp * CHUNK + (tb + 1) * 128, :],
                        in_=ynew[:, tb % 2, :]))
                yst = env.qd.n

                @block.scalar
                def _(s, yst=yst):
                    s.wait_ge(env.qd.sem, yst - 16)

        @block.sync
        def _(q):
            q.wait_ge(env.qd.sem, env.qd.n)
    return nc


_S1B_NC = None


def run_s1b(x_full, O_fm, w_proj, b_proj, w_fc, b_fc, w_mlp, b_mlp):
    global _S1B_NC
    if _S1B_NC is None:
        _S1B_NC = build_s1b()
    eye = np.eye(128, dtype=np.float32)
    wp = f32c(np.asarray(w_proj).T)
    bp = f32c(np.asarray(b_proj)[None, :])
    wf = f32c(np.asarray(w_fc).T)
    bfc = f32c(np.asarray(b_fc).reshape(32, 128).T)
    wm = f32c(np.asarray(w_mlp).T)
    bm = f32c(np.asarray(b_mlp)[None, :])
    in_maps = []
    for core in range(NCORES):
        b, hf = core // 2, core % 2
        in_maps.append(dict(
            x=f32c(x_full[b, hf * HALF:(hf + 1) * HALF]),
            oin=np.ascontiguousarray(O_fm[b][:, :, hf * HALF:(hf + 1) * HALF]),
            wp=wp, bp=bp, wf=wf, bfc=bfc, wm=wm, bm=bm, eye=eye))
    res = run_bass_kernel_spmd(_S1B_NC, in_maps, core_ids=list(range(NCORES)))
    y = np.empty((B, T, C), np.float32)
    for core in range(NCORES):
        b, hf = core // 2, core % 2
        y[b, hf * HALF:(hf + 1) * HALF] = res.results[core]["y"]
    return y


DD = 256
NSUB = 128
GS3 = 0.1
EPS3 = 1e-6


S3_PHASES = 4
S3_DEBUG = False


def build_s3():
    nc = bass.Bass("TRN2", target_bir_lowering=False)
    x_d = nc.dram_tensor("x", [T, C], F32, kind="ExternalInput")
    wkvq_d = nc.dram_tensor("wkvq", [C, 768], F32, kind="ExternalInput")
    waet_d = nc.dram_tensor("waet", [C, 768], F32, kind="ExternalInput")
    baet_d = nc.dram_tensor("baet", [128, 6], F32, kind="ExternalInput")
    dwc_d = nc.dram_tensor("dwc", [128, 6, 4], F32, kind="ExternalInput")
    dwb_d = nc.dram_tensor("dwb", [128, 6], F32, kind="ExternalInput")
    mw1_d = nc.dram_tensor("mw1", [DD, DD], F32, kind="ExternalInput")
    mw2_d = nc.dram_tensor("mw2", [DD, DD], F32, kind="ExternalInput")
    wo_d = nc.dram_tensor("wo", [DD, C], F32, kind="ExternalInput")
    gw_d = nc.dram_tensor("gw", [C, C], F32, kind="ExternalInput")
    gb_d = nc.dram_tensor("gb", [1, C], F32, kind="ExternalInput")
    ng_d = nc.dram_tensor("ng", [1, C], F32, kind="ExternalInput")
    eye_d = nc.dram_tensor("eye", [128, 128], F32, kind="ExternalInput")
    triu_d = nc.dram_tensor("triu", [128, 128], F32, kind="ExternalInput")
    tril_d = nc.dram_tensor("tril", [128, 128], F32, kind="ExternalInput")
    sel_d = nc.dram_tensor("sel", [128, 4], F32, kind="ExternalInput")
    y_d = nc.dram_tensor("y", [T, C], F32, kind="ExternalOutput")
    dbgkind = dict(kind="ExternalOutput") if S3_DEBUG else {}
    praw_d = nc.dram_tensor("praw", [6, 128, T], BF16, **dbgkind)
    kdw_d = nc.dram_tensor("kdw", [6, 128, T], BF16, **dbgkind)
    retr_d = nc.dram_tensor("retr", [32 * 128, DD], F32, **dbgkind)
    if S3_DEBUG:
        dbgst = {n: nc.dram_tensor("dbg_" + n, [128, 2, DD], F32, kind="ExternalOutput")
                 for n in ("w1", "w2", "s1", "s2")}
        dbgco = nc.dram_tensor("dbg_co", [128, 640], F32, kind="ExternalOutput")

    with ExitStack() as ctx:
        env = Env(nc, ctx)
        sb, ps = env.sb, env.ps
        wkvq_b = sb("wkvq_b", [128, 8, 768], BF16)
        waet_b = sb("waet_b", [128, 8, 768], BF16)
        baet_s = sb("baet_s", [128, 6])
        dwc_s = sb("dwc_s", [128, 6, 4])
        dwb_s = sb("dwb_s", [128, 6])
        wo_b = sb("wo_b", [128, 2, C], BF16)
        gw_b = sb("gw_b", [128, 8, C], BF16)
        gb_b = sb("gb_b", [1, C], BF16)
        ng_s = sb("ng_s", [1, C])
        eye_b = sb("eye_b", [128, 128], BF16)
        eye_f = sb("eye_f", [128, 128])
        triu_f = sb("triu_f", [128, 128])
        tril_f = sb("tril_f", [128, 128])
        sel_f = sb("sel_f", [128, 4])
        onesr_b = sb("onesr_b", [1, 128], BF16)
        onesr_f = sb("onesr_f", [1, 128])
        xres = sb("xres", [128, 4, C])
        xbf = sb("xbf", [128, 4, C], BF16)
        xfm = sb("xfm", [128, 8, CHUNK], BF16)
        junk_b = sb("junk_b", [128, C], BF16)
        st = sb("st", [128, 12])
        epsl = sb("epsl", [128, 1])
        eps3c = sb("eps3c", [128, 1])
        scal = sb("scal", [128, 32, 4])
        ubig = sb("ubig", [128, 2 * T])
        sig_s = ubig[:, 0:1024].rearrange("p (a b) -> p a b", a=2)
        sp_s = ubig[:, 1024:2048].rearrange("p (a b) -> p a b", a=2)
        cacc = [ubig[:, 0:T], ubig[:, T:2 * T]]
        crow = sb("crow", [128, T], BF16)
        w1t = sb("w1t", [128, 2, DD]); w2t = sb("w2t", [128, 2, DD])
        s1t = sb("s1t", [128, 2, DD]); s2t = sb("s2t", [128, 2, DD])
        w1b = sb("w1b", [128, 2, DD], BF16); w2b = sb("w2b", [128, 2, DD], BF16)
        s1b = sb("s1b", [128, 2, DD], BF16); s2b = sb("s2b", [128, 2, DD], BF16)
        w2pb = sb("w2pb", [128, 2, DD], BF16)
        kc_b = sb("kc_b", [128, 2, CHUNK], BF16)
        vc_b = sb("vc_b", [128, 2, CHUNK], BF16)
        qc_b = sb("qc_b", [128, 2, CHUNK], BF16)
        afm = sb("afm", [128, 2, CHUNK], BF16)
        dsfm = sb("dsfm", [128, 2, CHUNK])
        efm = sb("efm", [128, 2, CHUNK], BF16)
        ufm = sb("ufm", [128, 2, CHUNK], BF16)
        n2 = sb("n2", [128, 2, CHUNK], BF16)
        wthc = sb("wthc", [128, 4])
        gncol = sb("gncol", [128, 24])
        clarow = sb("clarow", [1, 128]); clerow = sb("clerow", [1, 128])
        lacol = sb("lacol", [128, 6])
        expin = sb("expin", [128, 128])
        m1t_f = sb("m1t_f", [128, 128]); m2t_f = sb("m2t_f", [128, 128])
        dt_sb = sb("dt_sb", [128, 128])
        acol = sb("acol", [128, 4])
        arow = sb("arow", [1, 128])
        lasts = sb("lasts", [128, 4])
        eyeA = sb("eyeA", [128, 128], BF16)
        eyeB = sb("eyeB", [128, 128], BF16)
        eyeE = sb("eyeE", [128, 128], BF16)
        ktm = sb("ktm", [128, DD], BF16)
        utm = sb("utm", [128, DD], BF16)
        etm = sb("etm", [128, DD], BF16)
        atm = sb("atm", [128, DD], BF16)
        kw_b = sb("kw_b", [128, DD], BF16)
        aw_b = sb("aw_b", [128, DD], BF16)
        kw2_b = sb("kw2_b", [128, DD], BF16)
        aw2_b = sb("aw2_b", [128, DD], BF16)
        hsb2 = sb("hsb2", [128, DD])
        onesc_f = sb("onesc_f", [128, 1])
        onesc_b = sb("onesc_b", [128, 1], BF16)
        negcle = sb("negcle", [1, 128])
        p1_b = sb("p1_b", [128, 128], BF16)
        p2_b = sb("p2_b", [128, 128], BF16)
        hsb = sb("hsb", [128, DD])
        hstm = sb("hstm", [128, DD], BF16)
        hsfm = sb("hsfm", [128, 2, 128], BF16)
        ott = sb("ott", [128, DD])
        retfm = sb("retfm", [128, 2, 128], BF16)
        ret_s = sb("ret_s", [128, C])
        gate_s = sb("gate_s", [128, C])
        g128 = sb("g128", [128, C])
        o3_s = sb("o3_s", [128, C])
        s_sb = sb("s_sb", [128, C])
        ynew = sb("ynew", [128, 2, C])
        dl_c = sb("dl_c", [128, 4])
        pA = ps("pA", [128, 512])
        pB = ps("pB", [128, 512])
        pC = ps("pC", [128, 512])
        pD = ps("pD", [128, 512])
        pE = ps("pE", [128, 512])
        pF2 = ps("pF2", [128, 512])
        pT = ps("pT3", [128, 1024], BF16)
        block = ctx.enter_context(nc.Block())

        @block.gpsimd
        def _(g):
            env.gd.mark(g.dma_start(out=wkvq_b[:, :, :],
                                    in_=wkvq_d.ap().rearrange("(f p) o -> p f o", p=128)))
            env.gd.mark(g.dma_start(out=waet_b[:, :, :],
                                    in_=waet_d.ap().rearrange("(f p) o -> p f o", p=128)))
            env.gd.mark(g.dma_start(out=wo_b[:, :, :],
                                    in_=wo_d.ap().rearrange("(f p) o -> p f o", p=128)))
            env.gd.mark(g.dma_start(out=gw_b[:, :, :],
                                    in_=gw_d.ap().rearrange("(f p) o -> p f o", p=128)))
            env.gd.mark(g.dma_start(out=gb_b[:, :], in_=gb_d[:, :]))
            env.gd.mark(g.dma_start(out=eye_b[:, :], in_=eye_d[:, :]))
            env.gd.mark(g.dma_start(out=w1t[:, :, :],
                                    in_=mw1_d.ap().rearrange("(c p) o -> p c o", p=128)))
            env.gd.mark(g.dma_start(out=w2t[:, :, :],
                                    in_=mw2_d.ap().rearrange("(c p) o -> p c o", p=128)))
        wload_g = env.gd.n

        @block.sync
        def _(q):
            env.qd.mark(q.dma_start(out=baet_s[:, :], in_=baet_d[:, :]))
            env.qd.mark(q.dma_start(out=dwc_s[:, :, :], in_=dwc_d[:, :, :]))
            env.qd.mark(q.dma_start(out=dwb_s[:, :], in_=dwb_d[:, :]))
            env.qd.mark(q.dma_start(out=eye_f[:, :], in_=eye_d[:, :]))
            env.qd.mark(q.dma_start(out=triu_f[:, :], in_=triu_d[:, :]))
            env.qd.mark(q.dma_start(out=tril_f[:, :], in_=tril_d[:, :]))
            env.qd.mark(q.dma_start(out=sel_f[:, :], in_=sel_d[:, :]))
            env.qd.mark(q.dma_start(out=ng_s[:, :], in_=ng_d[:, :]))
        wload_q = env.qd.n

        @block.vector
        def _(v):
            v.wait_ge(env.gd.sem, wload_g)
            v.wait_ge(env.qd.sem, wload_q)
            v.memset(epsl[:, :], float(LNEPS))
            v.memset(eps3c[:, :], float(EPS3))
            v.memset(onesr_b[:, :], 1.0)
            v.memset(onesr_f[:, :], 1.0)
            v.memset(onesc_f[:, :], 1.0)
            v.memset(onesc_b[:, :], 1.0)
            v.memset(s1t[:, :, :], 0.0)
            v.memset(s2t[:, :, :], 0.0)
            v.memset(s1b[:, :, :], 0.0)
            v.memset(s2b[:, :, :], 0.0)
            v.tensor_copy(w1b[:, :, :], w1t[:, :, :])
            inst = v.tensor_copy(w2b[:, :, :], w2t[:, :, :])
            env.ve.mark(v.drain())
        const_ve = env.ve.n

        @block.tensor
        def _(te):
            te.wait_ge(env.ve.sem, const_ve)
            ng_bf = junk_b
            te.wait_ge(env.ve.sem, const_ve)
            inst = te.matmul(pA[:, :], onesr_f[0:1, :], ng_s[0:1, 0:512], start=True, stop=True)
            inst = te.matmul(pB[:, :], onesr_f[0:1, :], ng_s[0:1, 512:1024], start=True, stop=True)
            env.te.mark(inst)
        ngb_tok = env.te.n

        @block.vector
        def _(v, ngb_tok=ngb_tok):
            v.wait_ge(env.te.sem, ngb_tok)
            v.tensor_copy(g128[:, 0:512], pA[:, :])
            inst = v.tensor_copy(g128[:, 512:1024], pB[:, :])
            env.ve.mark(v.drain())
        g128_tok = env.ve.n

        evac_tok = [0]
        pn_free = [None]
        for sp in range(NSP):
            t0 = sp * CHUNK

            @block.sync
            def _(q, t0=t0, sp=sp, ve_prev=env.ve.n):
                if sp > 0:
                    q.wait_ge(env.ve.sem, ve_prev)
                for tb in range(4):
                    env.qd.mark(q.dma_start(out=xres[:, tb, :],
                                            in_=x_d[t0 + tb * 128: t0 + (tb + 1) * 128, :]))
            loads_q = env.qd.n

            @block.vector
            def _(v, loads_q=loads_q, te_prev=env.te.n, cv=const_ve):
                v.wait_ge(env.qd.sem, loads_q)
                v.wait_ge(env.ve.sem, cv)
                if te_prev:
                    v.wait_ge(env.te.sem, te_prev)
                for tb in range(4):
                    inst = v.tensor_copy(xbf[:, tb, :], xres[:, tb, :])
                env.ve.mark(inst)
            cast_done = env.ve.n

            for tb in range(4):
                @block.tensor
                def _(te, tb=tb, cast_done=cast_done, ev=evac_tok[0]):
                    te.wait_ge(env.ve.sem, cast_done)
                    if ev:
                        te.wait_ge(env.ve.sem, ev)
                    for kf in range(8):
                        inst = te.transpose(pT[:, kf * 128:(kf + 1) * 128],
                                            xbf[:, tb, kf * 128:(kf + 1) * 128], eye_b[:, :])
                    env.te.mark(inst)
                tr_done = env.te.n

                @block.vector
                def _(v, tb=tb, tr_done=tr_done):
                    v.wait_ge(env.te.sem, tr_done)
                    src = pT[:, :].rearrange("p (f t) -> p f t", f=8)
                    inst = v.tensor_copy(xfm[:, :, tb * 128:(tb + 1) * 128], src)
                    env.ve.mark(inst)
                evac_tok[0] = env.ve.n
            fm_done = env.ve.n

            for oc in range(6):
                @block.tensor
                def _(te, oc=oc, fm_done=fm_done, pf=pn_free[0]):
                    te.wait_ge(env.ve.sem, fm_done)
                    if pf:
                        te.wait_ge(env.se.sem if pf[0] == "se" else env.ve.sem, pf[1])
                    for kf in range(8):
                        inst = te.matmul(pA[:, :], wkvq_b[:, kf, oc * 128:(oc + 1) * 128],
                                         xfm[:, kf, :], start=(kf == 0), stop=(kf == 7))
                    env.te.mark(inst)
                mm_tok = env.te.n

                @block.scalar
                def _(s, oc=oc, sp=sp, mm_tok=mm_tok, gprev=env.gd.n):
                    s.wait_ge(env.te.sem, mm_tok)
                    if gprev >= 16 * 10:
                        s.wait_ge(env.gd.sem, gprev - 16)
                    inst = s.activation(crow[:, sp * CHUNK:(sp + 1) * CHUNK], pA[:, :],
                                        AF.Identity)
                    env.se.mark(inst)
                pn_free[0] = ("se", env.se.n)
                ev_tok = env.se.n

                @block.gpsimd
                def _(g, oc=oc, sp=sp, ev_tok=ev_tok):
                    g.wait_ge(env.se.sem, ev_tok)
                    env.gd.mark(g.dma_start(out=praw_d.ap()[oc, :, sp * CHUNK:(sp + 1) * CHUNK],
                                            in_=crow[:, sp * CHUNK:(sp + 1) * CHUNK]))

            for pi in range(3):
                for h2 in range(2):
                    oc = pi * 2 + h2

                    @block.tensor
                    def _(te, oc=oc, fm_done=fm_done, pf=pn_free[0]):
                        te.wait_ge(env.ve.sem, fm_done)
                        if pf:
                            te.wait_ge(env.se.sem if pf[0] == "se" else env.ve.sem, pf[1])
                        for kf in range(8):
                            inst = te.matmul(pB[:, :], waet_b[:, kf, oc * 128:(oc + 1) * 128],
                                             xfm[:, kf, :], start=(kf == 0), stop=(kf == 7))
                        env.te.mark(inst)
                    mm_tok = env.te.n

                    @block.scalar
                    def _(s, pi=pi, oc=oc, h2=h2, mm_tok=mm_tok, vprev=env.ve.n):
                        s.wait_ge(env.te.sem, mm_tok)
                        if vprev:
                            s.wait_ge(env.ve.sem, vprev)
                        if pi < 2:
                            s.activation(sig_s[:, h2, :], pB[:, :], AF.Sigmoid,
                                         bias=baet_s[:, oc:oc + 1])
                        else:
                            s.activation(sp_s[:, h2, :], pB[:, :], AF.Abs,
                                         bias=baet_s[:, oc:oc + 1])
                            s.activation(sig_s[:, h2, :], pB[:, :], AF.Identity,
                                         bias=baet_s[:, oc:oc + 1])
                        env.se.mark(s.drain())
                    pn_free[0] = ("se", env.se.n)
                act_tok = env.se.n

                if pi == 2:
                    @block.scalar
                    def _(s, act_tok=act_tok):
                        s.activation(sp_s[:, 0, :], sp_s[:, 0, :], AF.Exp, scale=-1.0)
                        s.activation(sp_s[:, 1, :], sp_s[:, 1, :], AF.Exp, scale=-1.0)
                        s.drain()
                        s.activation(sp_s[:, 0, :], sp_s[:, 0, :], AF.Ln, bias=1.0)
                        s.activation(sp_s[:, 1, :], sp_s[:, 1, :], AF.Ln, bias=1.0)
                        env.se.mark(s.drain())
                    sp_tok = env.se.n

                    @block.vector
                    def _(v, sp_tok=sp_tok):
                        v.wait_ge(env.se.sem, sp_tok)
                        for h2 in range(2):
                            v.tensor_scalar(sig_s[:, h2, :], sig_s[:, h2, :], 0.0, None,
                                            ALU.max)
                        v.drain()
                        for h2 in range(2):
                            v.tensor_add(sig_s[:, h2, :], sig_s[:, h2, :], sp_s[:, h2, :])
                        env.ve.mark(v.drain())
                    act_tok = ("ve", env.ve.n)
                else:
                    act_tok = ("se", act_tok)

                @block.vector
                def _(v, act_tok=act_tok):
                    v.wait_ge(env.se.sem if act_tok[0] == "se" else env.ve.sem, act_tok[1])
                    v.tensor_copy(n2[:, 0, :], sig_s[:, 0, :])
                    inst = v.tensor_copy(n2[:, 1, :], sig_s[:, 1, :])
                    env.ve.mark(v.drain())
                cp_tok = env.ve.n

                @block.tensor
                def _(te, cp_tok=cp_tok, pi=pi):
                    te.wait_ge(env.ve.sem, cp_tok)
                    for blk in range(4):
                        for h2 in range(2):
                            inst = te.matmul(pC[:, pi * 4 + blk: pi * 4 + blk + 1],
                                             n2[:, h2, blk * 128:(blk + 1) * 128],
                                             onesc_b[:, 0:1],
                                             start=(h2 == 0), stop=(h2 == 1))
                        env.te.mark(inst)
            red_tok = env.te.n

            @block.vector
            def _(v, sp=sp, red_tok=red_tok):
                v.wait_ge(env.te.sem, red_tok)
                for blk in range(4):
                    bi = sp * 4 + blk
                    v.tensor_scalar(scal[:, bi, 0:1], pC[:, 8 + blk:9 + blk],
                                    0.1 / DD, None, ALU.mult)
                    v.tensor_scalar(scal[:, bi, 1:2], pC[:, 0 + blk:1 + blk],
                                    0.1 / DD, None, ALU.mult)
                    v.tensor_scalar(scal[:, bi, 2:3], pC[:, 4 + blk:5 + blk],
                                    0.9 / DD, None, ALU.mult)
                v.drain()
                for blk in range(4):
                    bi = sp * 4 + blk
                    inst = v.tensor_scalar(scal[:, bi, 2:3], scal[:, bi, 2:3],
                                           0.05, None, ALU.add)
                env.ve.mark(v.drain())
            mean_tok = env.ve.n

            @block.scalar
            def _(s, sp=sp, mean_tok=mean_tok):
                s.wait_ge(env.ve.sem, mean_tok)
                for blk in range(4):
                    bi = sp * 4 + blk
                    s.activation(scal[:, bi, 1:2], scal[:, bi, 1:2], AF.Ln,
                                 scale=-1.0, bias=1.0)
                    s.activation(scal[:, bi, 2:3], scal[:, bi, 2:3], AF.Ln)
                env.se.mark(s.drain())
            pn_free[0] = ("se", env.se.n)
        t1_se = env.se.n
        t1_gd = env.gd.n

        if S3_PHASES < 2:
            @block.sync
            def _(q):
                q.wait_ge(env.qd.sem, env.qd.n)
                q.wait_ge(env.gd.sem, env.gd.n)
            return nc
        conv_tok = [0]
        for oc in range(6):
            @block.sync
            def _(q, oc=oc, t1_gd=t1_gd, vp=conv_tok[0], gp=env.gd.n):
                if oc == 0:
                    q.wait_ge(env.gd.sem, t1_gd)
                if vp:
                    q.wait_ge(env.ve.sem, vp)
                q.wait_ge(env.gd.sem, gp)
                env.qd.mark(q.dma_start(out=crow[:, :], in_=praw_d.ap()[oc, :, :]))
            ld_tok = env.qd.n

            @block.vector
            def _(v, oc=oc, ld_tok=ld_tok, gprev=env.gd.n):
                v.wait_ge(env.qd.sem, ld_tok)
                if gprev > t1_gd:
                    v.wait_ge(env.gd.sem, gprev)
                v.tensor_scalar(cacc[0][:, :], crow[:, :], dwc_s[:, oc, 3:4], None, ALU.mult)
                v.drain()
                v.scalar_tensor_tensor(cacc[1][:, 1:T], crow[:, 0:T - 1],
                                       dwc_s[:, oc, 2:3], cacc[0][:, 1:T],
                                       ALU.mult, ALU.add)
                v.tensor_copy(cacc[1][:, 0:1], cacc[0][:, 0:1])
                v.drain()
                v.scalar_tensor_tensor(cacc[0][:, 2:T], crow[:, 0:T - 2],
                                       dwc_s[:, oc, 1:2], cacc[1][:, 2:T],
                                       ALU.mult, ALU.add)
                v.tensor_copy(cacc[0][:, 0:2], cacc[1][:, 0:2])
                v.drain()
                v.scalar_tensor_tensor(cacc[1][:, 3:T], crow[:, 0:T - 3],
                                       dwc_s[:, oc, 0:1], cacc[0][:, 3:T],
                                       ALU.mult, ALU.add)
                v.tensor_copy(cacc[1][:, 0:3], cacc[0][:, 0:3])
                v.drain()
                inst = v.tensor_scalar(crow[:, :], cacc[1][:, :], dwb_s[:, oc:oc + 1],
                                       None, ALU.add)
                env.ve.mark(v.drain())
            conv_tok[0] = env.ve.n

            @block.gpsimd
            def _(g, oc=oc, ct=conv_tok[0]):
                g.wait_ge(env.ve.sem, ct)
                env.gd.mark(g.dma_start(out=kdw_d.ap()[oc, :, :], in_=crow[:, :]))
        t2_gd = env.gd.n

        if S3_PHASES < 3:
            @block.sync
            def _(q):
                q.wait_ge(env.qd.sem, env.qd.n)
                q.wait_ge(env.gd.sem, env.gd.n)
            return nc
        psum_free = {}

        def mmseq(te, out, pairs, extra_wait=None):
            for i, (lh, rh) in enumerate(pairs):
                inst = te.matmul(out, lh, rh, start=(i == 0), stop=(i == len(pairs) - 1))
            return inst

        for cc in range(8):
            @block.sync
            def _(q, cc=cc, t2_gd=t2_gd, te_prev=env.te.n):
                if cc == 0:
                    q.wait_ge(env.gd.sem, t2_gd)
                else:
                    q.wait_ge(env.te.sem, te_prev)
                for dc in range(2):
                    env.qd.mark(q.dma_start(out=kc_b[:, dc, :],
                                            in_=kdw_d.ap()[0 + dc, :, cc * CHUNK:(cc + 1) * CHUNK]))
                    env.qd.mark(q.dma_start(out=vc_b[:, dc, :],
                                            in_=kdw_d.ap()[2 + dc, :, cc * CHUNK:(cc + 1) * CHUNK]))
                    env.qd.mark(q.dma_start(out=qc_b[:, dc, :],
                                            in_=kdw_d.ap()[4 + dc, :, cc * CHUNK:(cc + 1) * CHUNK]))
            kv_tok = env.qd.n

            for od in range(2):
                @block.tensor
                def _(te, od=od, cc=cc, kv_tok=kv_tok, vp=env.ve.n, sp_=env.se.n):
                    if od == 0:
                        te.wait_ge(env.qd.sem, kv_tok)
                        if vp:
                            te.wait_ge(env.ve.sem, vp)
                    if sp_:
                        te.wait_ge(env.se.sem, sp_)
                    inst = mmseq(te, pA[:, :],
                                 [(w1b[:, dc, od * 128:(od + 1) * 128], kc_b[:, dc, :])
                                  for dc in range(2)])
                    env.te.mark(inst)

                @block.scalar
                def _(s2, od=od, tok=env.te.n):
                    s2.wait_ge(env.te.sem, tok)
                    s2.activation(afm[:, od, :], pA[:, :], AF.Silu)
                    s2.activation(dsfm[:, od, :], pA[:, :], AF.Derivative_silu)
                    env.se.mark(s2.drain())
            a_tok = env.se.n

            for od in range(2):
                @block.tensor
                def _(te, od=od, a_tok=a_tok, vp=env.ve.n):
                    if od == 0:
                        te.wait_ge(env.se.sem, a_tok)
                    if vp:
                        te.wait_ge(env.ve.sem, vp)
                    inst = mmseq(te, pB[:, :],
                                 [(w2b[:, oc2, od * 128:(od + 1) * 128], afm[:, oc2, :])
                                  for oc2 in range(2)])
                    env.te.mark(inst)

                @block.vector
                def _(v2, od=od, tok=env.te.n):
                    v2.wait_ge(env.te.sem, tok)
                    inst2 = v2.tensor_sub(efm[:, od, :], pB[:, :], vc_b[:, od, :])
                    env.ve.mark(v2.drain())
            e_tok = env.ve.n

            @block.tensor
            def _(te, e_tok=e_tok):
                te.wait_ge(env.ve.sem, e_tok)
                for pc in range(2):
                    for oc2 in range(2):
                        inst = te.transpose(pT[:, (pc * 2 + oc2) * 128:(pc * 2 + oc2 + 1) * 128],
                                            w2b[:, oc2, pc * 128:(pc + 1) * 128], eye_b[:, :])
                env.te.mark(inst)
            w2p_tok = env.te.n

            @block.vector
            def _(v, w2p_tok=w2p_tok):
                v.wait_ge(env.te.sem, w2p_tok)
                src = pT[:, :].rearrange("p (a b) -> p a b", a=8)
                v.tensor_copy(w2pb[:, 0, 0:128], src[:, 0, :])
                v.tensor_copy(w2pb[:, 0, 128:256], src[:, 1, :])
                v.tensor_copy(w2pb[:, 1, 0:128], src[:, 2, :])
                inst = v.tensor_copy(w2pb[:, 1, 128:256], src[:, 3, :])
                env.ve.mark(v.drain())
            w2pb_tok = env.ve.n

            for od in range(2):
                @block.tensor
                def _(te, od=od, w2pb_tok=w2pb_tok, vp=env.ve.n):
                    if od == 0:
                        te.wait_ge(env.ve.sem, w2pb_tok)
                    else:
                        te.wait_ge(env.ve.sem, vp)
                    inst = mmseq(te, pC[:, :],
                                 [(w2pb[:, pc, od * 128:(od + 1) * 128], efm[:, pc, :])
                                  for pc in range(2)])
                    env.te.mark(inst)

                @block.vector
                def _(v2, od=od, tok=env.te.n):
                    v2.wait_ge(env.te.sem, tok)
                    inst2 = v2.tensor_mul(ufm[:, od, :], pC[:, :], dsfm[:, od, :])
                    env.ve.mark(v2.drain())
            u_tok = env.ve.n

            @block.scalar
            def _(s, u_tok=u_tok):
                s.wait_ge(env.ve.sem, u_tok)
                for od in range(2):
                    s.activation(n2[:, od, :], ufm[:, od, :], AF.Square)
                env.se.mark(s.drain())
            nu_tok = env.se.n

            def norm_reduce(src_tag, dst_col, wait_tok):
                @block.tensor
                def _(te, dst_col=dst_col, wait_tok=wait_tok):
                    te.wait_ge(env.se.sem, wait_tok)
                    for blk in range(4):
                        for od in range(2):
                            inst = te.matmul(pD[:, dst_col * 4 + blk: dst_col * 4 + blk + 1],
                                             n2[:, od, blk * 128:(blk + 1) * 128],
                                             onesc_b[:, 0:1], start=(od == 0), stop=(od == 1))
                    env.te.mark(inst)
                return env.te.n

            nmm = norm_reduce("u", 0, nu_tok)

            @block.scalar
            def _(s, nmm=nmm):
                s.wait_ge(env.te.sem, nmm)
                for od in range(2):
                    s.activation(n2[:, od, :], kc_b[:, od, :], AF.Square)
                env.se.mark(s.drain())
            nmm = norm_reduce("k", 1, env.se.n)

            @block.scalar
            def _(s, nmm=nmm):
                s.wait_ge(env.te.sem, nmm)
                for od in range(2):
                    s.activation(n2[:, od, :], efm[:, od, :], AF.Square)
                env.se.mark(s.drain())
            nmm = norm_reduce("e", 2, env.se.n)

            @block.scalar
            def _(s, nmm=nmm):
                s.wait_ge(env.te.sem, nmm)
                for od in range(2):
                    s.activation(n2[:, od, :], afm[:, od, :], AF.Square)
                env.se.mark(s.drain())
            nmm = norm_reduce("a", 3, env.se.n)

            @block.vector
            def _(v, cc=cc, nmm=nmm):
                v.wait_ge(env.te.sem, nmm)
                v.tensor_copy(gncol[:, 0:16], pD[:, 0:16])
                v.drain()
                v.tensor_mul(gncol[:, 16:20], gncol[:, 0:4], gncol[:, 4:8])
                v.tensor_mul(gncol[:, 20:24], gncol[:, 8:12], gncol[:, 12:16])
                v.drain()
                inst = v.tensor_add(gncol[:, 3:7], gncol[:, 16:20], gncol[:, 20:24])
                env.ve.mark(v.drain())
            gn_tok = env.ve.n

            @block.scalar
            def _(s, gn_tok=gn_tok):
                s.wait_ge(env.ve.sem, gn_tok)
                inst = s.activation(gncol[:, 3:7], gncol[:, 3:7], AF.Sqrt)
                env.se.mark(inst)
            sq2_tok = env.se.n

            @block.vector
            def _(v, cc=cc, sq2_tok=sq2_tok):
                v.wait_ge(env.se.sem, sq2_tok)
                v.tensor_scalar(gncol[:, 3:7], gncol[:, 3:7], 2.0 * GS3, None, ALU.mult)
                v.drain()
                v.reciprocal(gncol[:, 3:7], gncol[:, 3:7])
                v.drain()
                v.tensor_scalar(gncol[:, 3:7], gncol[:, 3:7], 1.0, None, ALU.min)
                v.drain()
                v.tensor_scalar(gncol[:, 3:7], gncol[:, 3:7], 2.0 * GS3, None, ALU.mult)
                v.drain()
                for blk in range(4):
                    inst = v.tensor_mul(wthc[:, blk:blk + 1], gncol[:, 3 + blk:4 + blk],
                                        scal[:, cc * 4 + blk, 0:1])
                env.ve.mark(v.drain())
            wthc_tok = env.ve.n

            for m in range(4):
                bi = cc * 4 + m

                @block.tensor
                def _(te, bi=bi, wthc_tok=wthc_tok, vp=env.ve.n):
                    te.wait_ge(env.ve.sem, max(wthc_tok, vp))
                    inst = te.matmul(pE[:, 0:2], triu_f[:, :], scal[:, bi, 1:3],
                                     start=True, stop=True)
                    env.te.mark(inst)
                cum_tok = env.te.n

                @block.vector
                def _(v, cum_tok=cum_tok):
                    v.wait_ge(env.te.sem, cum_tok)
                    inst = v.tensor_copy(lacol[:, 0:2], pE[:, 0:2])
                    env.ve.mark(v.drain())
                cumc_tok = env.ve.n

                @block.tensor
                def _(te, cumc_tok=cumc_tok):
                    te.wait_ge(env.ve.sem, cumc_tok)
                    inst = te.transpose(pE[0:2, 0:128] if False else pE[0:1, 0:128],
                                        lacol[:, 0:1], eye_f[:, :])
                    inst = te.transpose(pE[0:1, 128:256], lacol[:, 1:2], eye_f[:, :])
                    env.te.mark(inst)
                rowt_tok = env.te.n

                @block.vector
                def _(v, rowt_tok=rowt_tok):
                    v.wait_ge(env.te.sem, rowt_tok)
                    v.tensor_copy(clarow[:, :], pE[0:1, 0:128])
                    inst = v.tensor_copy(clerow[:, :], pE[0:1, 128:256])
                    v.drain()
                    v.tensor_sub(lacol[:, 2:3], lacol[:, 1:2], lacol[:, 0:1])
                    v.tensor_scalar(lacol[:, 3:4], lacol[:, 0:1], -1.0, None, ALU.mult)
                    v.tensor_scalar(lacol[:, 4:5], lacol[:, 1:2], -1.0, None, ALU.mult)
                    v.tensor_scalar(negcle[:, :], clerow[:, :], -1.0, None, ALU.mult)
                    env.ve.mark(v.drain())
                rows_tok = env.ve.n

                @block.tensor
                def _(te, rows_tok=rows_tok):
                    te.wait_ge(env.ve.sem, rows_tok)
                    te.matmul(pE[:, 0:128], onesr_f[0:1, :], clarow[0:1, :],
                              start=True, stop=True)
                    inst = te.matmul(pE[:, 128:256], onesr_f[0:1, :], negcle[0:1, :],
                                     start=True, stop=True)
                    env.te.mark(inst)
                bc_tok = env.te.n

                @block.scalar
                def _(s, bc_tok=bc_tok):
                    s.wait_ge(env.te.sem, bc_tok)
                    s.activation(m1t_f[:, :], pE[:, 0:128], AF.Exp,
                                 bias=lacol[:, 3:4], scale=1.0)
                    s.activation(m2t_f[:, :], pE[:, 128:256], AF.Exp,
                                 bias=lacol[:, 1:2], scale=1.0)
                    s.activation(expin[:, :], pE[:, 0:128], AF.Exp,
                                 bias=lacol[:, 2:3], scale=1.0)
                    s.activation(acol[:, 0:1], lacol[:, 0:1], AF.Exp)
                    s.activation(acol[:, 2:3], lacol[:, 1:2], AF.Exp)
                    env.se.mark(s.drain())
                exp_tok = env.se.n

                @block.vector
                def _(v, exp_tok=exp_tok):
                    v.wait_ge(env.se.sem, exp_tok)
                    v.tensor_mul(m1t_f[:, :], m1t_f[:, :], triu_f[:, :])
                    v.tensor_mul(expin[:, :], expin[:, :], triu_f[:, :])
                    env.ve.mark(v.drain())
                msk1_tok = env.ve.n

                @block.vector
                def _(v, msk1_tok=msk1_tok):
                    inst = v.tensor_mul(m2t_f[:, :], m2t_f[:, :], tril_f[:, :])
                    env.ve.mark(v.drain())
                msk_tok = env.ve.n

                @block.tensor
                def _(te, msk_tok=msk_tok):
                    te.wait_ge(env.ve.sem, msk_tok)
                    te.matmul(pF2[:, 0:128], m2t_f[:, :], m1t_f[:, :], start=True, stop=True)
                    inst = te.matmul(pF2[:, 128:129], expin[:, :], onesc_f[:, 0:1],
                                     start=True, stop=True)
                    env.te.mark(inst)
                dt_tok = env.te.n

                @block.vector
                def _(v, dt_tok=dt_tok):
                    v.wait_ge(env.te.sem, dt_tok)
                    v.tensor_copy(dt_sb[:, :], pF2[:, 0:128])
                    inst = v.tensor_copy(acol[:, 1:2], pF2[:, 128:129])
                    env.ve.mark(v.drain())
                dt_sb_tok = env.ve.n

                @block.vector
                def _(v, dt_sb_tok=dt_sb_tok):
                    v.wait_ge(env.ve.sem, dt_sb_tok)
                    inst = v.tensor_copy(acol[:, 3:4], lacol[:, 1:2])
                    env.ve.mark(v.drain())
                a4_tok = env.ve.n

                @block.tensor
                def _(te, a4_tok=a4_tok):
                    te.wait_ge(env.ve.sem, a4_tok)
                    inst = te.matmul(pE[0:1, 300:304], sel_f[:, 0:1], acol[:, 0:4],
                                     start=True, stop=True)
                    env.te.mark(inst)
                l1_tok = env.te.n

                @block.vector
                def _(v, l1_tok=l1_tok):
                    v.wait_ge(env.te.sem, l1_tok)
                    inst = v.tensor_copy(arow[0:1, 0:4], pE[0:1, 300:304])
                    env.ve.mark(v.drain())
                l2_tok = env.ve.n

                @block.tensor
                def _(te, l2_tok=l2_tok):
                    te.wait_ge(env.ve.sem, l2_tok)
                    inst = te.matmul(pE[:, 300:304], onesr_f[0:1, :], arow[0:1, 0:4],
                                     start=True, stop=True)
                    env.te.mark(inst)
                l3_tok = env.te.n

                @block.vector
                def _(v, bi=bi, l3_tok=l3_tok):
                    v.wait_ge(env.te.sem, l3_tok)
                    v.tensor_copy(lasts[:, 0:4], pE[:, 300:304])
                    v.drain()
                    v.tensor_mul(dl_c[:, 0:1], dt_sb[:, 127:128], wthc[:, bi % 4:bi % 4 + 1])
                    v.drain()
                    inst = v.tensor_scalar(dl_c[:, 0:1], dl_c[:, 0:1], -1.0, None, ALU.mult)
                    env.ve.mark(v.drain())
                la_tok = env.ve.n

                @block.scalar
                def _(s, la_tok=la_tok):
                    s.wait_ge(env.ve.sem, la_tok)
                    inst = s.activation(dl_c[:, 1:2], lacol[:, 4:5], AF.Exp,
                                        bias=lasts[:, 3:4], scale=1.0)
                    env.se.mark(inst)
                ds_tok = env.se.n

                @block.vector
                def _(v, bi=bi, ds_tok=ds_tok):
                    v.wait_ge(env.se.sem, ds_tok)
                    v.tensor_mul(dl_c[:, 1:2], dl_c[:, 1:2], wthc[:, bi % 4:bi % 4 + 1])
                    v.drain()
                    v.tensor_scalar(dl_c[:, 1:2], dl_c[:, 1:2], -1.0, None, ALU.mult)
                    v.tensor_scalar(eyeA[:, :], eye_f[:, :], lasts[:, 0:1], None, ALU.mult)
                    v.tensor_scalar(eyeB[:, :], eye_f[:, :], lasts[:, 1:2], None, ALU.mult)
                    v.tensor_scalar(eyeE[:, :], eye_f[:, :], lasts[:, 2:3], None, ALU.mult)
                    env.ve.mark(v.drain())
                eyes_tok = env.ve.n

                @block.tensor
                def _(te, m=m, eyes_tok=eyes_tok):
                    te.wait_ge(env.ve.sem, eyes_tok)
                    for mi, (fmb, tmb) in enumerate(((kc_b, ktm), (ufm, utm),
                                                     (efm, etm), (afm, atm))):
                        for dc in range(2):
                            inst = te.transpose(
                                pT[:, (mi * 2 + dc) * 128:(mi * 2 + dc + 1) * 128],
                                fmb[:, dc, m * 128:(m + 1) * 128], eye_b[:, :])
                    env.te.mark(inst)
                trs_tok = env.te.n

                @block.vector
                def _(v, trs_tok=trs_tok):
                    v.wait_ge(env.te.sem, trs_tok)
                    src = pT[:, :].rearrange("p (a b) -> p a b", a=8)
                    for mi, tmb in enumerate((ktm, utm, etm, atm)):
                        for dc in range(2):
                            inst = v.tensor_copy(tmb[:, dc * 128:(dc + 1) * 128],
                                                 src[:, mi * 2 + dc, :])
                    env.ve.mark(v.drain())
                tm_tok = env.ve.n

                @block.tensor
                def _(te, m=m, tm_tok=tm_tok):
                    te.wait_ge(env.ve.sem, tm_tok)
                    mmseq(te, pA[:, 0:256], [(qc_b[:, dc, m * 128:(m + 1) * 128],
                                              w1b[:, dc, :]) for dc in range(2)])
                    mmseq(te, pB[:, 0:256], [(qc_b[:, dc, m * 128:(m + 1) * 128],
                                              s1b[:, dc, :]) for dc in range(2)])
                    inst = mmseq(te, pC[:, 0:128],
                                 [(kc_b[:, dc, m * 128:(m + 1) * 128],
                                   qc_b[:, dc, m * 128:(m + 1) * 128]) for dc in range(2)])
                    env.te.mark(inst)
                app1_tok = env.te.n

                @block.vector
                def _(v, m=m, bi=bi, app1_tok=app1_tok):
                    v.wait_ge(env.te.sem, app1_tok)
                    v.tensor_mul(hsb2[:, 0:128], pC[:, 0:128], dt_sb[:, :])
                    v.drain()
                    v.tensor_scalar(p1_b[:, :], hsb2[:, 0:128], wthc[:, bi % 4:bi % 4 + 1],
                                    None, ALU.mult)
                    env.ve.mark(v.drain())
                p1_tok = env.ve.n

                @block.tensor
                def _(te, p1_tok=p1_tok):
                    te.wait_ge(env.ve.sem, p1_tok)
                    inst = te.matmul(pD[:, 0:256], p1_b[:, :], utm[:, :],
                                     start=True, stop=True)
                    env.te.mark(inst)
                ct1_tok = env.te.n

                @block.vector
                def _(v, ct1_tok=ct1_tok):
                    v.wait_ge(env.te.sem, ct1_tok)
                    v.tensor_scalar(hsb[:, :], pA[:, 0:256], acol[:, 0:1], None, ALU.mult)
                    v.drain()
                    v.scalar_tensor_tensor(hsb2[:, :], pB[:, 0:256], acol[:, 1:2],
                                           hsb[:, :], ALU.mult, ALU.add)
                    v.drain()
                    inst = v.tensor_sub(hsb[:, :], hsb2[:, :], pD[:, 0:256])
                    env.ve.mark(v.drain())
                hf_tok = env.ve.n

                @block.scalar
                def _(s, hf_tok=hf_tok):
                    s.wait_ge(env.ve.sem, hf_tok)
                    inst = s.activation(hstm[:, :], hsb[:, :], AF.Silu)
                    env.se.mark(inst)
                hs_tok = env.se.n

                @block.tensor
                def _(te, hs_tok=hs_tok):
                    te.wait_ge(env.se.sem, hs_tok)
                    for oc2 in range(2):
                        inst = te.transpose(pT[:, oc2 * 128:(oc2 + 1) * 128],
                                            hstm[:, oc2 * 128:(oc2 + 1) * 128], eye_b[:, :])
                    env.te.mark(inst)
                hst_tok = env.te.n

                @block.vector
                def _(v, hst_tok=hst_tok):
                    v.wait_ge(env.te.sem, hst_tok)
                    src = pT[:, :].rearrange("p (a b) -> p a b", a=8)
                    v.tensor_copy(hsfm[:, 0, :], src[:, 0, :])
                    inst = v.tensor_copy(hsfm[:, 1, :], src[:, 1, :])
                    env.ve.mark(v.drain())
                hsf_tok = env.ve.n

                @block.tensor
                def _(te, m=m, hsf_tok=hsf_tok):
                    te.wait_ge(env.ve.sem, hsf_tok)
                    mmseq(te, pA[:, 0:256], [(hsfm[:, oc2, :], w2b[:, oc2, :])
                                             for oc2 in range(2)])
                    mmseq(te, pB[:, 0:256], [(hsfm[:, oc2, :], s2b[:, oc2, :])
                                             for oc2 in range(2)])
                    inst = mmseq(te, pC[:, 0:128],
                                 [(afm[:, oc2, m * 128:(m + 1) * 128], hsfm[:, oc2, :])
                                  for oc2 in range(2)])
                    env.te.mark(inst)
                app2_tok = env.te.n

                @block.vector
                def _(v, bi=bi, app2_tok=app2_tok):
                    v.wait_ge(env.te.sem, app2_tok)
                    v.tensor_mul(hsb2[:, 0:128], pC[:, 0:128], dt_sb[:, :])
                    v.drain()
                    v.tensor_scalar(p2_b[:, :], hsb2[:, 0:128], wthc[:, bi % 4:bi % 4 + 1],
                                    None, ALU.mult)
                    env.ve.mark(v.drain())
                p2_tok = env.ve.n

                @block.tensor
                def _(te, p2_tok=p2_tok):
                    te.wait_ge(env.ve.sem, p2_tok)
                    inst = te.matmul(pD[:, 0:256], p2_b[:, :], etm[:, :],
                                     start=True, stop=True)
                    env.te.mark(inst)
                ct2_tok = env.te.n

                @block.vector
                def _(v, ct2_tok=ct2_tok):
                    v.wait_ge(env.te.sem, ct2_tok)
                    v.tensor_scalar(ott[:, :], pA[:, 0:256], acol[:, 0:1], None, ALU.mult)
                    v.drain()
                    v.scalar_tensor_tensor(hsb2[:, :], pB[:, 0:256], acol[:, 1:2],
                                           ott[:, :], ALU.mult, ALU.add)
                    v.drain()
                    inst = v.tensor_sub(ott[:, :], hsb2[:, :], pD[:, 0:256])
                    env.ve.mark(v.drain())
                ot_tok = env.ve.n

                @block.gpsimd
                def _(g, bi=bi, ot_tok=ot_tok):
                    g.wait_ge(env.ve.sem, ot_tok)
                    env.gd.mark(g.dma_start(out=retr_d[bi * 128:(bi + 1) * 128, :],
                                            in_=ott[:, :]))
                ret_st = env.gd.n

                @block.vector
                def _(v, tm_tok=tm_tok, ot_tok=ot_tok):
                    v.wait_ge(env.ve.sem, ot_tok)
                    v.tensor_scalar(kw_b[:, :], ktm[:, :], dl_c[:, 0:1], None, ALU.mult)
                    v.tensor_scalar(aw_b[:, :], atm[:, :], dl_c[:, 0:1], None, ALU.mult)
                    v.tensor_scalar(kw2_b[:, :], ktm[:, :], dl_c[:, 1:2], None, ALU.mult)
                    inst = v.tensor_scalar(aw2_b[:, :], atm[:, :], dl_c[:, 1:2],
                                           None, ALU.mult)
                    env.ve.mark(v.drain())
                kw_tok = env.ve.n

                @block.tensor
                def _(te, kw_tok=kw_tok):
                    te.wait_ge(env.ve.sem, kw_tok)
                    for dc in range(2):
                        inst = mmseq(te, (pA if dc == 0 else pB)[:, 0:256],
                                     [(eyeA[:, :], w1b[:, dc, :]),
                                      (eyeB[:, :], s1b[:, dc, :]),
                                      (kw_b[:, dc * 128:(dc + 1) * 128], utm[:, :])])
                    for oc2 in range(2):
                        inst = mmseq(te, (pC if oc2 == 0 else pD)[:, 0:256],
                                     [(eyeA[:, :], w2b[:, oc2, :]),
                                      (eyeB[:, :], s2b[:, oc2, :]),
                                      (aw_b[:, oc2 * 128:(oc2 + 1) * 128], etm[:, :])])
                    env.te.mark(inst)
                wup_tok = env.te.n

                @block.vector
                def _(v, wup_tok=wup_tok):
                    v.wait_ge(env.te.sem, wup_tok)
                    for dc, pp in ((0, pA), (1, pB)):
                        v.tensor_copy(w1t[:, dc, :], pp[:, 0:256])
                    for oc2, pp in ((0, pC), (1, pD)):
                        inst = v.tensor_copy(w2t[:, oc2, :], pp[:, 0:256])
                    env.ve.mark(v.drain())
                wev_tok = env.ve.n

                @block.tensor
                def _(te, wev_tok=wev_tok):
                    te.wait_ge(env.ve.sem, wev_tok)
                    for dc in range(2):
                        inst = mmseq(te, (pA if dc == 0 else pB)[:, 0:256],
                                     [(eyeE[:, :], s1b[:, dc, :]),
                                      (kw2_b[:, dc * 128:(dc + 1) * 128], utm[:, :])])
                    for oc2 in range(2):
                        inst = mmseq(te, (pC if oc2 == 0 else pD)[:, 0:256],
                                     [(eyeE[:, :], s2b[:, oc2, :]),
                                      (aw2_b[:, oc2 * 128:(oc2 + 1) * 128], etm[:, :])])
                    env.te.mark(inst)
                sup_tok = env.te.n

                @block.vector
                def _(v, sup_tok=sup_tok):
                    v.wait_ge(env.te.sem, sup_tok)
                    for dc, pp in ((0, pA), (1, pB)):
                        v.tensor_copy(s1t[:, dc, :], pp[:, 0:256])
                    for oc2, pp in ((0, pC), (1, pD)):
                        v.tensor_copy(s2t[:, oc2, :], pp[:, 0:256])
                    v.drain()
                    v.tensor_copy(w1b[:, :, :], w1t[:, :, :])
                    v.tensor_copy(w2b[:, :, :], w2t[:, :, :])
                    v.tensor_copy(s1b[:, :, :], s1t[:, :, :])
                    inst = v.tensor_copy(s2b[:, :, :], s2t[:, :, :])
                    env.ve.mark(v.drain())
                st_tok2 = env.ve.n

                if S3_DEBUG and cc == 0 and m == 0:
                    @block.gpsimd
                    def _(g, st_tok2=st_tok2):
                        g.wait_ge(env.ve.sem, st_tok2)
                        env.gd.mark(g.dma_start(out=dbgst["w1"].ap(), in_=w1t[:, :, :]))
                        env.gd.mark(g.dma_start(out=dbgst["w2"].ap(), in_=w2t[:, :, :]))
                        env.gd.mark(g.dma_start(out=dbgst["s1"].ap(), in_=s1t[:, :, :]))
                        env.gd.mark(g.dma_start(out=dbgst["s2"].ap(), in_=s2t[:, :, :]))
                        env.gd.mark(g.dma_start(out=dbgco.ap()[:, 0:128], in_=dt_sb[:, :]))
                        env.gd.mark(g.dma_start(out=dbgco.ap()[:, 128:132], in_=acol[:, :]))
                        env.gd.mark(g.dma_start(out=dbgco.ap()[:, 132:136], in_=lasts[:, :]))
                        env.gd.mark(g.dma_start(out=dbgco.ap()[:, 136:140], in_=dl_c[:, :]))
                        env.gd.mark(g.dma_start(out=dbgco.ap()[:, 140:144], in_=wthc[:, :]))
                        env.gd.mark(g.dma_start(out=dbgco.ap()[:, 144:150], in_=lacol[:, :]))
                    @block.vector
                    def _(v):
                        v.wait_ge(env.gd.sem, env.gd.n)
                        env.ve.mark(v.engine_nop())
        chain_ve = env.ve.n
        chain_gd = env.gd.n

        if S3_PHASES < 4:
            @block.sync
            def _(q):
                q.wait_ge(env.qd.sem, env.qd.n)
                q.wait_ge(env.gd.sem, env.gd.n)
            return nc
        @block.sync
        def _(q, chain_ve=chain_ve, chain_gd=chain_gd):
            q.wait_ge(env.ve.sem, chain_ve)
            q.wait_ge(env.gd.sem, chain_gd)

        rtm = ret_s
        for spl in range(8):
            for tb in range(4):
                bil = spl * 4 + tb

                @block.sync
                def _(q, spl=spl, tb=tb, ve_prev=env.ve.n, se_prev=env.se.n):
                    if ve_prev:
                        q.wait_ge(env.ve.sem, ve_prev)
                    if se_prev:
                        q.wait_ge(env.se.sem, se_prev)
                    bi2 = spl * 4 + tb
                    goff = spl * CHUNK + tb * 128
                    env.qd.mark(q.dma_start(out=xres[:, 0, :],
                                            in_=x_d[goff: goff + 128, :]))
                    env.qd.mark(q.dma_start(
                        out=ott[:, :], in_=retr_d[bi2 * 128:(bi2 + 1) * 128, :]))
                ld_tok = env.qd.n

                @block.vector
                def _(v, ld_tok=ld_tok, te_prev=env.te.n):
                    v.wait_ge(env.qd.sem, ld_tok)
                    if te_prev:
                        v.wait_ge(env.te.sem, te_prev)
                    v.tensor_copy(xbf[:, 0, :], xres[:, 0, :])
                    inst = v.tensor_copy(junk_b[:, 0:DD], ott[:, :])
                    env.ve.mark(v.drain())
                cast_tok = env.ve.n

                @block.tensor
                def _(te, cast_tok=cast_tok):
                    te.wait_ge(env.ve.sem, cast_tok)
                    for kf in range(8):
                        inst = te.transpose(pT[:, kf * 128:(kf + 1) * 128],
                                            xbf[:, 0, kf * 128:(kf + 1) * 128], eye_b[:, :])
                    env.te.mark(inst)
                xtr_tok = env.te.n

                @block.vector
                def _(v, xtr_tok=xtr_tok):
                    v.wait_ge(env.te.sem, xtr_tok)
                    src = pT[:, :].rearrange("p (f t) -> p f t", f=8)
                    inst = v.tensor_copy(xfm[:, :, 0:128], src)
                    env.ve.mark(v.drain())
                xfm_tok = env.ve.n

                @block.tensor
                def _(te, xfm_tok=xfm_tok):
                    te.wait_ge(env.ve.sem, xfm_tok)
                    for dc in range(2):
                        inst = te.transpose(pT[:, dc * 128:(dc + 1) * 128],
                                            junk_b[:, dc * 128:(dc + 1) * 128], eye_b[:, :])
                    env.te.mark(inst)
                rtr_tok = env.te.n

                @block.vector
                def _(v, rtr_tok=rtr_tok):
                    v.wait_ge(env.te.sem, rtr_tok)
                    src = pT[:, :].rearrange("p (f t) -> p f t", f=8)
                    v.tensor_copy(retfm[:, 0, :], src[:, 0, :])
                    inst = v.tensor_copy(retfm[:, 1, :], src[:, 1, :])
                    env.ve.mark(v.drain())
                rfm_tok = env.ve.n

                @block.tensor
                def _(te, rfm_tok=rfm_tok):
                    te.wait_ge(env.ve.sem, rfm_tok)
                    for oh, pp in ((0, pA), (1, pB)):
                        inst = mmseq(te, pp[:, :],
                                     [(retfm[:, dc, :], wo_b[:, dc, oh * 512:(oh + 1) * 512])
                                      for dc in range(2)])
                    for oh, pp in ((0, pC), (1, pD)):
                        pairs = [(xfm[:, kf, 0:128], gw_b[:, kf, oh * 512:(oh + 1) * 512])
                                 for kf in range(8)]
                        pairs.append((onesr_b[0:1, :], gb_b[0:1, oh * 512:(oh + 1) * 512]))
                        inst = mmseq(te, pp[:, :], pairs)
                    env.te.mark(inst)
                wog_tok = env.te.n

                @block.scalar
                def _(s, wog_tok=wog_tok):
                    s.wait_ge(env.te.sem, wog_tok)
                    s.activation(ret_s[:, 0:512], pA[:, :], AF.Identity)
                    s.activation(ret_s[:, 512:1024], pB[:, :], AF.Identity)
                    s.drain()
                    s.activation(junk_b[:, :], ret_s[:, 0:1024], AF.Square,
                                 accum_out=st[:, 0:1])
                    env.se.mark(s.drain())
                ms_tok = env.se.n

                @block.vector
                def _(v, ms_tok=ms_tok):
                    v.wait_ge(env.se.sem, ms_tok)
                    inst = v.tensor_scalar(st[:, 2:3], st[:, 0:1], 1.0 / C, None, ALU.mult)
                    env.ve.mark(v.drain())
                ms2_tok = env.ve.n

                @block.scalar
                def _(s, ms2_tok=ms2_tok):
                    s.wait_ge(env.ve.sem, ms2_tok)
                    inst = s.activation(st[:, 3:4], st[:, 2:3], AF.Sqrt, bias=eps3c[:, 0:1])
                    env.se.mark(inst)
                sd3_tok = env.se.n

                @block.vector
                def _(v, sd3_tok=sd3_tok):
                    v.wait_ge(env.se.sem, sd3_tok)
                    v.reciprocal(st[:, 4:5], st[:, 3:4])
                    env.ve.mark(v.drain())
                rc_tok = env.ve.n

                @block.scalar
                def _(s, rc_tok=rc_tok):
                    s.wait_ge(env.ve.sem, rc_tok)
                    inst = s.activation(o3_s[:, :], ret_s[:, 0:1024], AF.Identity,
                                        scale=st[:, 4:5])
                    env.se.mark(inst)
                o1_tok = env.se.n

                @block.vector
                def _(v, o1_tok=o1_tok, g128_tok=g128_tok):
                    v.wait_ge(env.se.sem, o1_tok)
                    v.tensor_mul(o3_s[:, :], o3_s[:, :], g128[:, :])
                    v.drain()
                    v.tensor_mul(o3_s[:, 0:512], o3_s[:, 0:512], pC[:, :])
                    v.tensor_mul(o3_s[:, 512:1024], o3_s[:, 512:1024], pD[:, :])
                    v.drain()
                    v.scalar_tensor_tensor(s_sb[:, :], o3_s[:, :], 1.0, xres[:, 0, :],
                                           ALU.mult, ALU.add, accum_out=st[:, 0:1])
                    env.ve.mark(v.drain())
                sum3_tok = env.ve.n

                @block.scalar
                def _(s, sum3_tok=sum3_tok):
                    s.wait_ge(env.ve.sem, sum3_tok)
                    s.activation(junk_b[:, :], s_sb[:, :], AF.Square, accum_out=st[:, 1:2])
                    env.se.mark(s.drain())
                sq3_tok = env.se.n

                y_done = emit_ln_stats_apply(env, block, s_sb[:, :], st,
                                             ynew[:, bil % 2, :], epsl[:, 0:1],
                                             sq3_tok, "se")

                @block.sync
                def _(q, spl=spl, tb=tb, bil=bil, y_done=y_done):
                    q.wait_ge(env.se.sem, y_done)
                    env.qd.mark(q.dma_start(
                        out=y_d[spl * CHUNK + tb * 128: spl * CHUNK + (tb + 1) * 128, :],
                        in_=ynew[:, bil % 2, :]))
                yst3 = env.qd.n

                @block.scalar
                def _(s, yst3=yst3):
                    if yst3 >= 32:
                        s.wait_ge(env.qd.sem, yst3 - 16)

        @block.sync
        def _(q):
            q.wait_ge(env.qd.sem, env.qd.n)
            q.wait_ge(env.gd.sem, env.gd.n)
    return nc


_S3_NC = None


def run_s3(x_full, Wk, Wv, Wq, Wo, ckw, ckb, cvw, cvb, cqw, cqb, mW1, mW2,
           aW, ab, eW, eb, tW, tb_, norm_g, gW, gb):
    global _S3_NC
    if _S3_NC is None:
        _S3_NC = build_s3()
    eye = np.eye(128, dtype=np.float32)
    rr = np.arange(128)
    triu = (rr[:, None] <= rr[None, :]).astype(np.float32)
    tril = (rr[:, None] >= rr[None, :]).astype(np.float32)
    sel = np.zeros((128, 4), np.float32)
    sel[127, 0] = 1.0
    sel[:, 1] = 1.0
    wkvq = f32c(np.concatenate([np.asarray(Wk).T, np.asarray(Wv).T, np.asarray(Wq).T], 1))
    waet = f32c(np.concatenate([np.asarray(aW).T, np.asarray(eW).T, np.asarray(tW).T], 1))
    baet = f32c(np.stack([np.asarray(ab)[0:128], np.asarray(ab)[128:256],
                          np.asarray(eb)[0:128], np.asarray(eb)[128:256],
                          np.asarray(tb_)[0:128], np.asarray(tb_)[128:256]], 1))
    dwc = f32c(np.stack([np.asarray(ckw)[0:128], np.asarray(ckw)[128:256],
                         np.asarray(cvw)[0:128], np.asarray(cvw)[128:256],
                         np.asarray(cqw)[0:128], np.asarray(cqw)[128:256]], 1))
    dwb = f32c(np.stack([np.asarray(ckb)[0:128], np.asarray(ckb)[128:256],
                         np.asarray(cvb)[0:128], np.asarray(cvb)[128:256],
                         np.asarray(cqb)[0:128], np.asarray(cqb)[128:256]], 1))
    in_common = dict(wkvq=wkvq, waet=waet, baet=baet, dwc=dwc, dwb=dwb,
                     mw1=f32c(np.asarray(mW1).T), mw2=f32c(np.asarray(mW2).T),
                     wo=f32c(np.asarray(Wo).T), gw=f32c(np.asarray(gW).T),
                     gb=f32c(np.asarray(gb)[None, :]), ng=f32c(np.asarray(norm_g)[None, :]),
                     eye=eye, triu=triu, tril=tril, sel=sel)
    in_maps = []
    for core in range(NCORES):
        b, hf = core // 2, core % 2
        im = dict(in_common)
        im["x"] = f32c(x_full[b])
        in_maps.append(im)
    res = run_bass_kernel_spmd(_S3_NC, in_maps, core_ids=list(range(NCORES)))
    y = np.empty((B, T, C), np.float32)
    for b in range(B):
        y[b] = res.results[2 * b]["y"]
    return y


# version 03
# speedup vs baseline: 1.0041x; 1.0041x over previous
import numpy as np
import ml_dtypes
from contextlib import ExitStack

import concourse.bass as bass
import concourse.mybir as mybir
from concourse.bass_utils import run_bass_kernel_spmd
from concourse.expressions import RuntimeValue

F32 = mybir.dt.float32
BF16 = mybir.dt.bfloat16
AF = mybir.ActivationFunctionType
ALU = mybir.AluOpType
AX = mybir.AxisListType.X
BF = ml_dtypes.bfloat16

NCORES = 8
B, T, C = 4, 4096, 1024
CHUNK = 512
HALF = T // 2
NCHOWN = HALF // CHUNK
LNEPS = 1e-5
LN0EPS = 1e-10


def f32c(x):
    return np.ascontiguousarray(np.asarray(x, dtype=np.float32))


class Ctr:
    def __init__(self, ctx, nc, name, step):
        self.sem = ctx.enter_context(nc.semaphore(name))
        self.n = 0
        self.step = step

    def mark(self, inst):
        self.n += self.step
        inst.then_inc(self.sem, self.step)
        return self.n


class Env:
    def __init__(self, nc, ctx):
        self.nc = nc
        self.ctx = ctx
        self.qd = Ctr(ctx, nc, "qd", 16)
        self.gd = Ctr(ctx, nc, "gd", 16)
        self.te = Ctr(ctx, nc, "te", 1)
        self.ve = Ctr(ctx, nc, "ve", 1)
        self.se = Ctr(ctx, nc, "se", 1)

    def sb(self, name, shape, dt=F32):
        return self.ctx.enter_context(self.nc.sbuf_tensor(name, shape, dt))

    def ps(self, name, shape, dt=F32):
        return self.ctx.enter_context(self.nc.psum_tensor(name, shape, dt))


def emit_postln(env, block, s_sb, st, junk_b, out_ap, eps_ap, sum_token, out2_ap=None,
                sum_eng="ve"):
    nc = env.nc

    @block.scalar
    def _(s):
        sem = env.ve.sem if sum_eng == "ve" else env.se.sem
        s.wait_ge(sem, sum_token)
        s.activation(junk_b[:, :], s_sb[:, :], AF.Square, accum_out=st[:, 1:2])
        env.se.mark(s.drain())
    sq_done = env.se.n

    return emit_ln_stats_apply(env, block, s_sb[:, :], st, out_ap, eps_ap,
                               sq_done, "se", out2_ap)


def emit_ln_stats_apply(env, block, src_ap, st, out_ap, eps_ap, sums_token,
                        sum_eng, out2_ap=None):
    @block.vector
    def _(v):
        v.wait_ge(env.se.sem if sum_eng == "se" else env.ve.sem, sums_token)
        v.tensor_scalar(st[:, 2:3], st[:, 0:1], 1.0 / C, None, ALU.mult)
        v.tensor_scalar(st[:, 3:4], st[:, 1:2], 1.0 / C, None, ALU.mult)
        v.drain()
        v.tensor_mul(st[:, 4:5], st[:, 2:3], st[:, 2:3])
        v.tensor_scalar(st[:, 8:9], st[:, 2:3], -1.0, None, ALU.mult)
        v.drain()
        v.tensor_sub(st[:, 5:6], st[:, 3:4], st[:, 4:5])
        env.ve.mark(v.drain())
    var_done = env.ve.n

    @block.scalar
    def _(s):
        s.wait_ge(env.ve.sem, var_done)
        inst = s.activation(st[:, 6:7], st[:, 5:6], AF.Sqrt, bias=eps_ap)
        env.se.mark(inst)
    sd_done = env.se.n

    @block.vector
    def _(v):
        v.wait_ge(env.se.sem, sd_done)
        v.reciprocal(st[:, 7:8], st[:, 6:7])
        v.drain()
        v.tensor_mul(st[:, 9:10], st[:, 8:9], st[:, 7:8])
        env.ve.mark(v.drain())
    nm_done = env.ve.n

    @block.scalar
    def _(s):
        s.wait_ge(env.ve.sem, nm_done)
        inst = s.activation(out_ap, src_ap, AF.Identity, bias=st[:, 9:10], scale=st[:, 7:8])
        if out2_ap is not None:
            inst = s.activation(out2_ap, src_ap, AF.Identity, bias=st[:, 9:10], scale=st[:, 7:8])
        env.se.mark(inst)
    return env.se.n


FFN_DEBUG = False


def build_ffn():
    nc = bass.Bass("TRN2", target_bir_lowering=False)
    x_d = nc.dram_tensor("x", [HALF, C], F32, kind="ExternalInput")
    hal_d = nc.dram_tensor("halos", [128, 8, NCHOWN], F32, kind="ExternalInput")
    wr_d = nc.dram_tensor("wr", [C, C], F32, kind="ExternalInput")
    wk_d = nc.dram_tensor("wk", [C, C], F32, kind="ExternalInput")
    wv_d = nc.dram_tensor("wv", [C, C], F32, kind="ExternalInput")
    mur_d = nc.dram_tensor("mur", [128, 8], F32, kind="ExternalInput")
    muk_d = nc.dram_tensor("muk", [128, 8], F32, kind="ExternalInput")
    eye_d = nc.dram_tensor("eye", [128, 128], F32, kind="ExternalInput")
    y_d = nc.dram_tensor("y", [HALF, C], F32, kind="ExternalOutput")
    if FFN_DEBUG:
        dbg = {n: nc.dram_tensor("dbg_" + n, sh, dt, kind="ExternalOutput")
               for n, sh, dt in [("xfm", [128, 8, CHUNK + 1], BF16),
                                 ("xr", [128, 8, CHUNK], BF16),
                                 ("xk", [128, 8, CHUNK], BF16),
                                 ("r", [128, C], F32),
                                 ("kact", [128, 8, CHUNK], BF16),
                                 ("s", [128, C], F32),
                                 ("st", [128, 16, 8], F32),
                                 ("sall", [128, 16, C], F32)]}

    with ExitStack() as ctx:
        env = Env(nc, ctx)
        sb, ps = env.sb, env.ps
        wr_b = sb("wr_b", [128, 8, C], BF16)
        wk_b = sb("wk_b", [128, 8, C], BF16)
        wv_b = sb("wv_b", [128, 8, C], BF16)
        mur_s = sb("mur_s", [128, 8])
        muk_s = sb("muk_s", [128, 8])
        eye_b = sb("eye_b", [128, 128], BF16)
        hal_s = sb("hal_s", [128, 8, NCHOWN])
        xres = sb("xres", [128, 4, C])
        xbf = sb("xbf", [128, 4, C], BF16)
        xfm = sb("xfm", [128, 8, CHUNK + 1], BF16)
        tmp_b = sb("tmp_b", [128, CHUNK], BF16)
        xr = sb("xr", [128, 8, CHUNK], BF16)
        xk = sb("xk", [128, 8, CHUNK], BF16)
        kact = sb("kact", [128, 8, CHUNK], BF16)
        krelu = sb("krelu", [128, 2, CHUNK])
        r_sb = sb("r_sb", [128, 4, C])
        o_sb = sb("o_sb", [128, C])
        s_sb = sb("s_sb", [128, C])
        junk_b = sb("junk_b", [128, C], BF16)
        st = sb("st", [128, 12])
        epsc = sb("epsc", [128, 1])
        ynew = sb("ynew", [128, 2, C])
        pT = [ps("pT0", [128, 1024], BF16), ps("pT1", [128, 1024], BF16)]
        pR = [ps("pR0", [128, 512]), ps("pR1", [128, 512])]
        pK = [ps("pK0", [128, 512]), ps("pK1", [128, 512])]
        pV = [ps("pV0", [128, 512]), ps("pV1", [128, 512])]
        block = ctx.enter_context(nc.Block())

        @block.gpsimd
        def _(g):
            for wsb, wd in ((wr_b, wr_d), (wk_b, wk_d), (wv_b, wv_d)):
                inst = g.dma_start(out=wsb[:, :, :],
                                   in_=wd.ap().rearrange("(k p) o -> p k o", p=128))
                env.gd.mark(inst)
            env.gd.mark(g.dma_start(out=eye_b[:, :], in_=eye_d[:, :]))
        wload_g = env.gd.n

        @block.vector
        def _(v):
            inst = v.memset(epsc[:, :], float(LNEPS))
            env.ve.mark(inst)

        @block.sync
        def _(q):
            env.qd.mark(q.dma_start(out=hal_s[:, :, :], in_=hal_d[:, :, :]))
            env.qd.mark(q.dma_start(out=mur_s[:, :], in_=mur_d[:, :]))
            env.qd.mark(q.dma_start(out=muk_s[:, :], in_=muk_d[:, :]))
        wload_q = env.qd.n

        evac_tok = [0, 0]
        prev_y_done = [0]
        sig_tok = {}
        kact_read = {}
        pv_tok = {}

        for ci in range(NCHOWN):
            t0 = ci * CHUNK

            @block.sync
            def _(q, t0=t0, ci=ci, ve_prev=env.ve.n):
                if ci > 0:
                    q.wait_ge(env.ve.sem, ve_prev)
                for tb in range(4):
                    env.qd.mark(q.dma_start(out=xres[:, tb, :],
                                            in_=x_d[t0 + tb * 128: t0 + (tb + 1) * 128, :]))
            loads_q = env.qd.n

            @block.vector
            def _(v, loads_q=loads_q, te_prev=env.te.n):
                v.wait_ge(env.qd.sem, loads_q)
                if te_prev:
                    v.wait_ge(env.te.sem, te_prev)
                for tb in range(4):
                    v.tensor_copy(xbf[:, tb, :], xres[:, tb, :])
                inst = v.tensor_copy(xfm[:, :, 0:1], hal_s[:, :, ci:ci + 1])
                env.ve.mark(inst)
            cast_done = env.ve.n

            for tb in range(4):
                @block.tensor
                def _(te, tb=tb, ci=ci):
                    if ci == 0 and tb == 0:
                        te.wait_ge(env.gd.sem, wload_g)
                    te.wait_ge(env.ve.sem, max(cast_done, evac_tok[tb % 2]))
                    for k in range(8):
                        inst = te.transpose(pT[tb % 2][:, k * 128:(k + 1) * 128],
                                            xbf[:, tb, k * 128:(k + 1) * 128], eye_b[:, :])
                    env.te.mark(inst)
                tr_done = env.te.n

                @block.vector
                def _(v, tb=tb, tr_done=tr_done):
                    v.wait_ge(env.te.sem, tr_done)
                    src = pT[tb % 2][:, :].rearrange("p (k t) -> p k t", k=8)
                    inst = v.tensor_copy(xfm[:, :, 1 + tb * 128: 1 + (tb + 1) * 128], src)
                    env.ve.mark(inst)
                evac_tok[tb % 2] = env.ve.n


            @block.vector
            def _(v, ci=ci):
                if ci == 0:
                    v.wait_ge(env.gd.sem, wload_g)
                    v.wait_ge(env.qd.sem, wload_q)
                for k in range(8):
                    v.tensor_sub(tmp_b[:, :], xfm[:, k, 1:513], xfm[:, k, 0:512])
                    v.scalar_tensor_tensor(xr[:, k, :], tmp_b[:, :], mur_s[:, k:k + 1],
                                           xfm[:, k, 0:512], ALU.mult, ALU.add)
                    inst = v.scalar_tensor_tensor(xk[:, k, :], tmp_b[:, :], muk_s[:, k:k + 1],
                                                  xfm[:, k, 0:512], ALU.mult, ALU.add)
                env.ve.mark(inst)
            mix_done = env.ve.n
            if FFN_DEBUG and ci == 0:
                @block.gpsimd
                def _(g):
                    g.wait_ge(env.ve.sem, mix_done)
                    env.gd.mark(g.dma_start(out=dbg["xfm"].ap(), in_=xfm[:, :, :]))
                    env.gd.mark(g.dma_start(out=dbg["xr"].ap(), in_=xr[:, :, :]))
                    env.gd.mark(g.dma_start(out=dbg["xk"].ap(), in_=xk[:, :, :]))

            for tb in range(4):
                for oh in range(2):
                    @block.tensor
                    def _(te, tb=tb, oh=oh):
                        te.wait_ge(env.ve.sem, mix_done)
                        prev = sig_tok.get(oh)
                        if prev:
                            te.wait_ge(env.se.sem, prev)
                        pbank = pR[oh]
                        for kf in range(8):
                            inst = te.matmul(pbank[:, :], xr[:, kf, tb * 128:(tb + 1) * 128],
                                             wr_b[:, kf, oh * 512:(oh + 1) * 512],
                                             start=(kf == 0), stop=(kf == 7))
                        env.te.mark(inst)
                    mm_done = env.te.n

                    @block.scalar
                    def _(s, tb=tb, oh=oh, mm_done=mm_done):
                        s.wait_ge(env.te.sem, mm_done)
                        inst = s.activation(r_sb[:, tb, oh * 512:(oh + 1) * 512],
                                            pR[oh][:, :], AF.Sigmoid)
                        env.se.mark(inst)
                    sig_tok[oh] = env.se.n

            for oc in range(8):
                @block.tensor
                def _(te, oc=oc):
                    if oc == 0:
                        te.wait_ge(env.ve.sem, mix_done)
                    prev = kact_read.get(oc % 2)
                    if prev:
                        te.wait_ge(env.se.sem, prev)
                    for kf in range(8):
                        inst = te.matmul(pK[oc % 2][:, :], wk_b[:, kf, oc * 128:(oc + 1) * 128],
                                         xk[:, kf, :], start=(kf == 0), stop=(kf == 7))
                    env.te.mark(inst)
                kk_done = env.te.n

                @block.vector
                def _(v, oc=oc, kk_done=kk_done):
                    v.wait_ge(env.te.sem, kk_done)
                    if oc >= 2:
                        v.wait_ge(env.se.sem, kact_read[oc % 2])
                    inst = v.tensor_scalar(krelu[:, oc % 2, :], pK[oc % 2][:, :],
                                           0.0, None, ALU.max)
                    env.ve.mark(inst)
                kr_done = env.ve.n

                @block.scalar
                def _(s, oc=oc, kr_done=kr_done):
                    s.wait_ge(env.ve.sem, kr_done)
                    inst = s.activation(kact[:, oc, :], krelu[:, oc % 2, :], AF.Square)
                    env.se.mark(inst)
                kact_read[oc % 2] = env.se.n
            kact_done = env.se.n
            if FFN_DEBUG and ci == 0:
                @block.gpsimd
                def _(g):
                    g.wait_ge(env.se.sem, kact_done)
                    env.gd.mark(g.dma_start(out=dbg["r"].ap(), in_=r_sb[:, 0, :]))
                    env.gd.mark(g.dma_start(out=dbg["kact"].ap(), in_=kact[:, :, :]))

            for tb in range(4):
                @block.tensor
                def _(te, tb=tb):
                    te.wait_ge(env.se.sem, kact_done)
                    for oh in range(2):
                        prev = pv_tok.get(oh)
                        if prev:
                            te.wait_ge(env.ve.sem, prev)
                        for oc in range(8):
                            inst = te.matmul(pV[oh][:, :], kact[:, oc, tb * 128:(tb + 1) * 128],
                                             wv_b[:, oc, oh * 512:(oh + 1) * 512],
                                             start=(oc == 0), stop=(oc == 7))
                        env.te.mark(inst)
                kv_done = env.te.n

                @block.vector
                def _(v, tb=tb, kv_done=kv_done, prev_y=prev_y_done[0]):
                    v.wait_ge(env.te.sem, kv_done)
                    if prev_y:
                        v.wait_ge(env.se.sem, prev_y)
                    for oh in range(2):
                        v.tensor_mul(o_sb[:, oh * 512:(oh + 1) * 512], pV[oh][:, :],
                                     r_sb[:, tb, oh * 512:(oh + 1) * 512])
                    v.scalar_tensor_tensor(s_sb[:, :], o_sb[:, :], 1.0, xres[:, tb, :],
                                             ALU.mult, ALU.add, accum_out=st[:, 0:1])
                    env.ve.mark(v.drain())
                pv_tok[0] = pv_tok[1] = env.ve.n
                sum_tok = env.ve.n
                if FFN_DEBUG and ci == 0 and tb == 0:
                    @block.gpsimd
                    def _(g, sum_tok=sum_tok):
                        g.wait_ge(env.ve.sem, sum_tok)
                        env.gd.mark(g.dma_start(out=dbg["s"].ap(), in_=s_sb[:, :]))

                y_done = emit_postln(env, block, s_sb, st, junk_b,
                                     ynew[:, tb % 2, :], epsc[:, 0:1], sum_tok)
                prev_y_done[0] = y_done
                if FFN_DEBUG:
                    @block.gpsimd
                    def _(g, y_done=y_done, idx=ci * 4 + tb):
                        g.wait_ge(env.se.sem, y_done)
                        env.gd.mark(g.dma_start(out=dbg["st"].ap()[:, idx, :],
                                                in_=st[:, :]))
                        env.gd.mark(g.dma_start(out=dbg["sall"].ap()[:, idx, :],
                                                in_=s_sb[:, :]))
                    @block.vector
                    def _(v):
                        v.wait_ge(env.gd.sem, env.gd.n)
                        env.ve.mark(v.engine_nop())

                @block.sync
                def _(q, ci=ci, tb=tb, y_done=y_done):
                    q.wait_ge(env.se.sem, y_done)
                    env.qd.mark(q.dma_start(
                        out=y_d[ci * CHUNK + tb * 128: ci * CHUNK + (tb + 1) * 128, :],
                        in_=ynew[:, tb % 2, :]))
                st_tok = env.qd.n

                @block.scalar
                def _(s, st_tok=st_tok):
                    if st_tok >= 32:
                        s.wait_ge(env.qd.sem, st_tok - 16)

        @block.sync
        def _(q):
            q.wait_ge(env.qd.sem, env.qd.n)
    return nc


_FFN_NC = None


def run_ffn(x_full, W_r, W_k, W_v, mu_r, mu_k):
    global _FFN_NC
    if _FFN_NC is None:
        _FFN_NC = build_ffn()
    eye = np.eye(128, dtype=np.float32)
    mur = f32c(np.asarray(mu_r).reshape(8, 128).T)
    muk = f32c(np.asarray(mu_k).reshape(8, 128).T)
    wr = f32c(np.asarray(W_r).T)
    wk = f32c(np.asarray(W_k).T)
    wv = f32c(np.asarray(W_v).T)
    in_maps = []
    for core in range(NCORES):
        b, h = core // 2, core % 2
        xo = f32c(x_full[b, h * HALF:(h + 1) * HALF])
        halos = np.zeros((128, 8, NCHOWN), np.float32)
        for ci in range(NCHOWN):
            tg = h * HALF + ci * CHUNK - 1
            if tg >= 0:
                halos[:, :, ci] = x_full[b, tg].reshape(8, 128).T
        in_maps.append(dict(x=xo, halos=halos, wr=wr, wk=wk, wv=wv,
                            mur=mur, muk=muk, eye=eye))
    res = run_bass_kernel_spmd(_FFN_NC, in_maps, core_ids=list(range(NCORES)))
    y = np.empty((B, T, C), np.float32)
    for core in range(NCORES):
        b, h = core // 2, core % 2
        y[b, h * HALF:(h + 1) * HALF] = res.results[core]["y"]
    return y


def kernel(**inputs):
    z = inputs
    x = f32c(z["L0_attn_x"])
    O = run_s1a(x, z["L0_attn_w_attn"], z["L0_attn_b_attn"])
    x = run_s1b(x, O, z["L0_attn_w_proj"], z["L0_attn_b_proj"],
                z["L0_attn_w_fc"], z["L0_attn_b_fc"],
                z["L0_attn_w_mlp_proj"], z["L0_attn_b_mlp_proj"])
    x = run_ffn(x, z["L0_ffn_W_r"], z["L0_ffn_W_k"], z["L0_ffn_W_v"],
                z["L0_ffn_mu_r"], z["L0_ffn_mu_k"])
    x = run_s3(x, z["L1_attn_Wk"], z["L1_attn_Wv"], z["L1_attn_Wq"], z["L1_attn_Wo"],
               z["L1_attn_ckw"], z["L1_attn_ckb"], z["L1_attn_cvw"], z["L1_attn_cvb"],
               z["L1_attn_cqw"], z["L1_attn_cqb"], z["L1_attn_mW1"], z["L1_attn_mW2"],
               z["L1_attn_aW"], z["L1_attn_ab"], z["L1_attn_eW"], z["L1_attn_eb"],
               z["L1_attn_tW"], z["L1_attn_tb"], z["L1_attn_norm_g"],
               z["L1_attn_gW"], z["L1_attn_gb"])
    x = run_ffn(x, z["L1_ffn_W_r"], z["L1_ffn_W_k"], z["L1_ffn_W_v"],
                z["L1_ffn_mu_r"], z["L1_ffn_mu_k"])
    return x


NH = 8
HD = 64
NSP = T // CHUNK


def build_s1a():
    nc = bass.Bass("TRN2", target_bir_lowering=False)
    x_d = nc.dram_tensor("x", [T, C], F32, kind="ExternalInput")
    wq_d = nc.dram_tensor("wq", [C, 512], F32, kind="ExternalInput")
    wk_d = nc.dram_tensor("wk", [C, 512], F32, kind="ExternalInput")
    wv_d = nc.dram_tensor("wv", [C, 512], F32, kind="ExternalInput")
    bq_d = nc.dram_tensor("bq", [128, 4], F32, kind="ExternalInput")
    bk_d = nc.dram_tensor("bk", [128, 4], F32, kind="ExternalInput")
    bv_d = nc.dram_tensor("bv", [1, 512], F32, kind="ExternalInput")
    msk_d = nc.dram_tensor("masks", [128, 4, 512], F32, kind="ExternalInput")
    eye_d = nc.dram_tensor("eye", [128, 128], F32, kind="ExternalInput")
    on2_d = nc.dram_tensor("ones2", [128, 2], F32, kind="ExternalInput")
    o_d = nc.dram_tensor("o", [NH, HD, T], BF16, kind="ExternalOutput")

    with ExitStack() as ctx:
        env = Env(nc, ctx)
        sb, ps = env.sb, env.ps
        wq_b = sb("wq_b", [128, 8, 512], BF16)
        wk_b = sb("wk_b", [128, 8, 512], BF16)
        wv_b = sb("wv_b", [128, 8, 512], BF16)
        bq_s = sb("bq_s", [128, 4])
        bk_s = sb("bk_s", [128, 4])
        bv_b = sb("bv_b", [1, 512], BF16)
        msk_s = sb("msk_s", [128, 4, 512])
        eye_b = sb("eye_b", [128, 128], BF16)
        eye_f = sb("eye_f", [128, 128])
        on2_b = sb("on2_b", [128, 2], BF16)
        onesc_f = sb("onesc_f", [1, 128])
        onesr_b = sb("onesr_b", [1, 128], BF16)
        xres = sb("xres", [128, 4, C])
        xbf = sb("xbf", [128, 4, C], BF16)
        junk_b = sb("junk_b", [128, C], BF16)
        st = sb("st", [128, 12])
        st4 = sb("st4", [128, 40])
        eps0 = sb("eps0", [128, 1])
        ln1b4 = sb("ln1b4", [128, 4, C], BF16)
        ln1fm = sb("ln1fm", [128, 8, CHUNK], BF16)
        q_all = sb("q_all", [128, 4, T], BF16)
        k_all = sb("k_all", [128, 4, T], BF16)
        v_all = sb("v_all", [128, 32, 4, 130], BF16)
        q2_b = sb("q2_b", [128, 512], BF16)
        ncoll = sb("ncoll", [128, 64])
        mhat = sb("mhat", [128, 1])
        m1x1 = sb("m1x1", [1, 4])
        p_sb = sb("p_sb", [128, 4, CHUNK], BF16)
        recl = sb("recl", [1, CHUNK])
        r_sb2 = sb("r_sb2", [64, CHUNK])
        onrm = sb("onrm", [64, CHUNK], BF16)
        pPool = ps("pPool", [128, 2048])
        pO = ps("pO", [128, 512])
        pR = ps("pR2", [64, 512])
        pN = ps("pN", [128, 512])
        pT = ps("pTa", [128, 1024], BF16)
        block = ctx.enter_context(nc.Block())

        @block.gpsimd
        def _(g):
            for wsb, wd in ((wq_b, wq_d), (wk_b, wk_d), (wv_b, wv_d)):
                env.gd.mark(g.dma_start(out=wsb[:, :, :],
                                        in_=wd.ap().rearrange("(f p) o -> p f o", p=128)))
            env.gd.mark(g.dma_start(out=eye_b[:, :], in_=eye_d[:, :]))
            env.gd.mark(g.dma_start(out=on2_b[:, :], in_=on2_d[:, :]))
            env.gd.mark(g.dma_start(out=bv_b[:, :], in_=bv_d[:, :]))
        wload_g = env.gd.n

        @block.sync
        def _(q):
            env.qd.mark(q.dma_start(out=bq_s[:, :], in_=bq_d[:, :]))
            env.qd.mark(q.dma_start(out=bk_s[:, :], in_=bk_d[:, :]))
            env.qd.mark(q.dma_start(out=msk_s[:, :, :], in_=msk_d[:, :, :]))
            env.qd.mark(q.dma_start(out=eye_f[:, :], in_=eye_d[:, :]))
        wload_q = env.qd.n

        @block.vector
        def _(v):
            v.wait_ge(env.gd.sem, wload_g)
            v.wait_ge(env.qd.sem, wload_q)
            v.memset(eps0[:, :], float(LN0EPS))
            v.memset(onesc_f[:, :], 1.0)
            v.memset(onesr_b[:, :], 1.0)
            v.memset(v_all[:, :, :, 64:65], 1.0)
            v.memset(v_all[:, :, :, 129:130], 1.0)
            env.ve.mark(v.drain())
        const_ve = env.ve.n

        evac_tok = [0]
        pn_free = [None]
        po_free = [0]
        q2_free = [0]
        nunit = [0]
        for sp in range(NSP):
            t0 = sp * CHUNK

            @block.sync
            def _(q, t0=t0, sp=sp, ve_prev=env.ve.n):
                if sp > 0:
                    q.wait_ge(env.ve.sem, ve_prev)
                for tb in range(4):
                    env.qd.mark(q.dma_start(out=xres[:, tb, :],
                                            in_=x_d[t0 + tb * 128: t0 + (tb + 1) * 128, :]))
            loads_q = env.qd.n

            @block.scalar
            def _(s, loads_q=loads_q, te_prev=env.te.n):
                s.wait_ge(env.qd.sem, loads_q)
                s.wait_ge(env.ve.sem, const_ve)
                if te_prev:
                    s.wait_ge(env.te.sem, te_prev)
                for tb in range(4):
                    s.activation(xbf[:, tb, :], xres[:, tb, :], AF.Identity,
                                 accum_out=st4[:, tb:tb + 1])
                    s.activation(junk_b[:, :], xres[:, tb, :], AF.Square,
                                 accum_out=st4[:, 4 + tb:5 + tb])
                env.se.mark(s.drain())
            sums_tok = env.se.n

            @block.vector
            def _(v, sums_tok=sums_tok):
                v.wait_ge(env.se.sem, sums_tok)
                v.tensor_scalar(st4[:, 8:12], st4[:, 0:4], 1.0 / C, None, ALU.mult)
                v.tensor_scalar(st4[:, 12:16], st4[:, 4:8], 1.0 / C, None, ALU.mult)
                v.drain()
                v.tensor_mul(st4[:, 16:20], st4[:, 8:12], st4[:, 8:12])
                v.tensor_scalar(st4[:, 20:24], st4[:, 8:12], -1.0, None, ALU.mult)
                v.drain()
                v.tensor_sub(st4[:, 24:28], st4[:, 12:16], st4[:, 16:20])
                env.ve.mark(v.drain())
            var_tok = env.ve.n

            @block.scalar
            def _(s, var_tok=var_tok):
                s.wait_ge(env.ve.sem, var_tok)
                inst = s.activation(st4[:, 28:32], st4[:, 24:28], AF.Sqrt, bias=eps0[:, 0:1])
                env.se.mark(inst)
            sd_tok = env.se.n

            @block.vector
            def _(v, sd_tok=sd_tok):
                v.wait_ge(env.se.sem, sd_tok)
                v.reciprocal(st4[:, 32:36], st4[:, 28:32])
                v.drain()
                v.tensor_mul(st4[:, 36:40], st4[:, 20:24], st4[:, 32:36])
                env.ve.mark(v.drain())
            nm_tok = env.ve.n

            @block.scalar
            def _(s, nm_tok=nm_tok):
                s.wait_ge(env.ve.sem, nm_tok)
                for tb in range(4):
                    inst = s.activation(ln1b4[:, tb, :], xres[:, tb, :], AF.Identity,
                                        bias=st4[:, 36 + tb:37 + tb],
                                        scale=st4[:, 32 + tb:33 + tb])
                env.se.mark(inst)
            ln_done = env.se.n

            for tb in range(4):
                @block.tensor
                def _(te, tb=tb, ln_done=ln_done, ev=evac_tok[0]):
                    if tb == 0:
                        te.wait_ge(env.se.sem, ln_done)
                    if ev:
                        te.wait_ge(env.ve.sem, ev)
                    for kf in range(8):
                        inst = te.transpose(pT[:, kf * 128:(kf + 1) * 128],
                                            ln1b4[:, tb, kf * 128:(kf + 1) * 128], eye_b[:, :])
                    env.te.mark(inst)
                tr_done = env.te.n

                @block.vector
                def _(v, tb=tb, tr_done=tr_done):
                    v.wait_ge(env.te.sem, tr_done)
                    src = pT[:, :].rearrange("p (f t) -> p f t", f=8)
                    inst = v.tensor_copy(ln1fm[:, :, tb * 128:(tb + 1) * 128], src)
                    env.ve.mark(inst)
                evac_tok[0] = env.ve.n
            fm_done = env.ve.n

            for wsb, bias_s, dst in ((wq_b, bq_s, q_all), (wk_b, bk_s, k_all)):
                for oc in range(4):
                    @block.tensor
                    def _(te, oc=oc, fm_done=fm_done, pf=pn_free[0], wsb=wsb):
                        te.wait_ge(env.ve.sem, fm_done)
                        if pf:
                            te.wait_ge(env.se.sem if pf[0] == "se" else env.ve.sem, pf[1])
                        for kf in range(8):
                            inst = te.matmul(pN[:, :], wsb[:, kf, oc * 128:(oc + 1) * 128],
                                             ln1fm[:, kf, :], start=(kf == 0), stop=(kf == 7))
                        env.te.mark(inst)
                    mm_tok = env.te.n

                    @block.scalar
                    def _(s, oc=oc, sp=sp, mm_tok=mm_tok, bias_s=bias_s, dst=dst,
                          q2f=q2_free[0]):
                        s.wait_ge(env.te.sem, mm_tok)
                        inst = s.activation(dst[:, oc, sp * CHUNK:(sp + 1) * CHUNK],
                                            pN[:, :], AF.Identity, bias=bias_s[:, oc:oc + 1])
                        env.se.mark(inst)
                        if q2f:
                            s.wait_ge(env.te.sem, q2f)
                        inst = s.activation(q2_b[:, :], pN[:, :], AF.Square,
                                            bias=bias_s[:, oc:oc + 1])
                        env.se.mark(inst)
                    pn_free[0] = ("se", env.se.n)
                    sq_tok = env.se.n

                    @block.tensor
                    def _(te, sq_tok=sq_tok, pof=po_free[0]):
                        te.wait_ge(env.se.sem, sq_tok)
                        if pof:
                            te.wait_ge(env.ve.sem, pof)
                        for blk in range(4):
                            inst = te.matmul(pO[:, blk * 2: blk * 2 + 2],
                                             q2_b[:, blk * 128:(blk + 1) * 128], on2_b[:, :],
                                             start=True, stop=True)
                        env.te.mark(inst)
                    q2_free[0] = env.te.n
                    nm_tok = env.te.n

                    @block.vector
                    def _(v, nm_tok=nm_tok, u=nunit[0]):
                        v.wait_ge(env.te.sem, nm_tok)
                        inst = v.reduce_max(ncoll[:, u:u + 1], pO[:, 0:8], axis=AX)
                        env.ve.mark(inst)
                    po_free[0] = env.ve.n
                    nunit[0] += 1

            for blk in range(4):
                @block.tensor
                def _(te, blk=blk, pf=pn_free[0]):
                    if pf:
                        te.wait_ge(env.se.sem if pf[0] == "se" else env.ve.sem, pf[1])
                    for kf in range(8):
                        te.matmul(pN[:, :], ln1fm[:, kf, blk * 128:(blk + 1) * 128],
                                  wv_b[:, kf, :], start=(kf == 0), stop=False)
                    inst = te.matmul(pN[:, :], onesr_b[0:1, :], bv_b[:, :],
                                     start=False, stop=True)
                    env.te.mark(inst)
                mm_tok = env.te.n

                @block.vector
                def _(v, sp=sp, blk=blk, mm_tok=mm_tok):
                    v.wait_ge(env.te.sem, mm_tok)
                    gblk = sp * 4 + blk
                    vv = pN[:, :].rearrange("p (h two d) -> p h two d", two=2, d=64)
                    v.tensor_copy(v_all[:, gblk, :, 0:64], vv[:, :, 0, :])
                    inst = v.tensor_copy(v_all[:, gblk, :, 65:129], vv[:, :, 1, :])
                    env.ve.mark(inst)
                pn_free[0] = ("ve", env.ve.n)

        @block.vector
        def _(v, tok=env.ve.n):
            inst = v.reduce_max(st[:, 10:11], ncoll[:, :], axis=AX)
            env.ve.mark(inst)
        nmx_tok = env.ve.n

        @block.tensor
        def _(te, nmx_tok=nmx_tok):
            te.wait_ge(env.ve.sem, nmx_tok)
            inst = te.transpose(pN[0:1, 0:128], st[:, 10:11], eye_f[:, :])
            env.te.mark(inst)
        tr_tok = env.te.n

        @block.vector
        def _(v, tr_tok=tr_tok):
            v.wait_ge(env.te.sem, tr_tok)
            v.reduce_max(m1x1[0:1, 0:1], pN[0:1, 0:128], axis=AX)
            v.drain()
            inst = v.tensor_scalar(m1x1[0:1, 1:2], m1x1[0:1, 0:1], -0.125, None, ALU.mult)
            env.ve.mark(v.drain())
        neg_tok = env.ve.n

        @block.tensor
        def _(te, neg_tok=neg_tok):
            te.wait_ge(env.ve.sem, neg_tok)
            inst = te.matmul(pN[:, 0:1], onesc_f[0:1, :], m1x1[0:1, 1:2],
                             start=True, stop=True)
            env.te.mark(inst)
        bc_tok = env.te.n

        @block.vector
        def _(v, bc_tok=bc_tok):
            v.wait_ge(env.te.sem, bc_tok)
            inst = v.tensor_copy(mhat[:, :], pN[:, 0:1])
            env.ve.mark(inst)
        mhat_tok = env.ve.n

        ppool_free = [0]
        psb_free = [0]
        po_attn = [0]
        pr_free = [0]
        store_tok = [0]
        for h in range(NH):
            pair, w = h // 2, h % 2
            for sp2 in range(NSP):
                ngr = sp2 + 1
                for g in range(ngr):
                    @block.tensor
                    def _(te, h=h, sp2=sp2, g=g, pair=pair, w=w,
                          ppf=ppool_free[0], mh=mhat_tok):
                        if ppf:
                            te.wait_ge(env.se.sem, ppf)
                        if h == 0 and sp2 == 0 and g == 0:
                            te.wait_ge(env.ve.sem, mh)
                        for j in range(4):
                            kt = g * 4 + j
                            inst = te.matmul(pPool[:, j * 512:(j + 1) * 512],
                                             k_all[w * 64:(w + 1) * 64, pair,
                                                   kt * 128:(kt + 1) * 128],
                                             q_all[w * 64:(w + 1) * 64, pair,
                                                   sp2 * CHUNK:(sp2 + 1) * CHUNK],
                                             start=True, stop=True)
                        env.te.mark(inst)
                    sc_tok = env.te.n

                    if g == sp2:
                        @block.vector
                        def _(v, sc_tok=sc_tok):
                            v.wait_ge(env.te.sem, sc_tok)
                            for j in range(4):
                                inst = v.tensor_tensor(pPool[:, j * 512:(j + 1) * 512],
                                                       pPool[:, j * 512:(j + 1) * 512],
                                                       msk_s[:, j, :], ALU.add)
                            env.ve.mark(inst)
                        msk_tok = env.ve.n
                    else:
                        msk_tok = None

                    @block.scalar
                    def _(s, sc_tok=sc_tok, msk_tok=msk_tok, pvf=psb_free[0]):
                        s.wait_ge(env.te.sem, sc_tok)
                        if msk_tok:
                            s.wait_ge(env.ve.sem, msk_tok)
                        if pvf:
                            s.wait_ge(env.te.sem, pvf)
                        inst = s.activation(p_sb[:, :, :], pPool[:, :], AF.Exp,
                                            bias=mhat[:, 0:1], scale=0.125)
                        env.se.mark(inst)
                    ppool_free[0] = env.se.n
                    exp_tok = env.se.n

                    @block.tensor
                    def _(te, h=h, sp2=sp2, g=g, pair=pair, w=w, exp_tok=exp_tok,
                          poa=po_attn[0]):
                        te.wait_ge(env.se.sem, exp_tok)
                        if g == 0 and poa:
                            te.wait_ge(env.ve.sem, poa)
                        for j in range(4):
                            kt = g * 4 + j
                            inst = te.matmul(pO[0:65, :],
                                             v_all[:, kt, pair, w * 65:(w + 1) * 65],
                                             p_sb[:, j, :],
                                             start=(g == 0 and j == 0),
                                             stop=(g == sp2 and j == 3))
                        env.te.mark(inst)
                    psb_free[0] = env.te.n
                pv_tok = env.te.n

                @block.vector
                def _(v, pv_tok=pv_tok, prf=pr_free[0]):
                    v.wait_ge(env.te.sem, pv_tok)
                    inst = v.reciprocal(recl[:, :], pO[64:65, :])
                    env.ve.mark(v.drain())
                rec_tok = env.ve.n

                @block.tensor
                def _(te, rec_tok=rec_tok, prf=pr_free[0]):
                    te.wait_ge(env.ve.sem, rec_tok)
                    if prf:
                        te.wait_ge(env.ve.sem, prf)
                    inst = te.matmul(pR[:, :], onesc_f[0:1, 0:64], recl[0:1, :],
                                     start=True, stop=True)
                    env.te.mark(inst)
                rb_tok = env.te.n

                @block.vector
                def _(v, rb_tok=rb_tok, stok=store_tok[0]):
                    v.wait_ge(env.te.sem, rb_tok)
                    if stok:
                        v.wait_ge(env.qd.sem, stok)
                    v.tensor_copy(r_sb2[:, :], pR[:, :])
                    v.drain()
                    inst = v.tensor_mul(onrm[:, :], pO[0:64, :], r_sb2[:, :])
                    env.ve.mark(inst)
                pr_free[0] = env.ve.n
                po_attn[0] = env.ve.n
                on_tok = env.ve.n

                @block.sync
                def _(q, h=h, sp2=sp2, on_tok=on_tok):
                    q.wait_ge(env.ve.sem, on_tok)
                    env.qd.mark(q.dma_start(
                        out=o_d.ap()[h, :, sp2 * CHUNK:(sp2 + 1) * CHUNK],
                        in_=onrm[:, :]))
                store_tok[0] = env.qd.n

        @block.sync
        def _(q):
            q.wait_ge(env.qd.sem, env.qd.n)
    return nc


_S1A_NC = None


def s1a_inputs(x_full, w_attn, b_attn):
    w_attn = np.asarray(w_attn)
    b_attn = np.asarray(b_attn)
    eye = np.eye(128, dtype=np.float32)
    ones2 = np.zeros((128, 2), np.float32)
    ones2[0:64, 0] = 1.0
    ones2[64:128, 1] = 1.0
    masks = np.zeros((128, 4, 512), np.float32)
    r = np.arange(128)[:, None]
    cc = np.arange(512)[None, :]
    for m in range(4):
        masks[:, m, :] = np.where(m * 128 + r <= cc, 0.0, -1e9)
    in_maps = []
    for core in range(NCORES):
        b, hg = core // 2, core % 2
        rs = hg * 512
        wq = f32c(w_attn[rs:rs + 512, :].T)
        wk = f32c(w_attn[1024 + rs:1024 + rs + 512, :].T)
        wv = f32c(w_attn[2048 + rs:2048 + rs + 512, :].T)
        bq = f32c(b_attn[rs:rs + 512].reshape(4, 128).T)
        bk = f32c(b_attn[1024 + rs:1024 + rs + 512].reshape(4, 128).T)
        bv = f32c(b_attn[2048 + rs:2048 + rs + 512][None, :])
        in_maps.append(dict(x=f32c(x_full[b]), wq=wq, wk=wk, wv=wv,
                            bq=bq, bk=bk, bv=bv, masks=masks, eye=eye, ones2=ones2))
    return in_maps


def run_s1a(x_full, w_attn, b_attn):
    global _S1A_NC
    if _S1A_NC is None:
        _S1A_NC = build_s1a()
    res = run_bass_kernel_spmd(_S1A_NC, s1a_inputs(x_full, w_attn, b_attn),
                               core_ids=list(range(NCORES)))
    O = np.empty((B, 16, HD, T), BF)
    for core in range(NCORES):
        b, hg = core // 2, core % 2
        O[b, hg * 8:(hg + 1) * 8] = np.asarray(res.results[core]["o"], dtype=BF)
    return O


S1B_PHASES = 3
S1B_D1MASK = 15
S1B_DBGX1 = False
S1B_LOAD_WBIG = True
S1B_NSP = NCHOWN


def build_s1b():
    nc = bass.Bass("TRN2", target_bir_lowering=False)
    x_d = nc.dram_tensor("x", [HALF, C], F32, kind="ExternalInput")
    o_in = nc.dram_tensor("oin", [16, HD, HALF], BF16, kind="ExternalInput")
    wp_d = nc.dram_tensor("wp", [C, C], F32, kind="ExternalInput")
    bp_d = nc.dram_tensor("bp", [1, C], F32, kind="ExternalInput")
    wf_d = nc.dram_tensor("wf", [C, 4 * C], F32, kind="ExternalInput")
    bf_d = nc.dram_tensor("bfc", [128, 32], F32, kind="ExternalInput")
    wm_d = nc.dram_tensor("wm", [4 * C, C], F32, kind="ExternalInput")
    bm_d = nc.dram_tensor("bm", [1, C], F32, kind="ExternalInput")
    eye_d = nc.dram_tensor("eye", [128, 128], F32, kind="ExternalInput")
    y_d = nc.dram_tensor("y", [HALF, C], F32, kind="ExternalOutput")
    ln2sc = nc.dram_tensor("ln2sc", [NCHOWN, 128, 8, CHUNK], BF16)
    x1sc = nc.dram_tensor("x1sc", [NCHOWN, 4, 128, C], F32)
    hsc = nc.dram_tensor("hsc", [NCHOWN, 32, 128, CHUNK], BF16)

    with ExitStack() as ctx:
        env = Env(nc, ctx)
        sb, ps = env.sb, env.ps
        wp_b = sb("wp_b", [128, 8, C], BF16)
        bp_b = sb("bp_b", [1, C], BF16)
        bm_b = sb("bm_b", [1, C], BF16)
        bf_s = sb("bf_s", [128, 32])
        wbig = sb("wbig", [128, 32768], BF16)
        eye_b = sb("eye_b", [128, 128], BF16)
        onesr_b = sb("onesr_b", [1, 128], BF16)
        o_sb = sb("o_sb", [128, 8, CHUNK], BF16)
        xres = sb("xres", [128, C])
        x1_sb = sb("x1_sb", [128, C])
        t1_sb = sb("t1_sb", [128, C])
        junk_b = sb("junk_b", [128, C], BF16)
        st = sb("st", [128, 12])
        eps0 = sb("eps0", [128, 1])
        epsl = sb("epsl", [128, 1])
        ln2_b = sb("ln2_b", [128, C], BF16)
        ln2fm = sb("ln2fm", [128, 8, CHUNK], BF16)
        h_sb = sb("h_sb", [128, 32, CHUNK], BF16)
        hfc = sb("hfc", [128, 2, CHUNK], BF16)
        ynew = sb("ynew", [128, 2, C])
        pXt = [ps("pX0", [128, 512]), ps("pX1", [128, 512]),
               ps("pX2", [128, 512]), ps("pX3", [128, 512])]
        pX = [pXt[0:2], pXt[2:4]]
        pF = [ps("pF0", [128, 512]), ps("pF1", [128, 512])]
        pT = ps("pTb", [128, 1024], BF16)
        block = ctx.enter_context(nc.Block())
        wf_v = wbig[:, :].rearrange("p (f o) -> p f o", f=8)
        wm_v = wbig[:, :].rearrange("p (h o) -> p h o", h=32)

        @block.gpsimd
        def _(g):
            env.gd.mark(g.dma_start(out=wp_b[:, :, :],
                                    in_=wp_d.ap().rearrange("(f p) o -> p f o", p=128)))
            env.gd.mark(g.dma_start(out=bp_b[:, :], in_=bp_d[:, :]))
            env.gd.mark(g.dma_start(out=bm_b[:, :], in_=bm_d[:, :]))
            env.gd.mark(g.dma_start(out=eye_b[:, :], in_=eye_d[:, :]))
            if S1B_LOAD_WBIG:
                env.gd.mark(g.dma_start(out=wbig[:, :].rearrange("p (f o) -> p f o", f=8),
                                        in_=wf_d.ap().rearrange("(f p) o -> p f o", p=128)))
        wload_g = env.gd.n

        @block.sync
        def _(q):
            env.qd.mark(q.dma_start(out=bf_s[:, :], in_=bf_d[:, :]))
        wload_q = env.qd.n

        @block.vector
        def _(v):
            v.wait_ge(env.gd.sem, wload_g)
            v.wait_ge(env.qd.sem, wload_q)
            v.memset(eps0[:, :], float(LN0EPS))
            v.memset(epsl[:, :], float(LNEPS))
            v.memset(onesr_b[:, :], 1.0)
            env.ve.mark(v.drain())
        const_ve = env.ve.n

        evac_tok = [0]
        px_free = {}
        x1st_tok = [0]
        ln2st_tok = [0]
        for sp in range(S1B_NSP):
            @block.sync
            def _(q, sp=sp, ve_prev=env.ve.n):
                if sp > 0:
                    q.wait_ge(env.ve.sem, ve_prev)
                src = o_in.ap()[:, :, sp * CHUNK:(sp + 1) * CHUNK]
                env.qd.mark(q.dma_start(out=o_sb[:, :, :],
                                        in_=src.rearrange("(pr two) d t -> (two d) pr t", two=2)))
            oload_q = env.qd.n

            for tb in range(4):
                @block.sync
                def _(q, sp=sp, tb=tb, se_prev=env.se.n):
                    if se_prev:
                        q.wait_ge(env.se.sem, se_prev)
                    env.qd.mark(q.dma_start(
                        out=xres[:, :],
                        in_=x_d[sp * CHUNK + tb * 128: sp * CHUNK + (tb + 1) * 128, :]))
                xload_q = env.qd.n

                @block.tensor
                def _(te, sp=sp, tb=tb, oload_q=oload_q, cv=const_ve,
                      pxf=px_free.get(0), xl=xload_q):
                    if not (S1B_D1MASK & 1):
                        return
                    if tb == 0:
                        te.wait_ge(env.qd.sem, oload_q)
                        te.wait_ge(env.ve.sem, cv)
                    if pxf:
                        te.wait_ge(env.ve.sem, pxf)
                    for ot in range(2):
                        for p in range(8):
                            te.matmul(pX[0][ot][:, :],
                                      o_sb[:, p, tb * 128:(tb + 1) * 128],
                                      wp_b[:, p, ot * 512:(ot + 1) * 512],
                                      start=(p == 0), stop=False)
                        inst = te.matmul(pX[0][ot][:, :], onesr_b[0:1, :],
                                         bp_b[0:1, ot * 512:(ot + 1) * 512],
                                         start=False, stop=True)
                    env.te.mark(inst)
                mm_tok = env.te.n

                @block.vector
                def _(v, mm_tok=mm_tok, xl=xload_q, x1st=x1st_tok[0]):
                    if not (S1B_D1MASK & 2):
                        v.wait_ge(env.te.sem, mm_tok)
                        env.ve.mark(v.tensor_copy(x1_sb[:, :], xres[:, :]))
                        return
                    v.wait_ge(env.te.sem, mm_tok)
                    v.wait_ge(env.qd.sem, xl)
                    if x1st:
                        v.wait_ge(env.gd.sem, x1st)
                    v.tensor_add(x1_sb[:, 0:512], pX[0][0][:, :], xres[:, 0:512])
                    v.tensor_add(x1_sb[:, 512:1024], pX[0][1][:, :], xres[:, 512:1024])
                    env.ve.mark(v.drain())
                px_free[0] = env.ve.n
                sum_tok = env.ve.n

                if S1B_D1MASK & 2:
                    @block.scalar
                    def _(s, sum_tok=sum_tok):
                        s.wait_ge(env.ve.sem, sum_tok)
                        s.activation(junk_b[:, :], x1_sb[:, :], AF.Identity, accum_out=st[:, 0:1])
                        s.activation(junk_b[:, :], x1_sb[:, :], AF.Square, accum_out=st[:, 1:2])
                        env.se.mark(s.drain())
                    sq_tok = env.se.n

                    ln_done = emit_ln_stats_apply(env, block, x1_sb[:, :], st, ln2_b[:, :],
                                                  eps0[:, 0:1], sq_tok, "se")
                else:
                    @block.scalar
                    def _(s, sum_tok=sum_tok):
                        s.wait_ge(env.ve.sem, sum_tok)
                        env.se.mark(s.copy(ln2_b[:, :], x1_sb[:, :]))
                    ln_done = env.se.n

                if S1B_D1MASK & 4:
                    @block.gpsimd
                    def _(g, sp=sp, tb=tb, ln_done=ln_done):
                        g.wait_ge(env.se.sem, ln_done)
                        env.gd.mark(g.dma_start(out=x1sc.ap()[sp, tb, :, :], in_=x1_sb[:, :]))
                        if S1B_DBGX1:
                            env.gd.mark(g.dma_start(
                                out=y_d[sp * CHUNK + tb * 128: sp * CHUNK + (tb + 1) * 128, :],
                                in_=x1_sb[:, :]))
                    x1st_tok[0] = env.gd.n

                if not (S1B_D1MASK & 8):
                    continue
                @block.tensor
                def _(te, ln_done=ln_done, ev=evac_tok[0]):
                    te.wait_ge(env.se.sem, ln_done)
                    if ev:
                        te.wait_ge(env.ve.sem, ev)
                    for kf in range(8):
                        inst = te.transpose(pT[:, kf * 128:(kf + 1) * 128],
                                            ln2_b[:, kf * 128:(kf + 1) * 128], eye_b[:, :])
                    env.te.mark(inst)
                tr_done = env.te.n

                @block.vector
                def _(v, tb=tb, tr_done=tr_done, lst=ln2st_tok[0]):
                    v.wait_ge(env.te.sem, tr_done)
                    if tb == 0 and lst:
                        v.wait_ge(env.gd.sem, lst)
                    src = pT[:, :].rearrange("p (f t) -> p f t", f=8)
                    inst = v.tensor_copy(ln2fm[:, :, tb * 128:(tb + 1) * 128], src)
                    env.ve.mark(inst)
                evac_tok[0] = env.ve.n
            fm_done = env.ve.n

            if S1B_D1MASK & 8:
                @block.gpsimd
                def _(g, sp=sp, fm_done=fm_done):
                    g.wait_ge(env.ve.sem, fm_done)
                    env.gd.mark(g.dma_start(out=ln2sc.ap()[sp, :, :, :], in_=ln2fm[:, :, :]))
                ln2st_tok[0] = env.gd.n
        d1_g = env.gd.n
        d1_ve = env.ve.n

        if S1B_PHASES < 2:
            @block.sync
            def _(q):
                q.wait_ge(env.gd.sem, env.gd.n)
                q.wait_ge(env.qd.sem, env.qd.n)
            return nc
        pf_free = {}
        for sp in range(NCHOWN):
            @block.sync
            def _(q, sp=sp, d1_g=d1_g, te_prev=env.te.n):
                if sp == 0:
                    q.wait_ge(env.gd.sem, d1_g)
                else:
                    q.wait_ge(env.te.sem, te_prev)
                env.qd.mark(q.dma_start(out=ln2fm[:, :, :], in_=ln2sc.ap()[sp, :, :, :]))
            lload_q = env.qd.n

            for hc in range(32):
                @block.tensor
                def _(te, hc=hc, lload_q=lload_q, pff=pf_free.get(hc % 2)):
                    if hc == 0:
                        te.wait_ge(env.qd.sem, lload_q)
                    if pff:
                        te.wait_ge(env.se.sem, pff)
                    for kf in range(8):
                        inst = te.matmul(pF[hc % 2][:, :], wf_v[:, kf, hc * 128:(hc + 1) * 128],
                                         ln2fm[:, kf, :], start=(kf == 0), stop=(kf == 7))
                    env.te.mark(inst)
                mm_tok = env.te.n

                @block.scalar
                def _(s, hc=hc, mm_tok=mm_tok, gst=env.gd.n):
                    s.wait_ge(env.te.sem, mm_tok)
                    if gst >= 32:
                        s.wait_ge(env.gd.sem, gst - 16)
                    inst = s.activation(hfc[:, hc % 2, :], pF[hc % 2][:, :],
                                        AF.Gelu_apprx_tanh, bias=bf_s[:, hc:hc + 1])
                    env.se.mark(inst)
                pf_free[hc % 2] = env.se.n
                g_tok = env.se.n

                @block.gpsimd
                def _(g, sp=sp, hc=hc, g_tok=g_tok):
                    g.wait_ge(env.se.sem, g_tok)
                    env.gd.mark(g.dma_start(out=hsc.ap()[sp, hc, :, :],
                                            in_=hfc[:, hc % 2, :]))
        d2_g = env.gd.n

        if S1B_PHASES < 3:
            @block.sync
            def _(q):
                q.wait_ge(env.gd.sem, env.gd.n)
                q.wait_ge(env.qd.sem, env.qd.n)
            return nc
        @block.gpsimd
        def _(g, d2_g=d2_g, te_prev=env.te.n):
            g.wait_ge(env.gd.sem, d2_g)
            g.wait_ge(env.te.sem, te_prev)
            env.gd.mark(g.dma_start(out=wbig[:, :].rearrange("p (h o) -> p h o", h=32),
                                    in_=wm_d.ap().rearrange("(h p) o -> p h o", p=128)))
        wm_g = env.gd.n

        px_free2 = {}
        ylast = [0]
        for sp in range(NCHOWN):
            @block.sync
            def _(q, sp=sp, wm_g=wm_g, te_prev=env.te.n):
                if sp == 0:
                    q.wait_ge(env.gd.sem, wm_g)
                else:
                    q.wait_ge(env.te.sem, te_prev)
                env.qd.mark(q.dma_start(out=h_sb[:, :, :],
                                        in_=hsc.ap()[sp].rearrange("h p t -> p h t")))
            hload_q = env.qd.n

            for tb in range(4):
                @block.sync
                def _(q, sp=sp, tb=tb, ve_prev=env.ve.n):
                    if ve_prev:
                        q.wait_ge(env.ve.sem, ve_prev)
                    env.qd.mark(q.dma_start(
                        out=xres[:, :],
                        in_=x_d[sp * CHUNK + tb * 128: sp * CHUNK + (tb + 1) * 128, :]))
                    env.qd.mark(q.dma_start(out=x1_sb[:, :], in_=x1sc.ap()[sp, tb, :, :]))
                xl_q = env.qd.n

                @block.tensor
                def _(te, tb=tb, hload_q=hload_q, pxf=px_free2.get(tb % 2)):
                    if tb == 0:
                        te.wait_ge(env.qd.sem, hload_q)
                    if pxf:
                        te.wait_ge(env.ve.sem, pxf)
                    for ot in range(2):
                        for hc in range(32):
                            te.matmul(pX[tb % 2][ot][:, :],
                                      h_sb[:, hc, tb * 128:(tb + 1) * 128],
                                      wm_v[:, hc, ot * 512:(ot + 1) * 512],
                                      start=(hc == 0), stop=False)
                        inst = te.matmul(pX[tb % 2][ot][:, :], onesr_b[0:1, :],
                                         bm_b[0:1, ot * 512:(ot + 1) * 512],
                                         start=False, stop=True)
                    env.te.mark(inst)
                mm_tok = env.te.n

                @block.vector
                def _(v, tb=tb, mm_tok=mm_tok, xl=xl_q, yl=ylast[0]):
                    v.wait_ge(env.te.sem, mm_tok)
                    v.wait_ge(env.qd.sem, xl)
                    if yl:
                        v.wait_ge(env.se.sem, yl)
                    v.tensor_add(t1_sb[:, 0:512], pX[tb % 2][0][:, :], x1_sb[:, 0:512])
                    v.tensor_add(t1_sb[:, 512:1024], pX[tb % 2][1][:, :],
                                 x1_sb[:, 512:1024])
                    v.drain()
                    v.scalar_tensor_tensor(t1_sb[:, :], t1_sb[:, :], 1.0, xres[:, :],
                                           ALU.mult, ALU.add, accum_out=st[:, 0:1])
                    env.ve.mark(v.drain())
                px_free2[tb % 2] = env.ve.n
                sum_tok = env.ve.n

                @block.scalar
                def _(s, sum_tok=sum_tok):
                    s.wait_ge(env.ve.sem, sum_tok)
                    s.activation(junk_b[:, :], t1_sb[:, :], AF.Square, accum_out=st[:, 1:2])
                    env.se.mark(s.drain())
                sq_tok = env.se.n

                y_done = emit_ln_stats_apply(env, block, t1_sb[:, :], st,
                                             ynew[:, tb % 2, :], epsl[:, 0:1], sq_tok, "se")
                ylast[0] = y_done

                @block.sync
                def _(q, sp=sp, tb=tb, y_done=y_done):
                    q.wait_ge(env.se.sem, y_done)
                    env.qd.mark(q.dma_start(
                        out=y_d[sp * CHUNK + tb * 128: sp * CHUNK + (tb + 1) * 128, :],
                        in_=ynew[:, tb % 2, :]))
                yst = env.qd.n

                @block.scalar
                def _(s, yst=yst):
                    s.wait_ge(env.qd.sem, yst - 16)

        @block.sync
        def _(q):
            q.wait_ge(env.qd.sem, env.qd.n)
    return nc


_S1B_NC = None


def run_s1b(x_full, O_fm, w_proj, b_proj, w_fc, b_fc, w_mlp, b_mlp):
    global _S1B_NC
    if _S1B_NC is None:
        _S1B_NC = build_s1b()
    eye = np.eye(128, dtype=np.float32)
    wp = f32c(np.asarray(w_proj).T)
    bp = f32c(np.asarray(b_proj)[None, :])
    wf = f32c(np.asarray(w_fc).T)
    bfc = f32c(np.asarray(b_fc).reshape(32, 128).T)
    wm = f32c(np.asarray(w_mlp).T)
    bm = f32c(np.asarray(b_mlp)[None, :])
    in_maps = []
    for core in range(NCORES):
        b, hf = core // 2, core % 2
        in_maps.append(dict(
            x=f32c(x_full[b, hf * HALF:(hf + 1) * HALF]),
            oin=np.ascontiguousarray(O_fm[b][:, :, hf * HALF:(hf + 1) * HALF]),
            wp=wp, bp=bp, wf=wf, bfc=bfc, wm=wm, bm=bm, eye=eye))
    res = run_bass_kernel_spmd(_S1B_NC, in_maps, core_ids=list(range(NCORES)))
    y = np.empty((B, T, C), np.float32)
    for core in range(NCORES):
        b, hf = core // 2, core % 2
        y[b, hf * HALF:(hf + 1) * HALF] = res.results[core]["y"]
    return y


DD = 256
NSUB = 128
GS3 = 0.1
EPS3 = 1e-6


S3_PHASES = 4
S3_DEBUG = False


def build_s3():
    nc = bass.Bass("TRN2", target_bir_lowering=False)
    x_d = nc.dram_tensor("x", [T, C], F32, kind="ExternalInput")
    wkvq_d = nc.dram_tensor("wkvq", [C, 768], F32, kind="ExternalInput")
    waet_d = nc.dram_tensor("waet", [C, 768], F32, kind="ExternalInput")
    baet_d = nc.dram_tensor("baet", [128, 6], F32, kind="ExternalInput")
    dwc_d = nc.dram_tensor("dwc", [128, 6, 4], F32, kind="ExternalInput")
    dwb_d = nc.dram_tensor("dwb", [128, 6], F32, kind="ExternalInput")
    mw1_d = nc.dram_tensor("mw1", [DD, DD], F32, kind="ExternalInput")
    mw2_d = nc.dram_tensor("mw2", [DD, DD], F32, kind="ExternalInput")
    wo_d = nc.dram_tensor("wo", [DD, C], F32, kind="ExternalInput")
    gw_d = nc.dram_tensor("gw", [C, C], F32, kind="ExternalInput")
    gb_d = nc.dram_tensor("gb", [1, C], F32, kind="ExternalInput")
    ng_d = nc.dram_tensor("ng", [1, C], F32, kind="ExternalInput")
    eye_d = nc.dram_tensor("eye", [128, 128], F32, kind="ExternalInput")
    triu_d = nc.dram_tensor("triu", [128, 128], F32, kind="ExternalInput")
    tril_d = nc.dram_tensor("tril", [128, 128], F32, kind="ExternalInput")
    sel_d = nc.dram_tensor("sel", [128, 4], F32, kind="ExternalInput")
    y_d = nc.dram_tensor("y", [T, C], F32, kind="ExternalOutput")
    dbgkind = dict(kind="ExternalOutput") if S3_DEBUG else {}
    praw_d = nc.dram_tensor("praw", [6, 128, T], BF16, **dbgkind)
    kdw_d = nc.dram_tensor("kdw", [6, 128, T], BF16, **dbgkind)
    retr_d = nc.dram_tensor("retr", [32 * 128, DD], F32, **dbgkind)
    if S3_DEBUG:
        dbgst = {n: nc.dram_tensor("dbg_" + n, [128, 2, DD], F32, kind="ExternalOutput")
                 for n in ("w1", "w2", "s1", "s2")}
        dbgco = nc.dram_tensor("dbg_co", [128, 640], F32, kind="ExternalOutput")

    with ExitStack() as ctx:
        env = Env(nc, ctx)
        sb, ps = env.sb, env.ps
        wkvq_b = sb("wkvq_b", [128, 8, 768], BF16)
        waet_b = sb("waet_b", [128, 8, 768], BF16)
        baet_s = sb("baet_s", [128, 6])
        dwc_s = sb("dwc_s", [128, 6, 4])
        dwb_s = sb("dwb_s", [128, 6])
        wo_b = sb("wo_b", [128, 2, C], BF16)
        gw_b = sb("gw_b", [128, 8, C], BF16)
        gb_b = sb("gb_b", [1, C], BF16)
        ng_s = sb("ng_s", [1, C])
        eye_b = sb("eye_b", [128, 128], BF16)
        eye_f = sb("eye_f", [128, 128])
        triu_f = sb("triu_f", [128, 128])
        tril_f = sb("tril_f", [128, 128])
        sel_f = sb("sel_f", [128, 4])
        onesr_b = sb("onesr_b", [1, 128], BF16)
        onesr_f = sb("onesr_f", [1, 128])
        xres = sb("xres", [128, 4, C])
        xbf = sb("xbf", [128, 4, C], BF16)
        xfm = sb("xfm", [128, 8, CHUNK], BF16)
        junk_b = sb("junk_b", [128, C], BF16)
        st = sb("st", [128, 12])
        epsl = sb("epsl", [128, 1])
        eps3c = sb("eps3c", [128, 1])
        scal = sb("scal", [128, 32, 4])
        ubig = sb("ubig", [128, 2 * T])
        sig_s = ubig[:, 0:1024].rearrange("p (a b) -> p a b", a=2)
        sp_s = ubig[:, 1024:2048].rearrange("p (a b) -> p a b", a=2)
        cacc = [ubig[:, 0:T], ubig[:, T:2 * T]]
        crow = sb("crow", [128, T], BF16)
        w1t = sb("w1t", [128, 2, DD]); w2t = sb("w2t", [128, 2, DD])
        s1t = sb("s1t", [128, 2, DD]); s2t = sb("s2t", [128, 2, DD])
        w1b = sb("w1b", [128, 2, DD], BF16); w2b = sb("w2b", [128, 2, DD], BF16)
        s1b = sb("s1b", [128, 2, DD], BF16); s2b = sb("s2b", [128, 2, DD], BF16)
        w2pb = sb("w2pb", [128, 2, DD], BF16)
        kc_b = sb("kc_b", [128, 2, CHUNK], BF16)
        vc_b = sb("vc_b", [128, 2, CHUNK], BF16)
        qc_b = sb("qc_b", [128, 2, CHUNK], BF16)
        afm = sb("afm", [128, 2, CHUNK], BF16)
        dsfm = sb("dsfm", [128, 2, CHUNK])
        efm = sb("efm", [128, 2, CHUNK], BF16)
        ufm = sb("ufm", [128, 2, CHUNK], BF16)
        n2 = sb("n2", [128, 2, CHUNK], BF16)
        wthc = sb("wthc", [128, 4])
        gncol = sb("gncol", [128, 24])
        clarow = sb("clarow", [1, 128]); clerow = sb("clerow", [1, 128])
        lacol = sb("lacol", [128, 6])
        expin = sb("expin", [128, 128])
        m1t_f = sb("m1t_f", [128, 128]); m2t_f = sb("m2t_f", [128, 128])
        dt_sb = sb("dt_sb", [128, 128])
        acol = sb("acol", [128, 4])
        arow = sb("arow", [1, 128])
        lasts = sb("lasts", [128, 4])
        eyeA = sb("eyeA", [128, 128], BF16)
        eyeB = sb("eyeB", [128, 128], BF16)
        eyeE = sb("eyeE", [128, 128], BF16)
        ktm = sb("ktm", [128, DD], BF16)
        utm = sb("utm", [128, DD], BF16)
        etm = sb("etm", [128, DD], BF16)
        atm = sb("atm", [128, DD], BF16)
        kw_b = sb("kw_b", [128, DD], BF16)
        aw_b = sb("aw_b", [128, DD], BF16)
        kw2_b = sb("kw2_b", [128, DD], BF16)
        aw2_b = sb("aw2_b", [128, DD], BF16)
        hsb2 = sb("hsb2", [128, DD])
        onesc_f = sb("onesc_f", [128, 1])
        onesc_b = sb("onesc_b", [128, 1], BF16)
        negcle = sb("negcle", [1, 128])
        p1_b = sb("p1_b", [128, 128], BF16)
        p2_b = sb("p2_b", [128, 128], BF16)
        hsb = sb("hsb", [128, DD])
        hstm = sb("hstm", [128, DD], BF16)
        hsfm = sb("hsfm", [128, 2, 128], BF16)
        ott = sb("ott", [128, DD])
        retfm = sb("retfm", [128, 2, 128], BF16)
        ret_s = sb("ret_s", [128, C])
        gate_s = sb("gate_s", [128, C])
        g128 = sb("g128", [128, C])
        o3_s = sb("o3_s", [128, C])
        s_sb = sb("s_sb", [128, C])
        ynew = sb("ynew", [128, 2, C])
        dl_c = sb("dl_c", [128, 4])
        pA = ps("pA", [128, 512])
        pB = ps("pB", [128, 512])
        pC = ps("pC", [128, 512])
        pD = ps("pD", [128, 512])
        pE = ps("pE", [128, 512])
        pF2 = ps("pF2", [128, 512])
        pT = ps("pT3", [128, 1024], BF16)
        block = ctx.enter_context(nc.Block())

        @block.gpsimd
        def _(g):
            env.gd.mark(g.dma_start(out=wkvq_b[:, :, :],
                                    in_=wkvq_d.ap().rearrange("(f p) o -> p f o", p=128)))
            env.gd.mark(g.dma_start(out=waet_b[:, :, :],
                                    in_=waet_d.ap().rearrange("(f p) o -> p f o", p=128)))
            env.gd.mark(g.dma_start(out=wo_b[:, :, :],
                                    in_=wo_d.ap().rearrange("(f p) o -> p f o", p=128)))
            env.gd.mark(g.dma_start(out=gw_b[:, :, :],
                                    in_=gw_d.ap().rearrange("(f p) o -> p f o", p=128)))
            env.gd.mark(g.dma_start(out=gb_b[:, :], in_=gb_d[:, :]))
            env.gd.mark(g.dma_start(out=eye_b[:, :], in_=eye_d[:, :]))
            env.gd.mark(g.dma_start(out=w1t[:, :, :],
                                    in_=mw1_d.ap().rearrange("(c p) o -> p c o", p=128)))
            env.gd.mark(g.dma_start(out=w2t[:, :, :],
                                    in_=mw2_d.ap().rearrange("(c p) o -> p c o", p=128)))
        wload_g = env.gd.n

        @block.sync
        def _(q):
            env.qd.mark(q.dma_start(out=baet_s[:, :], in_=baet_d[:, :]))
            env.qd.mark(q.dma_start(out=dwc_s[:, :, :], in_=dwc_d[:, :, :]))
            env.qd.mark(q.dma_start(out=dwb_s[:, :], in_=dwb_d[:, :]))
            env.qd.mark(q.dma_start(out=eye_f[:, :], in_=eye_d[:, :]))
            env.qd.mark(q.dma_start(out=triu_f[:, :], in_=triu_d[:, :]))
            env.qd.mark(q.dma_start(out=tril_f[:, :], in_=tril_d[:, :]))
            env.qd.mark(q.dma_start(out=sel_f[:, :], in_=sel_d[:, :]))
            env.qd.mark(q.dma_start(out=ng_s[:, :], in_=ng_d[:, :]))
        wload_q = env.qd.n

        @block.vector
        def _(v):
            v.wait_ge(env.gd.sem, wload_g)
            v.wait_ge(env.qd.sem, wload_q)
            v.memset(epsl[:, :], float(LNEPS))
            v.memset(eps3c[:, :], float(EPS3))
            v.memset(onesr_b[:, :], 1.0)
            v.memset(onesr_f[:, :], 1.0)
            v.memset(onesc_f[:, :], 1.0)
            v.memset(onesc_b[:, :], 1.0)
            v.memset(s1t[:, :, :], 0.0)
            v.memset(s2t[:, :, :], 0.0)
            v.memset(s1b[:, :, :], 0.0)
            v.memset(s2b[:, :, :], 0.0)
            v.tensor_copy(w1b[:, :, :], w1t[:, :, :])
            inst = v.tensor_copy(w2b[:, :, :], w2t[:, :, :])
            env.ve.mark(v.drain())
        const_ve = env.ve.n

        @block.tensor
        def _(te):
            te.wait_ge(env.ve.sem, const_ve)
            ng_bf = junk_b
            te.wait_ge(env.ve.sem, const_ve)
            inst = te.matmul(pA[:, :], onesr_f[0:1, :], ng_s[0:1, 0:512], start=True, stop=True)
            inst = te.matmul(pB[:, :], onesr_f[0:1, :], ng_s[0:1, 512:1024], start=True, stop=True)
            env.te.mark(inst)
        ngb_tok = env.te.n

        @block.vector
        def _(v, ngb_tok=ngb_tok):
            v.wait_ge(env.te.sem, ngb_tok)
            v.tensor_copy(g128[:, 0:512], pA[:, :])
            inst = v.tensor_copy(g128[:, 512:1024], pB[:, :])
            env.ve.mark(v.drain())
        g128_tok = env.ve.n

        evac_tok = [0]
        pn_free = [None]
        for sp in range(NSP):
            t0 = sp * CHUNK

            @block.sync
            def _(q, t0=t0, sp=sp, ve_prev=env.ve.n):
                if sp > 0:
                    q.wait_ge(env.ve.sem, ve_prev)
                for tb in range(4):
                    env.qd.mark(q.dma_start(out=xres[:, tb, :],
                                            in_=x_d[t0 + tb * 128: t0 + (tb + 1) * 128, :]))
            loads_q = env.qd.n

            @block.vector
            def _(v, loads_q=loads_q, te_prev=env.te.n, cv=const_ve):
                v.wait_ge(env.qd.sem, loads_q)
                v.wait_ge(env.ve.sem, cv)
                if te_prev:
                    v.wait_ge(env.te.sem, te_prev)
                for tb in range(4):
                    inst = v.tensor_copy(xbf[:, tb, :], xres[:, tb, :])
                env.ve.mark(inst)
            cast_done = env.ve.n

            for tb in range(4):
                @block.tensor
                def _(te, tb=tb, cast_done=cast_done, ev=evac_tok[0]):
                    te.wait_ge(env.ve.sem, cast_done)
                    if ev:
                        te.wait_ge(env.ve.sem, ev)
                    for kf in range(8):
                        inst = te.transpose(pT[:, kf * 128:(kf + 1) * 128],
                                            xbf[:, tb, kf * 128:(kf + 1) * 128], eye_b[:, :])
                    env.te.mark(inst)
                tr_done = env.te.n

                @block.vector
                def _(v, tb=tb, tr_done=tr_done):
                    v.wait_ge(env.te.sem, tr_done)
                    src = pT[:, :].rearrange("p (f t) -> p f t", f=8)
                    inst = v.tensor_copy(xfm[:, :, tb * 128:(tb + 1) * 128], src)
                    env.ve.mark(inst)
                evac_tok[0] = env.ve.n
            fm_done = env.ve.n

            for oc in range(6):
                @block.tensor
                def _(te, oc=oc, fm_done=fm_done, pf=pn_free[0]):
                    te.wait_ge(env.ve.sem, fm_done)
                    if pf:
                        te.wait_ge(env.se.sem if pf[0] == "se" else env.ve.sem, pf[1])
                    for kf in range(8):
                        inst = te.matmul(pA[:, :], wkvq_b[:, kf, oc * 128:(oc + 1) * 128],
                                         xfm[:, kf, :], start=(kf == 0), stop=(kf == 7))
                    env.te.mark(inst)
                mm_tok = env.te.n

                @block.scalar
                def _(s, oc=oc, sp=sp, mm_tok=mm_tok, gprev=env.gd.n):
                    s.wait_ge(env.te.sem, mm_tok)
                    if gprev >= 16 * 10:
                        s.wait_ge(env.gd.sem, gprev - 16)
                    inst = s.activation(crow[:, sp * CHUNK:(sp + 1) * CHUNK], pA[:, :],
                                        AF.Identity)
                    env.se.mark(inst)
                pn_free[0] = ("se", env.se.n)
                ev_tok = env.se.n

                @block.gpsimd
                def _(g, oc=oc, sp=sp, ev_tok=ev_tok):
                    g.wait_ge(env.se.sem, ev_tok)
                    env.gd.mark(g.dma_start(out=praw_d.ap()[oc, :, sp * CHUNK:(sp + 1) * CHUNK],
                                            in_=crow[:, sp * CHUNK:(sp + 1) * CHUNK]))

            for pi in range(3):
                for h2 in range(2):
                    oc = pi * 2 + h2

                    @block.tensor
                    def _(te, oc=oc, fm_done=fm_done, pf=pn_free[0]):
                        te.wait_ge(env.ve.sem, fm_done)
                        if pf:
                            te.wait_ge(env.se.sem if pf[0] == "se" else env.ve.sem, pf[1])
                        for kf in range(8):
                            inst = te.matmul(pB[:, :], waet_b[:, kf, oc * 128:(oc + 1) * 128],
                                             xfm[:, kf, :], start=(kf == 0), stop=(kf == 7))
                        env.te.mark(inst)
                    mm_tok = env.te.n

                    @block.scalar
                    def _(s, pi=pi, oc=oc, h2=h2, mm_tok=mm_tok, vprev=env.ve.n):
                        s.wait_ge(env.te.sem, mm_tok)
                        if vprev:
                            s.wait_ge(env.ve.sem, vprev)
                        if pi < 2:
                            s.activation(sig_s[:, h2, :], pB[:, :], AF.Sigmoid,
                                         bias=baet_s[:, oc:oc + 1])
                        else:
                            s.activation(sp_s[:, h2, :], pB[:, :], AF.Abs,
                                         bias=baet_s[:, oc:oc + 1])
                            s.activation(sig_s[:, h2, :], pB[:, :], AF.Identity,
                                         bias=baet_s[:, oc:oc + 1])
                        env.se.mark(s.drain())
                    pn_free[0] = ("se", env.se.n)
                act_tok = env.se.n

                if pi == 2:
                    @block.scalar
                    def _(s, act_tok=act_tok):
                        s.activation(sp_s[:, 0, :], sp_s[:, 0, :], AF.Exp, scale=-1.0)
                        s.activation(sp_s[:, 1, :], sp_s[:, 1, :], AF.Exp, scale=-1.0)
                        s.drain()
                        s.activation(sp_s[:, 0, :], sp_s[:, 0, :], AF.Ln, bias=1.0)
                        s.activation(sp_s[:, 1, :], sp_s[:, 1, :], AF.Ln, bias=1.0)
                        env.se.mark(s.drain())
                    sp_tok = env.se.n

                    @block.vector
                    def _(v, sp_tok=sp_tok):
                        v.wait_ge(env.se.sem, sp_tok)
                        for h2 in range(2):
                            v.tensor_scalar(sig_s[:, h2, :], sig_s[:, h2, :], 0.0, None,
                                            ALU.max)
                        v.drain()
                        for h2 in range(2):
                            v.tensor_add(sig_s[:, h2, :], sig_s[:, h2, :], sp_s[:, h2, :])
                        env.ve.mark(v.drain())
                    act_tok = ("ve", env.ve.n)
                else:
                    act_tok = ("se", act_tok)

                @block.vector
                def _(v, act_tok=act_tok):
                    v.wait_ge(env.se.sem if act_tok[0] == "se" else env.ve.sem, act_tok[1])
                    v.tensor_copy(n2[:, 0, :], sig_s[:, 0, :])
                    inst = v.tensor_copy(n2[:, 1, :], sig_s[:, 1, :])
                    env.ve.mark(v.drain())
                cp_tok = env.ve.n

                @block.tensor
                def _(te, cp_tok=cp_tok, pi=pi):
                    te.wait_ge(env.ve.sem, cp_tok)
                    for blk in range(4):
                        for h2 in range(2):
                            inst = te.matmul(pC[:, pi * 4 + blk: pi * 4 + blk + 1],
                                             n2[:, h2, blk * 128:(blk + 1) * 128],
                                             onesc_b[:, 0:1],
                                             start=(h2 == 0), stop=(h2 == 1))
                        env.te.mark(inst)
            red_tok = env.te.n

            @block.vector
            def _(v, sp=sp, red_tok=red_tok):
                v.wait_ge(env.te.sem, red_tok)
                for blk in range(4):
                    bi = sp * 4 + blk
                    v.tensor_scalar(scal[:, bi, 0:1], pC[:, 8 + blk:9 + blk],
                                    0.1 / DD, None, ALU.mult)
                    v.tensor_scalar(scal[:, bi, 1:2], pC[:, 0 + blk:1 + blk],
                                    0.1 / DD, None, ALU.mult)
                    v.tensor_scalar(scal[:, bi, 2:3], pC[:, 4 + blk:5 + blk],
                                    0.9 / DD, None, ALU.mult)
                v.drain()
                for blk in range(4):
                    bi = sp * 4 + blk
                    inst = v.tensor_scalar(scal[:, bi, 2:3], scal[:, bi, 2:3],
                                           0.05, None, ALU.add)
                env.ve.mark(v.drain())
            mean_tok = env.ve.n

            @block.scalar
            def _(s, sp=sp, mean_tok=mean_tok):
                s.wait_ge(env.ve.sem, mean_tok)
                for blk in range(4):
                    bi = sp * 4 + blk
                    s.activation(scal[:, bi, 1:2], scal[:, bi, 1:2], AF.Ln,
                                 scale=-1.0, bias=1.0)
                    s.activation(scal[:, bi, 2:3], scal[:, bi, 2:3], AF.Ln)
                env.se.mark(s.drain())
            pn_free[0] = ("se", env.se.n)
        t1_se = env.se.n
        t1_gd = env.gd.n

        if S3_PHASES < 2:
            @block.sync
            def _(q):
                q.wait_ge(env.qd.sem, env.qd.n)
                q.wait_ge(env.gd.sem, env.gd.n)
            return nc
        conv_tok = [0]
        for oc in range(6):
            @block.sync
            def _(q, oc=oc, t1_gd=t1_gd, vp=conv_tok[0], gp=env.gd.n):
                if oc == 0:
                    q.wait_ge(env.gd.sem, t1_gd)
                if vp:
                    q.wait_ge(env.ve.sem, vp)
                q.wait_ge(env.gd.sem, gp)
                env.qd.mark(q.dma_start(out=crow[:, :], in_=praw_d.ap()[oc, :, :]))
            ld_tok = env.qd.n

            @block.vector
            def _(v, oc=oc, ld_tok=ld_tok, gprev=env.gd.n):
                v.wait_ge(env.qd.sem, ld_tok)
                if gprev > t1_gd:
                    v.wait_ge(env.gd.sem, gprev)
                v.tensor_scalar(cacc[0][:, :], crow[:, :], dwc_s[:, oc, 3:4], None, ALU.mult)
                v.drain()
                v.scalar_tensor_tensor(cacc[1][:, 1:T], crow[:, 0:T - 1],
                                       dwc_s[:, oc, 2:3], cacc[0][:, 1:T],
                                       ALU.mult, ALU.add)
                v.tensor_copy(cacc[1][:, 0:1], cacc[0][:, 0:1])
                v.drain()
                v.scalar_tensor_tensor(cacc[0][:, 2:T], crow[:, 0:T - 2],
                                       dwc_s[:, oc, 1:2], cacc[1][:, 2:T],
                                       ALU.mult, ALU.add)
                v.tensor_copy(cacc[0][:, 0:2], cacc[1][:, 0:2])
                v.drain()
                v.scalar_tensor_tensor(cacc[1][:, 3:T], crow[:, 0:T - 3],
                                       dwc_s[:, oc, 0:1], cacc[0][:, 3:T],
                                       ALU.mult, ALU.add)
                v.tensor_copy(cacc[1][:, 0:3], cacc[0][:, 0:3])
                v.drain()
                inst = v.tensor_scalar(crow[:, :], cacc[1][:, :], dwb_s[:, oc:oc + 1],
                                       None, ALU.add)
                env.ve.mark(v.drain())
            conv_tok[0] = env.ve.n

            @block.gpsimd
            def _(g, oc=oc, ct=conv_tok[0]):
                g.wait_ge(env.ve.sem, ct)
                env.gd.mark(g.dma_start(out=kdw_d.ap()[oc, :, :], in_=crow[:, :]))
        t2_gd = env.gd.n

        if S3_PHASES < 3:
            @block.sync
            def _(q):
                q.wait_ge(env.qd.sem, env.qd.n)
                q.wait_ge(env.gd.sem, env.gd.n)
            return nc
        psum_free = {}

        def mmseq(te, out, pairs, extra_wait=None):
            for i, (lh, rh) in enumerate(pairs):
                inst = te.matmul(out, lh, rh, start=(i == 0), stop=(i == len(pairs) - 1))
            return inst

        for cc in range(8):
            @block.sync
            def _(q, cc=cc, t2_gd=t2_gd, te_prev=env.te.n):
                if cc == 0:
                    q.wait_ge(env.gd.sem, t2_gd)
                else:
                    q.wait_ge(env.te.sem, te_prev)
                for dc in range(2):
                    env.qd.mark(q.dma_start(out=kc_b[:, dc, :],
                                            in_=kdw_d.ap()[0 + dc, :, cc * CHUNK:(cc + 1) * CHUNK]))
                    env.qd.mark(q.dma_start(out=vc_b[:, dc, :],
                                            in_=kdw_d.ap()[2 + dc, :, cc * CHUNK:(cc + 1) * CHUNK]))
                    env.qd.mark(q.dma_start(out=qc_b[:, dc, :],
                                            in_=kdw_d.ap()[4 + dc, :, cc * CHUNK:(cc + 1) * CHUNK]))
            kv_tok = env.qd.n

            for od in range(2):
                @block.tensor
                def _(te, od=od, cc=cc, kv_tok=kv_tok, vp=env.ve.n, sp_=env.se.n):
                    if od == 0:
                        te.wait_ge(env.qd.sem, kv_tok)
                        if vp:
                            te.wait_ge(env.ve.sem, vp)
                    if sp_:
                        te.wait_ge(env.se.sem, sp_)
                    inst = mmseq(te, pA[:, :],
                                 [(w1b[:, dc, od * 128:(od + 1) * 128], kc_b[:, dc, :])
                                  for dc in range(2)])
                    env.te.mark(inst)

                @block.scalar
                def _(s2, od=od, tok=env.te.n):
                    s2.wait_ge(env.te.sem, tok)
                    s2.activation(afm[:, od, :], pA[:, :], AF.Silu)
                    s2.activation(dsfm[:, od, :], pA[:, :], AF.Derivative_silu)
                    env.se.mark(s2.drain())
            a_tok = env.se.n

            for od in range(2):
                @block.tensor
                def _(te, od=od, a_tok=a_tok, vp=env.ve.n):
                    if od == 0:
                        te.wait_ge(env.se.sem, a_tok)
                    if vp:
                        te.wait_ge(env.ve.sem, vp)
                    inst = mmseq(te, pB[:, :],
                                 [(w2b[:, oc2, od * 128:(od + 1) * 128], afm[:, oc2, :])
                                  for oc2 in range(2)])
                    env.te.mark(inst)

                @block.vector
                def _(v2, od=od, tok=env.te.n):
                    v2.wait_ge(env.te.sem, tok)
                    inst2 = v2.tensor_sub(efm[:, od, :], pB[:, :], vc_b[:, od, :])
                    env.ve.mark(v2.drain())
            e_tok = env.ve.n

            @block.tensor
            def _(te, e_tok=e_tok):
                te.wait_ge(env.ve.sem, e_tok)
                for pc in range(2):
                    for oc2 in range(2):
                        inst = te.transpose(pT[:, (pc * 2 + oc2) * 128:(pc * 2 + oc2 + 1) * 128],
                                            w2b[:, oc2, pc * 128:(pc + 1) * 128], eye_b[:, :])
                env.te.mark(inst)
            w2p_tok = env.te.n

            @block.vector
            def _(v, w2p_tok=w2p_tok):
                v.wait_ge(env.te.sem, w2p_tok)
                src = pT[:, :].rearrange("p (a b) -> p a b", a=8)
                v.tensor_copy(w2pb[:, 0, 0:128], src[:, 0, :])
                v.tensor_copy(w2pb[:, 0, 128:256], src[:, 1, :])
                v.tensor_copy(w2pb[:, 1, 0:128], src[:, 2, :])
                inst = v.tensor_copy(w2pb[:, 1, 128:256], src[:, 3, :])
                env.ve.mark(v.drain())
            w2pb_tok = env.ve.n

            for od in range(2):
                @block.tensor
                def _(te, od=od, w2pb_tok=w2pb_tok, vp=env.ve.n):
                    if od == 0:
                        te.wait_ge(env.ve.sem, w2pb_tok)
                    else:
                        te.wait_ge(env.ve.sem, vp)
                    inst = mmseq(te, pC[:, :],
                                 [(w2pb[:, pc, od * 128:(od + 1) * 128], efm[:, pc, :])
                                  for pc in range(2)])
                    env.te.mark(inst)

                @block.vector
                def _(v2, od=od, tok=env.te.n):
                    v2.wait_ge(env.te.sem, tok)
                    inst2 = v2.tensor_mul(ufm[:, od, :], pC[:, :], dsfm[:, od, :])
                    env.ve.mark(v2.drain())
            u_tok = env.ve.n

            @block.scalar
            def _(s, u_tok=u_tok):
                s.wait_ge(env.ve.sem, u_tok)
                for od in range(2):
                    s.activation(n2[:, od, :], ufm[:, od, :], AF.Square)
                env.se.mark(s.drain())
            nu_tok = env.se.n

            def norm_reduce(src_tag, dst_col, wait_tok):
                @block.tensor
                def _(te, dst_col=dst_col, wait_tok=wait_tok):
                    te.wait_ge(env.se.sem, wait_tok)
                    for blk in range(4):
                        for od in range(2):
                            inst = te.matmul(pD[:, dst_col * 4 + blk: dst_col * 4 + blk + 1],
                                             n2[:, od, blk * 128:(blk + 1) * 128],
                                             onesc_b[:, 0:1], start=(od == 0), stop=(od == 1))
                    env.te.mark(inst)
                return env.te.n

            nmm = norm_reduce("u", 0, nu_tok)

            @block.scalar
            def _(s, nmm=nmm):
                s.wait_ge(env.te.sem, nmm)
                for od in range(2):
                    s.activation(n2[:, od, :], kc_b[:, od, :], AF.Square)
                env.se.mark(s.drain())
            nmm = norm_reduce("k", 1, env.se.n)

            @block.scalar
            def _(s, nmm=nmm):
                s.wait_ge(env.te.sem, nmm)
                for od in range(2):
                    s.activation(n2[:, od, :], efm[:, od, :], AF.Square)
                env.se.mark(s.drain())
            nmm = norm_reduce("e", 2, env.se.n)

            @block.scalar
            def _(s, nmm=nmm):
                s.wait_ge(env.te.sem, nmm)
                for od in range(2):
                    s.activation(n2[:, od, :], afm[:, od, :], AF.Square)
                env.se.mark(s.drain())
            nmm = norm_reduce("a", 3, env.se.n)

            @block.vector
            def _(v, cc=cc, nmm=nmm):
                v.wait_ge(env.te.sem, nmm)
                v.tensor_copy(gncol[:, 0:16], pD[:, 0:16])
                v.drain()
                v.tensor_mul(gncol[:, 16:20], gncol[:, 0:4], gncol[:, 4:8])
                v.tensor_mul(gncol[:, 20:24], gncol[:, 8:12], gncol[:, 12:16])
                v.drain()
                inst = v.tensor_add(gncol[:, 3:7], gncol[:, 16:20], gncol[:, 20:24])
                env.ve.mark(v.drain())
            gn_tok = env.ve.n

            @block.scalar
            def _(s, gn_tok=gn_tok):
                s.wait_ge(env.ve.sem, gn_tok)
                inst = s.activation(gncol[:, 3:7], gncol[:, 3:7], AF.Sqrt)
                env.se.mark(inst)
            sq2_tok = env.se.n

            @block.vector
            def _(v, cc=cc, sq2_tok=sq2_tok):
                v.wait_ge(env.se.sem, sq2_tok)
                v.tensor_scalar(gncol[:, 3:7], gncol[:, 3:7], 2.0 * GS3, None, ALU.mult)
                v.drain()
                v.reciprocal(gncol[:, 3:7], gncol[:, 3:7])
                v.drain()
                v.tensor_scalar(gncol[:, 3:7], gncol[:, 3:7], 1.0, None, ALU.min)
                v.drain()
                v.tensor_scalar(gncol[:, 3:7], gncol[:, 3:7], 2.0 * GS3, None, ALU.mult)
                v.drain()
                for blk in range(4):
                    inst = v.tensor_mul(wthc[:, blk:blk + 1], gncol[:, 3 + blk:4 + blk],
                                        scal[:, cc * 4 + blk, 0:1])
                env.ve.mark(v.drain())
            wthc_tok = env.ve.n

            for m in range(4):
                bi = cc * 4 + m

                @block.tensor
                def _(te, bi=bi, wthc_tok=wthc_tok, vp=env.ve.n):
                    te.wait_ge(env.ve.sem, max(wthc_tok, vp))
                    inst = te.matmul(pE[:, 0:2], triu_f[:, :], scal[:, bi, 1:3],
                                     start=True, stop=True)
                    env.te.mark(inst)
                cum_tok = env.te.n

                @block.vector
                def _(v, cum_tok=cum_tok):
                    v.wait_ge(env.te.sem, cum_tok)
                    inst = v.tensor_copy(lacol[:, 0:2], pE[:, 0:2])
                    env.ve.mark(v.drain())
                cumc_tok = env.ve.n

                @block.tensor
                def _(te, cumc_tok=cumc_tok):
                    te.wait_ge(env.ve.sem, cumc_tok)
                    inst = te.transpose(pE[0:2, 0:128] if False else pE[0:1, 0:128],
                                        lacol[:, 0:1], eye_f[:, :])
                    inst = te.transpose(pE[0:1, 128:256], lacol[:, 1:2], eye_f[:, :])
                    env.te.mark(inst)
                rowt_tok = env.te.n

                @block.vector
                def _(v, rowt_tok=rowt_tok):
                    v.wait_ge(env.te.sem, rowt_tok)
                    v.tensor_copy(clarow[:, :], pE[0:1, 0:128])
                    inst = v.tensor_copy(clerow[:, :], pE[0:1, 128:256])
                    v.drain()
                    v.tensor_sub(lacol[:, 2:3], lacol[:, 1:2], lacol[:, 0:1])
                    v.tensor_scalar(lacol[:, 3:4], lacol[:, 0:1], -1.0, None, ALU.mult)
                    v.tensor_scalar(lacol[:, 4:5], lacol[:, 1:2], -1.0, None, ALU.mult)
                    v.tensor_scalar(negcle[:, :], clerow[:, :], -1.0, None, ALU.mult)
                    env.ve.mark(v.drain())
                rows_tok = env.ve.n

                @block.tensor
                def _(te, rows_tok=rows_tok):
                    te.wait_ge(env.ve.sem, rows_tok)
                    te.matmul(pE[:, 0:128], onesr_f[0:1, :], clarow[0:1, :],
                              start=True, stop=True)
                    inst = te.matmul(pE[:, 128:256], onesr_f[0:1, :], negcle[0:1, :],
                                     start=True, stop=True)
                    env.te.mark(inst)
                bc_tok = env.te.n

                @block.scalar
                def _(s, bc_tok=bc_tok):
                    s.wait_ge(env.te.sem, bc_tok)
                    s.activation(m1t_f[:, :], pE[:, 0:128], AF.Exp,
                                 bias=lacol[:, 3:4], scale=1.0)
                    s.activation(m2t_f[:, :], pE[:, 128:256], AF.Exp,
                                 bias=lacol[:, 1:2], scale=1.0)
                    s.activation(expin[:, :], pE[:, 0:128], AF.Exp,
                                 bias=lacol[:, 2:3], scale=1.0)
                    s.activation(acol[:, 0:1], lacol[:, 0:1], AF.Exp)
                    s.activation(acol[:, 2:3], lacol[:, 1:2], AF.Exp)
                    env.se.mark(s.drain())
                exp_tok = env.se.n

                @block.vector
                def _(v, exp_tok=exp_tok):
                    v.wait_ge(env.se.sem, exp_tok)
                    v.tensor_mul(m1t_f[:, :], m1t_f[:, :], triu_f[:, :])
                    v.tensor_mul(expin[:, :], expin[:, :], triu_f[:, :])
                    env.ve.mark(v.drain())
                msk1_tok = env.ve.n

                @block.vector
                def _(v, msk1_tok=msk1_tok):
                    inst = v.tensor_mul(m2t_f[:, :], m2t_f[:, :], tril_f[:, :])
                    env.ve.mark(v.drain())
                msk_tok = env.ve.n

                @block.tensor
                def _(te, msk_tok=msk_tok):
                    te.wait_ge(env.ve.sem, msk_tok)
                    te.matmul(pF2[:, 0:128], m2t_f[:, :], m1t_f[:, :], start=True, stop=True)
                    inst = te.matmul(pF2[:, 128:129], expin[:, :], onesc_f[:, 0:1],
                                     start=True, stop=True)
                    env.te.mark(inst)
                dt_tok = env.te.n

                @block.vector
                def _(v, dt_tok=dt_tok):
                    v.wait_ge(env.te.sem, dt_tok)
                    v.tensor_copy(dt_sb[:, :], pF2[:, 0:128])
                    inst = v.tensor_copy(acol[:, 1:2], pF2[:, 128:129])
                    env.ve.mark(v.drain())
                dt_sb_tok = env.ve.n

                @block.vector
                def _(v, dt_sb_tok=dt_sb_tok):
                    v.wait_ge(env.ve.sem, dt_sb_tok)
                    inst = v.tensor_copy(acol[:, 3:4], lacol[:, 1:2])
                    env.ve.mark(v.drain())
                a4_tok = env.ve.n

                @block.tensor
                def _(te, a4_tok=a4_tok):
                    te.wait_ge(env.ve.sem, a4_tok)
                    inst = te.matmul(pE[0:1, 300:304], sel_f[:, 0:1], acol[:, 0:4],
                                     start=True, stop=True)
                    env.te.mark(inst)
                l1_tok = env.te.n

                @block.vector
                def _(v, l1_tok=l1_tok):
                    v.wait_ge(env.te.sem, l1_tok)
                    inst = v.tensor_copy(arow[0:1, 0:4], pE[0:1, 300:304])
                    env.ve.mark(v.drain())
                l2_tok = env.ve.n

                @block.tensor
                def _(te, l2_tok=l2_tok):
                    te.wait_ge(env.ve.sem, l2_tok)
                    inst = te.matmul(pE[:, 300:304], onesr_f[0:1, :], arow[0:1, 0:4],
                                     start=True, stop=True)
                    env.te.mark(inst)
                l3_tok = env.te.n

                @block.vector
                def _(v, bi=bi, l3_tok=l3_tok):
                    v.wait_ge(env.te.sem, l3_tok)
                    v.tensor_copy(lasts[:, 0:4], pE[:, 300:304])
                    v.drain()
                    v.tensor_mul(dl_c[:, 0:1], dt_sb[:, 127:128], wthc[:, bi % 4:bi % 4 + 1])
                    v.drain()
                    inst = v.tensor_scalar(dl_c[:, 0:1], dl_c[:, 0:1], -1.0, None, ALU.mult)
                    env.ve.mark(v.drain())
                la_tok = env.ve.n

                @block.scalar
                def _(s, la_tok=la_tok):
                    s.wait_ge(env.ve.sem, la_tok)
                    inst = s.activation(dl_c[:, 1:2], lacol[:, 4:5], AF.Exp,
                                        bias=lasts[:, 3:4], scale=1.0)
                    env.se.mark(inst)
                ds_tok = env.se.n

                @block.vector
                def _(v, bi=bi, ds_tok=ds_tok):
                    v.wait_ge(env.se.sem, ds_tok)
                    v.tensor_mul(dl_c[:, 1:2], dl_c[:, 1:2], wthc[:, bi % 4:bi % 4 + 1])
                    v.drain()
                    v.tensor_scalar(dl_c[:, 1:2], dl_c[:, 1:2], -1.0, None, ALU.mult)
                    v.tensor_scalar(eyeA[:, :], eye_f[:, :], lasts[:, 0:1], None, ALU.mult)
                    v.tensor_scalar(eyeB[:, :], eye_f[:, :], lasts[:, 1:2], None, ALU.mult)
                    v.tensor_scalar(eyeE[:, :], eye_f[:, :], lasts[:, 2:3], None, ALU.mult)
                    env.ve.mark(v.drain())
                eyes_tok = env.ve.n

                @block.tensor
                def _(te, m=m, eyes_tok=eyes_tok):
                    te.wait_ge(env.ve.sem, eyes_tok)
                    for mi, (fmb, tmb) in enumerate(((kc_b, ktm), (ufm, utm),
                                                     (efm, etm), (afm, atm))):
                        for dc in range(2):
                            inst = te.transpose(
                                pT[:, (mi * 2 + dc) * 128:(mi * 2 + dc + 1) * 128],
                                fmb[:, dc, m * 128:(m + 1) * 128], eye_b[:, :])
                    env.te.mark(inst)
                trs_tok = env.te.n

                @block.vector
                def _(v, trs_tok=trs_tok):
                    v.wait_ge(env.te.sem, trs_tok)
                    src = pT[:, :].rearrange("p (a b) -> p a b", a=8)
                    for mi, tmb in enumerate((ktm, utm, etm, atm)):
                        for dc in range(2):
                            inst = v.tensor_copy(tmb[:, dc * 128:(dc + 1) * 128],
                                                 src[:, mi * 2 + dc, :])
                    env.ve.mark(v.drain())
                tm_tok = env.ve.n

                @block.tensor
                def _(te, m=m, tm_tok=tm_tok):
                    te.wait_ge(env.ve.sem, tm_tok)
                    mmseq(te, pA[:, 0:256], [(qc_b[:, dc, m * 128:(m + 1) * 128],
                                              w1b[:, dc, :]) for dc in range(2)])
                    mmseq(te, pB[:, 0:256], [(qc_b[:, dc, m * 128:(m + 1) * 128],
                                              s1b[:, dc, :]) for dc in range(2)])
                    inst = mmseq(te, pC[:, 0:128],
                                 [(kc_b[:, dc, m * 128:(m + 1) * 128],
                                   qc_b[:, dc, m * 128:(m + 1) * 128]) for dc in range(2)])
                    env.te.mark(inst)
                app1_tok = env.te.n

                @block.vector
                def _(v, m=m, bi=bi, app1_tok=app1_tok):
                    v.wait_ge(env.te.sem, app1_tok)
                    v.tensor_mul(hsb2[:, 0:128], pC[:, 0:128], dt_sb[:, :])
                    v.drain()
                    v.tensor_scalar(p1_b[:, :], hsb2[:, 0:128], wthc[:, bi % 4:bi % 4 + 1],
                                    None, ALU.mult)
                    env.ve.mark(v.drain())
                p1_tok = env.ve.n

                @block.tensor
                def _(te, p1_tok=p1_tok):
                    te.wait_ge(env.ve.sem, p1_tok)
                    inst = te.matmul(pD[:, 0:256], p1_b[:, :], utm[:, :],
                                     start=True, stop=True)
                    env.te.mark(inst)
                ct1_tok = env.te.n

                @block.vector
                def _(v, ct1_tok=ct1_tok):
                    v.wait_ge(env.te.sem, ct1_tok)
                    v.tensor_scalar(hsb[:, :], pA[:, 0:256], acol[:, 0:1], None, ALU.mult)
                    v.drain()
                    v.scalar_tensor_tensor(hsb2[:, :], pB[:, 0:256], acol[:, 1:2],
                                           hsb[:, :], ALU.mult, ALU.add)
                    v.drain()
                    inst = v.tensor_sub(hsb[:, :], hsb2[:, :], pD[:, 0:256])
                    env.ve.mark(v.drain())
                hf_tok = env.ve.n

                @block.scalar
                def _(s, hf_tok=hf_tok):
                    s.wait_ge(env.ve.sem, hf_tok)
                    inst = s.activation(hstm[:, :], hsb[:, :], AF.Silu)
                    env.se.mark(inst)
                hs_tok = env.se.n

                @block.tensor
                def _(te, hs_tok=hs_tok):
                    te.wait_ge(env.se.sem, hs_tok)
                    for oc2 in range(2):
                        inst = te.transpose(pT[:, oc2 * 128:(oc2 + 1) * 128],
                                            hstm[:, oc2 * 128:(oc2 + 1) * 128], eye_b[:, :])
                    env.te.mark(inst)
                hst_tok = env.te.n

                @block.vector
                def _(v, hst_tok=hst_tok):
                    v.wait_ge(env.te.sem, hst_tok)
                    src = pT[:, :].rearrange("p (a b) -> p a b", a=8)
                    v.tensor_copy(hsfm[:, 0, :], src[:, 0, :])
                    inst = v.tensor_copy(hsfm[:, 1, :], src[:, 1, :])
                    env.ve.mark(v.drain())
                hsf_tok = env.ve.n

                @block.tensor
                def _(te, m=m, hsf_tok=hsf_tok):
                    te.wait_ge(env.ve.sem, hsf_tok)
                    mmseq(te, pA[:, 0:256], [(hsfm[:, oc2, :], w2b[:, oc2, :])
                                             for oc2 in range(2)])
                    mmseq(te, pB[:, 0:256], [(hsfm[:, oc2, :], s2b[:, oc2, :])
                                             for oc2 in range(2)])
                    inst = mmseq(te, pC[:, 0:128],
                                 [(afm[:, oc2, m * 128:(m + 1) * 128], hsfm[:, oc2, :])
                                  for oc2 in range(2)])
                    env.te.mark(inst)
                app2_tok = env.te.n

                @block.vector
                def _(v, bi=bi, app2_tok=app2_tok):
                    v.wait_ge(env.te.sem, app2_tok)
                    v.tensor_mul(hsb2[:, 0:128], pC[:, 0:128], dt_sb[:, :])
                    v.drain()
                    v.tensor_scalar(p2_b[:, :], hsb2[:, 0:128], wthc[:, bi % 4:bi % 4 + 1],
                                    None, ALU.mult)
                    env.ve.mark(v.drain())
                p2_tok = env.ve.n

                @block.tensor
                def _(te, p2_tok=p2_tok):
                    te.wait_ge(env.ve.sem, p2_tok)
                    inst = te.matmul(pD[:, 0:256], p2_b[:, :], etm[:, :],
                                     start=True, stop=True)
                    env.te.mark(inst)
                ct2_tok = env.te.n

                @block.vector
                def _(v, ct2_tok=ct2_tok):
                    v.wait_ge(env.te.sem, ct2_tok)
                    v.tensor_scalar(ott[:, :], pA[:, 0:256], acol[:, 0:1], None, ALU.mult)
                    v.drain()
                    v.scalar_tensor_tensor(hsb2[:, :], pB[:, 0:256], acol[:, 1:2],
                                           ott[:, :], ALU.mult, ALU.add)
                    v.drain()
                    inst = v.tensor_sub(ott[:, :], hsb2[:, :], pD[:, 0:256])
                    env.ve.mark(v.drain())
                ot_tok = env.ve.n

                @block.gpsimd
                def _(g, bi=bi, ot_tok=ot_tok):
                    g.wait_ge(env.ve.sem, ot_tok)
                    env.gd.mark(g.dma_start(out=retr_d[bi * 128:(bi + 1) * 128, :],
                                            in_=ott[:, :]))
                ret_st = env.gd.n

                @block.vector
                def _(v, tm_tok=tm_tok, ot_tok=ot_tok):
                    v.wait_ge(env.ve.sem, ot_tok)
                    v.tensor_scalar(kw_b[:, :], ktm[:, :], dl_c[:, 0:1], None, ALU.mult)
                    v.tensor_scalar(aw_b[:, :], atm[:, :], dl_c[:, 0:1], None, ALU.mult)
                    v.tensor_scalar(kw2_b[:, :], ktm[:, :], dl_c[:, 1:2], None, ALU.mult)
                    inst = v.tensor_scalar(aw2_b[:, :], atm[:, :], dl_c[:, 1:2],
                                           None, ALU.mult)
                    env.ve.mark(v.drain())
                kw_tok = env.ve.n

                @block.tensor
                def _(te, kw_tok=kw_tok):
                    te.wait_ge(env.ve.sem, kw_tok)
                    for dc in range(2):
                        inst = mmseq(te, (pA if dc == 0 else pB)[:, 0:256],
                                     [(eyeA[:, :], w1b[:, dc, :]),
                                      (eyeB[:, :], s1b[:, dc, :]),
                                      (kw_b[:, dc * 128:(dc + 1) * 128], utm[:, :])])
                    for oc2 in range(2):
                        inst = mmseq(te, (pC if oc2 == 0 else pD)[:, 0:256],
                                     [(eyeA[:, :], w2b[:, oc2, :]),
                                      (eyeB[:, :], s2b[:, oc2, :]),
                                      (aw_b[:, oc2 * 128:(oc2 + 1) * 128], etm[:, :])])
                    env.te.mark(inst)
                wup_tok = env.te.n

                @block.vector
                def _(v, wup_tok=wup_tok):
                    v.wait_ge(env.te.sem, wup_tok)
                    for dc, pp in ((0, pA), (1, pB)):
                        v.tensor_copy(w1t[:, dc, :], pp[:, 0:256])
                    for oc2, pp in ((0, pC), (1, pD)):
                        inst = v.tensor_copy(w2t[:, oc2, :], pp[:, 0:256])
                    env.ve.mark(v.drain())
                wev_tok = env.ve.n

                @block.tensor
                def _(te, wev_tok=wev_tok):
                    te.wait_ge(env.ve.sem, wev_tok)
                    for dc in range(2):
                        inst = mmseq(te, (pA if dc == 0 else pB)[:, 0:256],
                                     [(eyeE[:, :], s1b[:, dc, :]),
                                      (kw2_b[:, dc * 128:(dc + 1) * 128], utm[:, :])])
                    for oc2 in range(2):
                        inst = mmseq(te, (pC if oc2 == 0 else pD)[:, 0:256],
                                     [(eyeE[:, :], s2b[:, oc2, :]),
                                      (aw2_b[:, oc2 * 128:(oc2 + 1) * 128], etm[:, :])])
                    env.te.mark(inst)
                sup_tok = env.te.n

                @block.vector
                def _(v, sup_tok=sup_tok):
                    v.wait_ge(env.te.sem, sup_tok)
                    for dc, pp in ((0, pA), (1, pB)):
                        v.tensor_copy(s1t[:, dc, :], pp[:, 0:256])
                    for oc2, pp in ((0, pC), (1, pD)):
                        v.tensor_copy(s2t[:, oc2, :], pp[:, 0:256])
                    v.drain()
                    v.tensor_copy(w1b[:, :, :], w1t[:, :, :])
                    v.tensor_copy(w2b[:, :, :], w2t[:, :, :])
                    v.tensor_copy(s1b[:, :, :], s1t[:, :, :])
                    inst = v.tensor_copy(s2b[:, :, :], s2t[:, :, :])
                    env.ve.mark(v.drain())
                st_tok2 = env.ve.n

                if S3_DEBUG and cc == 0 and m == 0:
                    @block.gpsimd
                    def _(g, st_tok2=st_tok2):
                        g.wait_ge(env.ve.sem, st_tok2)
                        env.gd.mark(g.dma_start(out=dbgst["w1"].ap(), in_=w1t[:, :, :]))
                        env.gd.mark(g.dma_start(out=dbgst["w2"].ap(), in_=w2t[:, :, :]))
                        env.gd.mark(g.dma_start(out=dbgst["s1"].ap(), in_=s1t[:, :, :]))
                        env.gd.mark(g.dma_start(out=dbgst["s2"].ap(), in_=s2t[:, :, :]))
                        env.gd.mark(g.dma_start(out=dbgco.ap()[:, 0:128], in_=dt_sb[:, :]))
                        env.gd.mark(g.dma_start(out=dbgco.ap()[:, 128:132], in_=acol[:, :]))
                        env.gd.mark(g.dma_start(out=dbgco.ap()[:, 132:136], in_=lasts[:, :]))
                        env.gd.mark(g.dma_start(out=dbgco.ap()[:, 136:140], in_=dl_c[:, :]))
                        env.gd.mark(g.dma_start(out=dbgco.ap()[:, 140:144], in_=wthc[:, :]))
                        env.gd.mark(g.dma_start(out=dbgco.ap()[:, 144:150], in_=lacol[:, :]))
                    @block.vector
                    def _(v):
                        v.wait_ge(env.gd.sem, env.gd.n)
                        env.ve.mark(v.engine_nop())
        chain_ve = env.ve.n
        chain_gd = env.gd.n

        if S3_PHASES < 4:
            @block.sync
            def _(q):
                q.wait_ge(env.qd.sem, env.qd.n)
                q.wait_ge(env.gd.sem, env.gd.n)
            return nc
        @block.sync
        def _(q, chain_ve=chain_ve, chain_gd=chain_gd):
            q.wait_ge(env.ve.sem, chain_ve)
            q.wait_ge(env.gd.sem, chain_gd)

        rtm = ret_s
        for spl in range(8):
            for tb in range(4):
                bil = spl * 4 + tb

                @block.sync
                def _(q, spl=spl, tb=tb, ve_prev=env.ve.n, se_prev=env.se.n):
                    if ve_prev:
                        q.wait_ge(env.ve.sem, ve_prev)
                    if se_prev:
                        q.wait_ge(env.se.sem, se_prev)
                    bi2 = spl * 4 + tb
                    goff = spl * CHUNK + tb * 128
                    env.qd.mark(q.dma_start(out=xres[:, 0, :],
                                            in_=x_d[goff: goff + 128, :]))
                    env.qd.mark(q.dma_start(
                        out=ott[:, :], in_=retr_d[bi2 * 128:(bi2 + 1) * 128, :]))
                ld_tok = env.qd.n

                @block.vector
                def _(v, ld_tok=ld_tok, te_prev=env.te.n):
                    v.wait_ge(env.qd.sem, ld_tok)
                    if te_prev:
                        v.wait_ge(env.te.sem, te_prev)
                    v.tensor_copy(xbf[:, 0, :], xres[:, 0, :])
                    inst = v.tensor_copy(junk_b[:, 0:DD], ott[:, :])
                    env.ve.mark(v.drain())
                cast_tok = env.ve.n

                @block.tensor
                def _(te, cast_tok=cast_tok):
                    te.wait_ge(env.ve.sem, cast_tok)
                    for kf in range(8):
                        inst = te.transpose(pT[:, kf * 128:(kf + 1) * 128],
                                            xbf[:, 0, kf * 128:(kf + 1) * 128], eye_b[:, :])
                    env.te.mark(inst)
                xtr_tok = env.te.n

                @block.vector
                def _(v, xtr_tok=xtr_tok):
                    v.wait_ge(env.te.sem, xtr_tok)
                    src = pT[:, :].rearrange("p (f t) -> p f t", f=8)
                    inst = v.tensor_copy(xfm[:, :, 0:128], src)
                    env.ve.mark(v.drain())
                xfm_tok = env.ve.n

                @block.tensor
                def _(te, xfm_tok=xfm_tok):
                    te.wait_ge(env.ve.sem, xfm_tok)
                    for dc in range(2):
                        inst = te.transpose(pT[:, dc * 128:(dc + 1) * 128],
                                            junk_b[:, dc * 128:(dc + 1) * 128], eye_b[:, :])
                    env.te.mark(inst)
                rtr_tok = env.te.n

                @block.vector
                def _(v, rtr_tok=rtr_tok):
                    v.wait_ge(env.te.sem, rtr_tok)
                    src = pT[:, :].rearrange("p (f t) -> p f t", f=8)
                    v.tensor_copy(retfm[:, 0, :], src[:, 0, :])
                    inst = v.tensor_copy(retfm[:, 1, :], src[:, 1, :])
                    env.ve.mark(v.drain())
                rfm_tok = env.ve.n

                @block.tensor
                def _(te, rfm_tok=rfm_tok):
                    te.wait_ge(env.ve.sem, rfm_tok)
                    for oh, pp in ((0, pA), (1, pB)):
                        inst = mmseq(te, pp[:, :],
                                     [(retfm[:, dc, :], wo_b[:, dc, oh * 512:(oh + 1) * 512])
                                      for dc in range(2)])
                    for oh, pp in ((0, pC), (1, pD)):
                        pairs = [(xfm[:, kf, 0:128], gw_b[:, kf, oh * 512:(oh + 1) * 512])
                                 for kf in range(8)]
                        pairs.append((onesr_b[0:1, :], gb_b[0:1, oh * 512:(oh + 1) * 512]))
                        inst = mmseq(te, pp[:, :], pairs)
                    env.te.mark(inst)
                wog_tok = env.te.n

                @block.scalar
                def _(s, wog_tok=wog_tok):
                    s.wait_ge(env.te.sem, wog_tok)
                    s.activation(ret_s[:, 0:512], pA[:, :], AF.Identity)
                    s.activation(ret_s[:, 512:1024], pB[:, :], AF.Identity)
                    s.drain()
                    s.activation(junk_b[:, :], ret_s[:, 0:1024], AF.Square,
                                 accum_out=st[:, 0:1])
                    env.se.mark(s.drain())
                ms_tok = env.se.n

                @block.vector
                def _(v, ms_tok=ms_tok):
                    v.wait_ge(env.se.sem, ms_tok)
                    inst = v.tensor_scalar(st[:, 2:3], st[:, 0:1], 1.0 / C, None, ALU.mult)
                    env.ve.mark(v.drain())
                ms2_tok = env.ve.n

                @block.scalar
                def _(s, ms2_tok=ms2_tok):
                    s.wait_ge(env.ve.sem, ms2_tok)
                    inst = s.activation(st[:, 3:4], st[:, 2:3], AF.Sqrt, bias=eps3c[:, 0:1])
                    env.se.mark(inst)
                sd3_tok = env.se.n

                @block.vector
                def _(v, sd3_tok=sd3_tok):
                    v.wait_ge(env.se.sem, sd3_tok)
                    v.reciprocal(st[:, 4:5], st[:, 3:4])
                    env.ve.mark(v.drain())
                rc_tok = env.ve.n

                @block.scalar
                def _(s, rc_tok=rc_tok):
                    s.wait_ge(env.ve.sem, rc_tok)
                    inst = s.activation(o3_s[:, :], ret_s[:, 0:1024], AF.Identity,
                                        scale=st[:, 4:5])
                    env.se.mark(inst)
                o1_tok = env.se.n

                @block.vector
                def _(v, o1_tok=o1_tok, g128_tok=g128_tok):
                    v.wait_ge(env.se.sem, o1_tok)
                    v.tensor_mul(o3_s[:, :], o3_s[:, :], g128[:, :])
                    v.drain()
                    v.tensor_mul(o3_s[:, 0:512], o3_s[:, 0:512], pC[:, :])
                    v.tensor_mul(o3_s[:, 512:1024], o3_s[:, 512:1024], pD[:, :])
                    v.drain()
                    v.scalar_tensor_tensor(s_sb[:, :], o3_s[:, :], 1.0, xres[:, 0, :],
                                           ALU.mult, ALU.add, accum_out=st[:, 0:1])
                    env.ve.mark(v.drain())
                sum3_tok = env.ve.n

                @block.scalar
                def _(s, sum3_tok=sum3_tok):
                    s.wait_ge(env.ve.sem, sum3_tok)
                    s.activation(junk_b[:, :], s_sb[:, :], AF.Square, accum_out=st[:, 1:2])
                    env.se.mark(s.drain())
                sq3_tok = env.se.n

                y_done = emit_ln_stats_apply(env, block, s_sb[:, :], st,
                                             ynew[:, bil % 2, :], epsl[:, 0:1],
                                             sq3_tok, "se")

                @block.sync
                def _(q, spl=spl, tb=tb, bil=bil, y_done=y_done):
                    q.wait_ge(env.se.sem, y_done)
                    env.qd.mark(q.dma_start(
                        out=y_d[spl * CHUNK + tb * 128: spl * CHUNK + (tb + 1) * 128, :],
                        in_=ynew[:, bil % 2, :]))
                yst3 = env.qd.n

                @block.scalar
                def _(s, yst3=yst3):
                    if yst3 >= 32:
                        s.wait_ge(env.qd.sem, yst3 - 16)

        @block.sync
        def _(q):
            q.wait_ge(env.qd.sem, env.qd.n)
            q.wait_ge(env.gd.sem, env.gd.n)
    return nc


_S3_NC = None


def run_s3(x_full, Wk, Wv, Wq, Wo, ckw, ckb, cvw, cvb, cqw, cqb, mW1, mW2,
           aW, ab, eW, eb, tW, tb_, norm_g, gW, gb):
    global _S3_NC
    if _S3_NC is None:
        _S3_NC = build_s3()
    eye = np.eye(128, dtype=np.float32)
    rr = np.arange(128)
    triu = (rr[:, None] <= rr[None, :]).astype(np.float32)
    tril = (rr[:, None] >= rr[None, :]).astype(np.float32)
    sel = np.zeros((128, 4), np.float32)
    sel[127, 0] = 1.0
    sel[:, 1] = 1.0
    wkvq = f32c(np.concatenate([np.asarray(Wk).T, np.asarray(Wv).T, np.asarray(Wq).T], 1))
    waet = f32c(np.concatenate([np.asarray(aW).T, np.asarray(eW).T, np.asarray(tW).T], 1))
    baet = f32c(np.stack([np.asarray(ab)[0:128], np.asarray(ab)[128:256],
                          np.asarray(eb)[0:128], np.asarray(eb)[128:256],
                          np.asarray(tb_)[0:128], np.asarray(tb_)[128:256]], 1))
    dwc = f32c(np.stack([np.asarray(ckw)[0:128], np.asarray(ckw)[128:256],
                         np.asarray(cvw)[0:128], np.asarray(cvw)[128:256],
                         np.asarray(cqw)[0:128], np.asarray(cqw)[128:256]], 1))
    dwb = f32c(np.stack([np.asarray(ckb)[0:128], np.asarray(ckb)[128:256],
                         np.asarray(cvb)[0:128], np.asarray(cvb)[128:256],
                         np.asarray(cqb)[0:128], np.asarray(cqb)[128:256]], 1))
    in_common = dict(wkvq=wkvq, waet=waet, baet=baet, dwc=dwc, dwb=dwb,
                     mw1=f32c(np.asarray(mW1).T), mw2=f32c(np.asarray(mW2).T),
                     wo=f32c(np.asarray(Wo).T), gw=f32c(np.asarray(gW).T),
                     gb=f32c(np.asarray(gb)[None, :]), ng=f32c(np.asarray(norm_g)[None, :]),
                     eye=eye, triu=triu, tril=tril, sel=sel)
    in_maps = []
    for core in range(NCORES):
        b, hf = core // 2, core % 2
        im = dict(in_common)
        im["x"] = f32c(x_full[b])
        in_maps.append(im)
    res = run_bass_kernel_spmd(_S3_NC, in_maps, core_ids=list(range(NCORES)))
    y = np.empty((B, T, C), np.float32)
    for b in range(B):
        y[b] = res.results[2 * b]["y"]
    return y


# version 04 (final)
# speedup vs baseline: 1.0053x; 1.0012x over previous
import numpy as np
import ml_dtypes
from contextlib import ExitStack

import concourse.bass as bass
import concourse.mybir as mybir
from concourse.bass_utils import run_bass_kernel_spmd
from concourse.expressions import RuntimeValue

F32 = mybir.dt.float32
BF16 = mybir.dt.bfloat16
AF = mybir.ActivationFunctionType
ALU = mybir.AluOpType
AX = mybir.AxisListType.X
BF = ml_dtypes.bfloat16

NCORES = 8
B, T, C = 4, 4096, 1024
CHUNK = 512
HALF = T // 2
NCHOWN = HALF // CHUNK
LNEPS = 1e-5
LN0EPS = 1e-10


def f32c(x):
    return np.ascontiguousarray(np.asarray(x, dtype=np.float32))


class Ctr:
    def __init__(self, ctx, nc, name, step):
        self.sem = ctx.enter_context(nc.semaphore(name))
        self.n = 0
        self.step = step

    def mark(self, inst):
        self.n += self.step
        inst.then_inc(self.sem, self.step)
        return self.n


class Env:
    def __init__(self, nc, ctx):
        self.nc = nc
        self.ctx = ctx
        self.qd = Ctr(ctx, nc, "qd", 16)
        self.gd = Ctr(ctx, nc, "gd", 16)
        self.te = Ctr(ctx, nc, "te", 1)
        self.ve = Ctr(ctx, nc, "ve", 1)
        self.se = Ctr(ctx, nc, "se", 1)

    def sb(self, name, shape, dt=F32):
        return self.ctx.enter_context(self.nc.sbuf_tensor(name, shape, dt))

    def ps(self, name, shape, dt=F32):
        return self.ctx.enter_context(self.nc.psum_tensor(name, shape, dt))


def emit_postln(env, block, s_sb, st, junk_b, out_ap, eps_ap, sum_token, out2_ap=None,
                sum_eng="ve"):
    nc = env.nc

    @block.scalar
    def _(s):
        sem = env.ve.sem if sum_eng == "ve" else env.se.sem
        s.wait_ge(sem, sum_token)
        s.activation(junk_b[:, :], s_sb[:, :], AF.Square, accum_out=st[:, 1:2])
        env.se.mark(s.drain())
    sq_done = env.se.n

    return emit_ln_stats_apply(env, block, s_sb[:, :], st, out_ap, eps_ap,
                               sq_done, "se", out2_ap)


def emit_ln_stats_apply(env, block, src_ap, st, out_ap, eps_ap, sums_token,
                        sum_eng, out2_ap=None):
    @block.vector
    def _(v):
        v.wait_ge(env.se.sem if sum_eng == "se" else env.ve.sem, sums_token)
        v.tensor_scalar(st[:, 2:3], st[:, 0:1], 1.0 / C, None, ALU.mult)
        v.tensor_scalar(st[:, 3:4], st[:, 1:2], 1.0 / C, None, ALU.mult)
        v.drain()
        v.tensor_mul(st[:, 4:5], st[:, 2:3], st[:, 2:3])
        v.tensor_scalar(st[:, 8:9], st[:, 2:3], -1.0, None, ALU.mult)
        v.drain()
        v.tensor_sub(st[:, 5:6], st[:, 3:4], st[:, 4:5])
        env.ve.mark(v.drain())
    var_done = env.ve.n

    @block.scalar
    def _(s):
        s.wait_ge(env.ve.sem, var_done)
        inst = s.activation(st[:, 6:7], st[:, 5:6], AF.Sqrt, bias=eps_ap)
        env.se.mark(inst)
    sd_done = env.se.n

    @block.vector
    def _(v):
        v.wait_ge(env.se.sem, sd_done)
        v.reciprocal(st[:, 7:8], st[:, 6:7])
        v.drain()
        v.tensor_mul(st[:, 9:10], st[:, 8:9], st[:, 7:8])
        env.ve.mark(v.drain())
    nm_done = env.ve.n

    @block.scalar
    def _(s):
        s.wait_ge(env.ve.sem, nm_done)
        inst = s.activation(out_ap, src_ap, AF.Identity, bias=st[:, 9:10], scale=st[:, 7:8])
        if out2_ap is not None:
            inst = s.activation(out2_ap, src_ap, AF.Identity, bias=st[:, 9:10], scale=st[:, 7:8])
        env.se.mark(inst)
    return env.se.n


FFN_DEBUG = False


def build_ffn():
    nc = bass.Bass("TRN2", target_bir_lowering=False)
    x_d = nc.dram_tensor("x", [HALF, C], F32, kind="ExternalInput")
    hal_d = nc.dram_tensor("halos", [128, 8, NCHOWN], F32, kind="ExternalInput")
    wr_d = nc.dram_tensor("wr", [C, C], F32, kind="ExternalInput")
    wk_d = nc.dram_tensor("wk", [C, C], F32, kind="ExternalInput")
    wv_d = nc.dram_tensor("wv", [C, C], F32, kind="ExternalInput")
    mur_d = nc.dram_tensor("mur", [128, 8], F32, kind="ExternalInput")
    muk_d = nc.dram_tensor("muk", [128, 8], F32, kind="ExternalInput")
    eye_d = nc.dram_tensor("eye", [128, 128], F32, kind="ExternalInput")
    y_d = nc.dram_tensor("y", [HALF, C], F32, kind="ExternalOutput")
    if FFN_DEBUG:
        dbg = {n: nc.dram_tensor("dbg_" + n, sh, dt, kind="ExternalOutput")
               for n, sh, dt in [("xfm", [128, 8, CHUNK + 1], BF16),
                                 ("xr", [128, 8, CHUNK], BF16),
                                 ("xk", [128, 8, CHUNK], BF16),
                                 ("r", [128, C], F32),
                                 ("kact", [128, 8, CHUNK], BF16),
                                 ("s", [128, C], F32),
                                 ("st", [128, 16, 8], F32),
                                 ("sall", [128, 16, C], F32)]}

    with ExitStack() as ctx:
        env = Env(nc, ctx)
        sb, ps = env.sb, env.ps
        wr_b = sb("wr_b", [128, 8, C], BF16)
        wk_b = sb("wk_b", [128, 8, C], BF16)
        wv_b = sb("wv_b", [128, 8, C], BF16)
        mur_s = sb("mur_s", [128, 8])
        muk_s = sb("muk_s", [128, 8])
        eye_b = sb("eye_b", [128, 128], BF16)
        hal_s = sb("hal_s", [128, 8, NCHOWN])
        xres = sb("xres", [128, 4, C])
        xbf = sb("xbf", [128, 4, C], BF16)
        xfm = sb("xfm", [128, 8, CHUNK + 1], BF16)
        tmp_b = sb("tmp_b", [128, CHUNK], BF16)
        xr = sb("xr", [128, 8, CHUNK], BF16)
        xk = sb("xk", [128, 8, CHUNK], BF16)
        kact = sb("kact", [128, 8, CHUNK], BF16)
        krelu = sb("krelu", [128, 2, CHUNK])
        r_sb = sb("r_sb", [128, 4, C])
        o_sb = sb("o_sb", [128, C])
        s_sb = sb("s_sb", [128, C])
        junk_b = sb("junk_b", [128, C], BF16)
        st = sb("st", [128, 12])
        epsc = sb("epsc", [128, 1])
        ynew = sb("ynew", [128, 2, C])
        s4 = sb("s4", [128, 4, C])
        st4f = sb("st4f", [128, 40])
        y4 = sb("y4", [128, 4, C])
        pT = [ps("pT0", [128, 1024], BF16), ps("pT1", [128, 1024], BF16)]
        pR = [ps("pR0", [128, 512]), ps("pR1", [128, 512])]
        pK = [ps("pK0", [128, 512]), ps("pK1", [128, 512])]
        pV = [ps("pV0", [128, 512]), ps("pV1", [128, 512])]
        block = ctx.enter_context(nc.Block())

        @block.gpsimd
        def _(g):
            for wsb, wd in ((wr_b, wr_d), (wk_b, wk_d), (wv_b, wv_d)):
                inst = g.dma_start(out=wsb[:, :, :],
                                   in_=wd.ap().rearrange("(k p) o -> p k o", p=128))
                env.gd.mark(inst)
            env.gd.mark(g.dma_start(out=eye_b[:, :], in_=eye_d[:, :]))
        wload_g = env.gd.n

        @block.vector
        def _(v):
            inst = v.memset(epsc[:, :], float(LNEPS))
            env.ve.mark(inst)

        @block.sync
        def _(q):
            env.qd.mark(q.dma_start(out=hal_s[:, :, :], in_=hal_d[:, :, :]))
            env.qd.mark(q.dma_start(out=mur_s[:, :], in_=mur_d[:, :]))
            env.qd.mark(q.dma_start(out=muk_s[:, :], in_=muk_d[:, :]))
        wload_q = env.qd.n

        evac_tok = [0, 0]
        prev_y_done = [0]
        sig_tok = {}
        kact_read = {}
        pv_tok = {}

        for ci in range(NCHOWN):
            t0 = ci * CHUNK

            @block.sync
            def _(q, t0=t0, ci=ci, ve_prev=env.ve.n):
                if ci > 0:
                    q.wait_ge(env.ve.sem, ve_prev)
                for tb in range(4):
                    env.qd.mark(q.dma_start(out=xres[:, tb, :],
                                            in_=x_d[t0 + tb * 128: t0 + (tb + 1) * 128, :]))
            loads_q = env.qd.n

            @block.vector
            def _(v, loads_q=loads_q, te_prev=env.te.n):
                v.wait_ge(env.qd.sem, loads_q)
                if te_prev:
                    v.wait_ge(env.te.sem, te_prev)
                for tb in range(4):
                    v.tensor_copy(xbf[:, tb, :], xres[:, tb, :])
                inst = v.tensor_copy(xfm[:, :, 0:1], hal_s[:, :, ci:ci + 1])
                env.ve.mark(inst)
            cast_done = env.ve.n

            for tb in range(4):
                @block.tensor
                def _(te, tb=tb, ci=ci):
                    if ci == 0 and tb == 0:
                        te.wait_ge(env.gd.sem, wload_g)
                    te.wait_ge(env.ve.sem, max(cast_done, evac_tok[tb % 2]))
                    for k in range(8):
                        inst = te.transpose(pT[tb % 2][:, k * 128:(k + 1) * 128],
                                            xbf[:, tb, k * 128:(k + 1) * 128], eye_b[:, :])
                    env.te.mark(inst)
                tr_done = env.te.n

                @block.vector
                def _(v, tb=tb, tr_done=tr_done):
                    v.wait_ge(env.te.sem, tr_done)
                    src = pT[tb % 2][:, :].rearrange("p (k t) -> p k t", k=8)
                    inst = v.tensor_copy(xfm[:, :, 1 + tb * 128: 1 + (tb + 1) * 128], src)
                    env.ve.mark(inst)
                evac_tok[tb % 2] = env.ve.n


            @block.vector
            def _(v, ci=ci):
                if ci == 0:
                    v.wait_ge(env.gd.sem, wload_g)
                    v.wait_ge(env.qd.sem, wload_q)
                for k in range(8):
                    v.tensor_sub(tmp_b[:, :], xfm[:, k, 1:513], xfm[:, k, 0:512])
                    v.scalar_tensor_tensor(xr[:, k, :], tmp_b[:, :], mur_s[:, k:k + 1],
                                           xfm[:, k, 0:512], ALU.mult, ALU.add)
                    inst = v.scalar_tensor_tensor(xk[:, k, :], tmp_b[:, :], muk_s[:, k:k + 1],
                                                  xfm[:, k, 0:512], ALU.mult, ALU.add)
                env.ve.mark(inst)
            mix_done = env.ve.n
            if FFN_DEBUG and ci == 0:
                @block.gpsimd
                def _(g):
                    g.wait_ge(env.ve.sem, mix_done)
                    env.gd.mark(g.dma_start(out=dbg["xfm"].ap(), in_=xfm[:, :, :]))
                    env.gd.mark(g.dma_start(out=dbg["xr"].ap(), in_=xr[:, :, :]))
                    env.gd.mark(g.dma_start(out=dbg["xk"].ap(), in_=xk[:, :, :]))

            for tb in range(4):
                for oh in range(2):
                    @block.tensor
                    def _(te, tb=tb, oh=oh):
                        te.wait_ge(env.ve.sem, mix_done)
                        prev = sig_tok.get(oh)
                        if prev:
                            te.wait_ge(env.se.sem, prev)
                        pbank = pR[oh]
                        for kf in range(8):
                            inst = te.matmul(pbank[:, :], xr[:, kf, tb * 128:(tb + 1) * 128],
                                             wr_b[:, kf, oh * 512:(oh + 1) * 512],
                                             start=(kf == 0), stop=(kf == 7))
                        env.te.mark(inst)
                    mm_done = env.te.n

                    @block.scalar
                    def _(s, tb=tb, oh=oh, mm_done=mm_done):
                        s.wait_ge(env.te.sem, mm_done)
                        inst = s.activation(r_sb[:, tb, oh * 512:(oh + 1) * 512],
                                            pR[oh][:, :], AF.Sigmoid)
                        env.se.mark(inst)
                    sig_tok[oh] = env.se.n

            for oc in range(8):
                @block.tensor
                def _(te, oc=oc):
                    if oc == 0:
                        te.wait_ge(env.ve.sem, mix_done)
                    prev = kact_read.get(oc % 2)
                    if prev:
                        te.wait_ge(env.se.sem, prev)
                    for kf in range(8):
                        inst = te.matmul(pK[oc % 2][:, :], wk_b[:, kf, oc * 128:(oc + 1) * 128],
                                         xk[:, kf, :], start=(kf == 0), stop=(kf == 7))
                    env.te.mark(inst)
                kk_done = env.te.n

                @block.vector
                def _(v, oc=oc, kk_done=kk_done):
                    v.wait_ge(env.te.sem, kk_done)
                    if oc >= 2:
                        v.wait_ge(env.se.sem, kact_read[oc % 2])
                    inst = v.tensor_scalar(krelu[:, oc % 2, :], pK[oc % 2][:, :],
                                           0.0, None, ALU.max)
                    env.ve.mark(inst)
                kr_done = env.ve.n

                @block.scalar
                def _(s, oc=oc, kr_done=kr_done):
                    s.wait_ge(env.ve.sem, kr_done)
                    inst = s.activation(kact[:, oc, :], krelu[:, oc % 2, :], AF.Square)
                    env.se.mark(inst)
                kact_read[oc % 2] = env.se.n
            kact_done = env.se.n
            if FFN_DEBUG and ci == 0:
                @block.gpsimd
                def _(g):
                    g.wait_ge(env.se.sem, kact_done)
                    env.gd.mark(g.dma_start(out=dbg["r"].ap(), in_=r_sb[:, 0, :]))
                    env.gd.mark(g.dma_start(out=dbg["kact"].ap(), in_=kact[:, :, :]))

            for tb in range(4):
                @block.tensor
                def _(te, tb=tb):
                    te.wait_ge(env.se.sem, kact_done)
                    for oh in range(2):
                        prev = pv_tok.get(oh)
                        if prev:
                            te.wait_ge(env.ve.sem, prev)
                        for oc in range(8):
                            inst = te.matmul(pV[oh][:, :], kact[:, oc, tb * 128:(tb + 1) * 128],
                                             wv_b[:, oc, oh * 512:(oh + 1) * 512],
                                             start=(oc == 0), stop=(oc == 7))
                        env.te.mark(inst)
                kv_done = env.te.n

                @block.vector
                def _(v, tb=tb, kv_done=kv_done, prev_y=prev_y_done[0]):
                    v.wait_ge(env.te.sem, kv_done)
                    if tb == 0 and prev_y:
                        v.wait_ge(env.se.sem, prev_y)
                    for oh in range(2):
                        v.tensor_mul(o_sb[:, oh * 512:(oh + 1) * 512], pV[oh][:, :],
                                     r_sb[:, tb, oh * 512:(oh + 1) * 512])
                    v.scalar_tensor_tensor(s4[:, tb, :], o_sb[:, :], 1.0, xres[:, tb, :],
                                           ALU.mult, ALU.add, accum_out=st4f[:, tb:tb + 1])
                    env.ve.mark(v.drain())
                pv_tok[0] = pv_tok[1] = env.ve.n
            sums_tok = env.ve.n

            @block.scalar
            def _(s, sums_tok=sums_tok):
                s.wait_ge(env.ve.sem, sums_tok)
                for tb in range(4):
                    s.activation(junk_b[:, :], s4[:, tb, :], AF.Square,
                                 accum_out=st4f[:, 4 + tb:5 + tb])
                env.se.mark(s.drain())
            sq_tok = env.se.n

            @block.vector
            def _(v, sq_tok=sq_tok):
                v.wait_ge(env.se.sem, sq_tok)
                v.tensor_scalar(st4f[:, 8:12], st4f[:, 0:4], 1.0 / C, None, ALU.mult)
                v.tensor_scalar(st4f[:, 12:16], st4f[:, 4:8], 1.0 / C, None, ALU.mult)
                v.drain()
                v.tensor_mul(st4f[:, 16:20], st4f[:, 8:12], st4f[:, 8:12])
                v.tensor_scalar(st4f[:, 20:24], st4f[:, 8:12], -1.0, None, ALU.mult)
                v.drain()
                v.tensor_sub(st4f[:, 24:28], st4f[:, 12:16], st4f[:, 16:20])
                env.ve.mark(v.drain())
            var_tok = env.ve.n

            @block.scalar
            def _(s, var_tok=var_tok):
                s.wait_ge(env.ve.sem, var_tok)
                inst = s.activation(st4f[:, 28:32], st4f[:, 24:28], AF.Sqrt,
                                    bias=epsc[:, 0:1])
                env.se.mark(inst)
            sd_tok = env.se.n

            @block.vector
            def _(v, sd_tok=sd_tok):
                v.wait_ge(env.se.sem, sd_tok)
                v.reciprocal(st4f[:, 32:36], st4f[:, 28:32])
                v.drain()
                v.tensor_mul(st4f[:, 36:40], st4f[:, 20:24], st4f[:, 32:36])
                env.ve.mark(v.drain())
            nm_tok = env.ve.n

            @block.scalar
            def _(s, nm_tok=nm_tok, stp=env.qd.n if ci > 0 else 0):
                s.wait_ge(env.ve.sem, nm_tok)
                if stp:
                    s.wait_ge(env.qd.sem, stp)
                for tb in range(4):
                    inst = s.activation(y4[:, tb, :], s4[:, tb, :], AF.Identity,
                                        bias=st4f[:, 36 + tb:37 + tb],
                                        scale=st4f[:, 32 + tb:33 + tb])
                env.se.mark(inst)
            y_done = env.se.n
            prev_y_done[0] = y_done

            @block.sync
            def _(q, ci=ci, y_done=y_done):
                q.wait_ge(env.se.sem, y_done)
                env.qd.mark(q.dma_start(
                    out=y_d[ci * CHUNK: (ci + 1) * CHUNK, :].rearrange(
                        "(t p) c -> p t c", p=128),
                    in_=y4[:, :, :]))

        @block.sync
        def _(q):
            q.wait_ge(env.qd.sem, env.qd.n)
    return nc


_FFN_NC = None


def run_ffn(x_full, W_r, W_k, W_v, mu_r, mu_k):
    global _FFN_NC
    if _FFN_NC is None:
        _FFN_NC = build_ffn()
    eye = np.eye(128, dtype=np.float32)
    mur = f32c(np.asarray(mu_r).reshape(8, 128).T)
    muk = f32c(np.asarray(mu_k).reshape(8, 128).T)
    wr = f32c(np.asarray(W_r).T)
    wk = f32c(np.asarray(W_k).T)
    wv = f32c(np.asarray(W_v).T)
    in_maps = []
    for core in range(NCORES):
        b, h = core // 2, core % 2
        xo = f32c(x_full[b, h * HALF:(h + 1) * HALF])
        halos = np.zeros((128, 8, NCHOWN), np.float32)
        for ci in range(NCHOWN):
            tg = h * HALF + ci * CHUNK - 1
            if tg >= 0:
                halos[:, :, ci] = x_full[b, tg].reshape(8, 128).T
        in_maps.append(dict(x=xo, halos=halos, wr=wr, wk=wk, wv=wv,
                            mur=mur, muk=muk, eye=eye))
    res = run_bass_kernel_spmd(_FFN_NC, in_maps, core_ids=list(range(NCORES)))
    y = np.empty((B, T, C), np.float32)
    for core in range(NCORES):
        b, h = core // 2, core % 2
        y[b, h * HALF:(h + 1) * HALF] = res.results[core]["y"]
    return y


def kernel(**inputs):
    z = inputs
    x = f32c(z["L0_attn_x"])
    O = run_s1a(x, z["L0_attn_w_attn"], z["L0_attn_b_attn"])
    x = run_s1b(x, O, z["L0_attn_w_proj"], z["L0_attn_b_proj"],
                z["L0_attn_w_fc"], z["L0_attn_b_fc"],
                z["L0_attn_w_mlp_proj"], z["L0_attn_b_mlp_proj"])
    x = run_ffn(x, z["L0_ffn_W_r"], z["L0_ffn_W_k"], z["L0_ffn_W_v"],
                z["L0_ffn_mu_r"], z["L0_ffn_mu_k"])
    x = run_s3(x, z["L1_attn_Wk"], z["L1_attn_Wv"], z["L1_attn_Wq"], z["L1_attn_Wo"],
               z["L1_attn_ckw"], z["L1_attn_ckb"], z["L1_attn_cvw"], z["L1_attn_cvb"],
               z["L1_attn_cqw"], z["L1_attn_cqb"], z["L1_attn_mW1"], z["L1_attn_mW2"],
               z["L1_attn_aW"], z["L1_attn_ab"], z["L1_attn_eW"], z["L1_attn_eb"],
               z["L1_attn_tW"], z["L1_attn_tb"], z["L1_attn_norm_g"],
               z["L1_attn_gW"], z["L1_attn_gb"])
    x = run_ffn(x, z["L1_ffn_W_r"], z["L1_ffn_W_k"], z["L1_ffn_W_v"],
                z["L1_ffn_mu_r"], z["L1_ffn_mu_k"])
    return x


NH = 8
HD = 64
NSP = T // CHUNK


def build_s1a():
    nc = bass.Bass("TRN2", target_bir_lowering=False)
    x_d = nc.dram_tensor("x", [T, C], F32, kind="ExternalInput")
    wq_d = nc.dram_tensor("wq", [C, 512], F32, kind="ExternalInput")
    wk_d = nc.dram_tensor("wk", [C, 512], F32, kind="ExternalInput")
    wv_d = nc.dram_tensor("wv", [C, 512], F32, kind="ExternalInput")
    bq_d = nc.dram_tensor("bq", [128, 4], F32, kind="ExternalInput")
    bk_d = nc.dram_tensor("bk", [128, 4], F32, kind="ExternalInput")
    bv_d = nc.dram_tensor("bv", [1, 512], F32, kind="ExternalInput")
    msk_d = nc.dram_tensor("masks", [128, 4, 512], F32, kind="ExternalInput")
    eye_d = nc.dram_tensor("eye", [128, 128], F32, kind="ExternalInput")
    on2_d = nc.dram_tensor("ones2", [128, 2], F32, kind="ExternalInput")
    o_d = nc.dram_tensor("o", [NH, HD, T], BF16, kind="ExternalOutput")

    with ExitStack() as ctx:
        env = Env(nc, ctx)
        sb, ps = env.sb, env.ps
        wq_b = sb("wq_b", [128, 8, 512], BF16)
        wk_b = sb("wk_b", [128, 8, 512], BF16)
        wv_b = sb("wv_b", [128, 8, 512], BF16)
        bq_s = sb("bq_s", [128, 4])
        bk_s = sb("bk_s", [128, 4])
        bv_b = sb("bv_b", [1, 512], BF16)
        msk_s = sb("msk_s", [128, 4, 512])
        eye_b = sb("eye_b", [128, 128], BF16)
        eye_f = sb("eye_f", [128, 128])
        on2_b = sb("on2_b", [128, 2], BF16)
        onesc_f = sb("onesc_f", [1, 128])
        onesr_b = sb("onesr_b", [1, 128], BF16)
        xres = sb("xres", [128, 4, C])
        xbf = sb("xbf", [128, 4, C], BF16)
        junk_b = sb("junk_b", [128, C], BF16)
        st = sb("st", [128, 12])
        st4 = sb("st4", [128, 40])
        eps0 = sb("eps0", [128, 1])
        ln1b4 = sb("ln1b4", [128, 4, C], BF16)
        ln1fm = sb("ln1fm", [128, 8, CHUNK], BF16)
        q_all = sb("q_all", [128, 4, T], BF16)
        k_all = sb("k_all", [128, 4, T], BF16)
        v_all = sb("v_all", [128, 32, 4, 130], BF16)
        q2_b = sb("q2_b", [128, 512], BF16)
        ncoll = sb("ncoll", [128, 64])
        mhat = sb("mhat", [128, 1])
        m1x1 = sb("m1x1", [1, 4])
        p_sb = sb("p_sb", [128, 4, CHUNK], BF16)
        recl = sb("recl", [1, CHUNK])
        r_sb2 = sb("r_sb2", [64, CHUNK])
        onrm = sb("onrm", [64, CHUNK], BF16)
        pPool = ps("pPool", [128, 2048])
        pO = ps("pO", [128, 512])
        pR = ps("pR2", [64, 512])
        pN = ps("pN", [128, 512])
        pT = ps("pTa", [128, 1024], BF16)
        block = ctx.enter_context(nc.Block())

        @block.gpsimd
        def _(g):
            for wsb, wd in ((wq_b, wq_d), (wk_b, wk_d), (wv_b, wv_d)):
                env.gd.mark(g.dma_start(out=wsb[:, :, :],
                                        in_=wd.ap().rearrange("(f p) o -> p f o", p=128)))
            env.gd.mark(g.dma_start(out=eye_b[:, :], in_=eye_d[:, :]))
            env.gd.mark(g.dma_start(out=on2_b[:, :], in_=on2_d[:, :]))
            env.gd.mark(g.dma_start(out=bv_b[:, :], in_=bv_d[:, :]))
        wload_g = env.gd.n

        @block.sync
        def _(q):
            env.qd.mark(q.dma_start(out=bq_s[:, :], in_=bq_d[:, :]))
            env.qd.mark(q.dma_start(out=bk_s[:, :], in_=bk_d[:, :]))
            env.qd.mark(q.dma_start(out=msk_s[:, :, :], in_=msk_d[:, :, :]))
            env.qd.mark(q.dma_start(out=eye_f[:, :], in_=eye_d[:, :]))
        wload_q = env.qd.n

        @block.vector
        def _(v):
            v.wait_ge(env.gd.sem, wload_g)
            v.wait_ge(env.qd.sem, wload_q)
            v.memset(eps0[:, :], float(LN0EPS))
            v.memset(onesc_f[:, :], 1.0)
            v.memset(onesr_b[:, :], 1.0)
            v.memset(v_all[:, :, :, 64:65], 1.0)
            v.memset(v_all[:, :, :, 129:130], 1.0)
            env.ve.mark(v.drain())
        const_ve = env.ve.n

        evac_tok = [0]
        pn_free = [None]
        po_free = [0]
        q2_free = [0]
        nunit = [0]
        for sp in range(NSP):
            t0 = sp * CHUNK

            @block.sync
            def _(q, t0=t0, sp=sp, ve_prev=env.ve.n):
                if sp > 0:
                    q.wait_ge(env.ve.sem, ve_prev)
                for tb in range(4):
                    env.qd.mark(q.dma_start(out=xres[:, tb, :],
                                            in_=x_d[t0 + tb * 128: t0 + (tb + 1) * 128, :]))
            loads_q = env.qd.n

            @block.scalar
            def _(s, loads_q=loads_q, te_prev=env.te.n):
                s.wait_ge(env.qd.sem, loads_q)
                s.wait_ge(env.ve.sem, const_ve)
                if te_prev:
                    s.wait_ge(env.te.sem, te_prev)
                for tb in range(4):
                    s.activation(xbf[:, tb, :], xres[:, tb, :], AF.Identity,
                                 accum_out=st4[:, tb:tb + 1])
                    s.activation(junk_b[:, :], xres[:, tb, :], AF.Square,
                                 accum_out=st4[:, 4 + tb:5 + tb])
                env.se.mark(s.drain())
            sums_tok = env.se.n

            @block.vector
            def _(v, sums_tok=sums_tok):
                v.wait_ge(env.se.sem, sums_tok)
                v.tensor_scalar(st4[:, 8:12], st4[:, 0:4], 1.0 / C, None, ALU.mult)
                v.tensor_scalar(st4[:, 12:16], st4[:, 4:8], 1.0 / C, None, ALU.mult)
                v.drain()
                v.tensor_mul(st4[:, 16:20], st4[:, 8:12], st4[:, 8:12])
                v.tensor_scalar(st4[:, 20:24], st4[:, 8:12], -1.0, None, ALU.mult)
                v.drain()
                v.tensor_sub(st4[:, 24:28], st4[:, 12:16], st4[:, 16:20])
                env.ve.mark(v.drain())
            var_tok = env.ve.n

            @block.scalar
            def _(s, var_tok=var_tok):
                s.wait_ge(env.ve.sem, var_tok)
                inst = s.activation(st4[:, 28:32], st4[:, 24:28], AF.Sqrt, bias=eps0[:, 0:1])
                env.se.mark(inst)
            sd_tok = env.se.n

            @block.vector
            def _(v, sd_tok=sd_tok):
                v.wait_ge(env.se.sem, sd_tok)
                v.reciprocal(st4[:, 32:36], st4[:, 28:32])
                v.drain()
                v.tensor_mul(st4[:, 36:40], st4[:, 20:24], st4[:, 32:36])
                env.ve.mark(v.drain())
            nm_tok = env.ve.n

            @block.scalar
            def _(s, nm_tok=nm_tok):
                s.wait_ge(env.ve.sem, nm_tok)
                for tb in range(4):
                    inst = s.activation(ln1b4[:, tb, :], xres[:, tb, :], AF.Identity,
                                        bias=st4[:, 36 + tb:37 + tb],
                                        scale=st4[:, 32 + tb:33 + tb])
                env.se.mark(inst)
            ln_done = env.se.n

            for tb in range(4):
                @block.tensor
                def _(te, tb=tb, ln_done=ln_done, ev=evac_tok[0]):
                    if tb == 0:
                        te.wait_ge(env.se.sem, ln_done)
                    if ev:
                        te.wait_ge(env.ve.sem, ev)
                    for kf in range(8):
                        inst = te.transpose(pT[:, kf * 128:(kf + 1) * 128],
                                            ln1b4[:, tb, kf * 128:(kf + 1) * 128], eye_b[:, :])
                    env.te.mark(inst)
                tr_done = env.te.n

                @block.vector
                def _(v, tb=tb, tr_done=tr_done):
                    v.wait_ge(env.te.sem, tr_done)
                    src = pT[:, :].rearrange("p (f t) -> p f t", f=8)
                    inst = v.tensor_copy(ln1fm[:, :, tb * 128:(tb + 1) * 128], src)
                    env.ve.mark(inst)
                evac_tok[0] = env.ve.n
            fm_done = env.ve.n

            for wsb, bias_s, dst in ((wq_b, bq_s, q_all), (wk_b, bk_s, k_all)):
                for oc in range(4):
                    @block.tensor
                    def _(te, oc=oc, fm_done=fm_done, pf=pn_free[0], wsb=wsb):
                        te.wait_ge(env.ve.sem, fm_done)
                        if pf:
                            te.wait_ge(env.se.sem if pf[0] == "se" else env.ve.sem, pf[1])
                        for kf in range(8):
                            inst = te.matmul(pN[:, :], wsb[:, kf, oc * 128:(oc + 1) * 128],
                                             ln1fm[:, kf, :], start=(kf == 0), stop=(kf == 7))
                        env.te.mark(inst)
                    mm_tok = env.te.n

                    @block.scalar
                    def _(s, oc=oc, sp=sp, mm_tok=mm_tok, bias_s=bias_s, dst=dst,
                          q2f=q2_free[0]):
                        s.wait_ge(env.te.sem, mm_tok)
                        inst = s.activation(dst[:, oc, sp * CHUNK:(sp + 1) * CHUNK],
                                            pN[:, :], AF.Identity, bias=bias_s[:, oc:oc + 1])
                        env.se.mark(inst)
                        if q2f:
                            s.wait_ge(env.te.sem, q2f)
                        inst = s.activation(q2_b[:, :], pN[:, :], AF.Square,
                                            bias=bias_s[:, oc:oc + 1])
                        env.se.mark(inst)
                    pn_free[0] = ("se", env.se.n)
                    sq_tok = env.se.n

                    @block.tensor
                    def _(te, sq_tok=sq_tok, pof=po_free[0]):
                        te.wait_ge(env.se.sem, sq_tok)
                        if pof:
                            te.wait_ge(env.ve.sem, pof)
                        for blk in range(4):
                            inst = te.matmul(pO[:, blk * 2: blk * 2 + 2],
                                             q2_b[:, blk * 128:(blk + 1) * 128], on2_b[:, :],
                                             start=True, stop=True)
                        env.te.mark(inst)
                    q2_free[0] = env.te.n
                    nm_tok = env.te.n

                    @block.vector
                    def _(v, nm_tok=nm_tok, u=nunit[0]):
                        v.wait_ge(env.te.sem, nm_tok)
                        inst = v.reduce_max(ncoll[:, u:u + 1], pO[:, 0:8], axis=AX)
                        env.ve.mark(inst)
                    po_free[0] = env.ve.n
                    nunit[0] += 1

            for blk in range(4):
                @block.tensor
                def _(te, blk=blk, pf=pn_free[0]):
                    if pf:
                        te.wait_ge(env.se.sem if pf[0] == "se" else env.ve.sem, pf[1])
                    for kf in range(8):
                        te.matmul(pN[:, :], ln1fm[:, kf, blk * 128:(blk + 1) * 128],
                                  wv_b[:, kf, :], start=(kf == 0), stop=False)
                    inst = te.matmul(pN[:, :], onesr_b[0:1, :], bv_b[:, :],
                                     start=False, stop=True)
                    env.te.mark(inst)
                mm_tok = env.te.n

                @block.vector
                def _(v, sp=sp, blk=blk, mm_tok=mm_tok):
                    v.wait_ge(env.te.sem, mm_tok)
                    gblk = sp * 4 + blk
                    vv = pN[:, :].rearrange("p (h two d) -> p h two d", two=2, d=64)
                    v.tensor_copy(v_all[:, gblk, :, 0:64], vv[:, :, 0, :])
                    inst = v.tensor_copy(v_all[:, gblk, :, 65:129], vv[:, :, 1, :])
                    env.ve.mark(inst)
                pn_free[0] = ("ve", env.ve.n)

        @block.vector
        def _(v, tok=env.ve.n):
            inst = v.reduce_max(st[:, 10:11], ncoll[:, :], axis=AX)
            env.ve.mark(inst)
        nmx_tok = env.ve.n

        @block.tensor
        def _(te, nmx_tok=nmx_tok):
            te.wait_ge(env.ve.sem, nmx_tok)
            inst = te.transpose(pN[0:1, 0:128], st[:, 10:11], eye_f[:, :])
            env.te.mark(inst)
        tr_tok = env.te.n

        @block.vector
        def _(v, tr_tok=tr_tok):
            v.wait_ge(env.te.sem, tr_tok)
            v.reduce_max(m1x1[0:1, 0:1], pN[0:1, 0:128], axis=AX)
            v.drain()
            inst = v.tensor_scalar(m1x1[0:1, 1:2], m1x1[0:1, 0:1], -0.125, None, ALU.mult)
            env.ve.mark(v.drain())
        neg_tok = env.ve.n

        @block.tensor
        def _(te, neg_tok=neg_tok):
            te.wait_ge(env.ve.sem, neg_tok)
            inst = te.matmul(pN[:, 0:1], onesc_f[0:1, :], m1x1[0:1, 1:2],
                             start=True, stop=True)
            env.te.mark(inst)
        bc_tok = env.te.n

        @block.vector
        def _(v, bc_tok=bc_tok):
            v.wait_ge(env.te.sem, bc_tok)
            inst = v.tensor_copy(mhat[:, :], pN[:, 0:1])
            env.ve.mark(inst)
        mhat_tok = env.ve.n

        ppool_free = [0]
        psb_free = [0]
        po_attn = [0]
        pr_free = [0]
        store_tok = [0]
        for h in range(NH):
            pair, w = h // 2, h % 2
            for sp2 in range(NSP):
                ngr = sp2 + 1
                for g in range(ngr):
                    @block.tensor
                    def _(te, h=h, sp2=sp2, g=g, pair=pair, w=w,
                          ppf=ppool_free[0], mh=mhat_tok):
                        if ppf:
                            te.wait_ge(env.se.sem, ppf)
                        if h == 0 and sp2 == 0 and g == 0:
                            te.wait_ge(env.ve.sem, mh)
                        for j in range(4):
                            kt = g * 4 + j
                            inst = te.matmul(pPool[:, j * 512:(j + 1) * 512],
                                             k_all[w * 64:(w + 1) * 64, pair,
                                                   kt * 128:(kt + 1) * 128],
                                             q_all[w * 64:(w + 1) * 64, pair,
                                                   sp2 * CHUNK:(sp2 + 1) * CHUNK],
                                             start=True, stop=True)
                        env.te.mark(inst)
                    sc_tok = env.te.n

                    if g == sp2:
                        @block.vector
                        def _(v, sc_tok=sc_tok):
                            v.wait_ge(env.te.sem, sc_tok)
                            for j in range(4):
                                inst = v.tensor_tensor(pPool[:, j * 512:(j + 1) * 512],
                                                       pPool[:, j * 512:(j + 1) * 512],
                                                       msk_s[:, j, :], ALU.add)
                            env.ve.mark(inst)
                        msk_tok = env.ve.n
                    else:
                        msk_tok = None

                    @block.scalar
                    def _(s, sc_tok=sc_tok, msk_tok=msk_tok, pvf=psb_free[0]):
                        s.wait_ge(env.te.sem, sc_tok)
                        if msk_tok:
                            s.wait_ge(env.ve.sem, msk_tok)
                        if pvf:
                            s.wait_ge(env.te.sem, pvf)
                        inst = s.activation(p_sb[:, :, :], pPool[:, :], AF.Exp,
                                            bias=mhat[:, 0:1], scale=0.125)
                        env.se.mark(inst)
                    ppool_free[0] = env.se.n
                    exp_tok = env.se.n

                    @block.tensor
                    def _(te, h=h, sp2=sp2, g=g, pair=pair, w=w, exp_tok=exp_tok,
                          poa=po_attn[0]):
                        te.wait_ge(env.se.sem, exp_tok)
                        if g == 0 and poa:
                            te.wait_ge(env.ve.sem, poa)
                        for j in range(4):
                            kt = g * 4 + j
                            inst = te.matmul(pO[0:65, :],
                                             v_all[:, kt, pair, w * 65:(w + 1) * 65],
                                             p_sb[:, j, :],
                                             start=(g == 0 and j == 0),
                                             stop=(g == sp2 and j == 3))
                        env.te.mark(inst)
                    psb_free[0] = env.te.n
                pv_tok = env.te.n

                @block.vector
                def _(v, pv_tok=pv_tok, prf=pr_free[0]):
                    v.wait_ge(env.te.sem, pv_tok)
                    inst = v.reciprocal(recl[:, :], pO[64:65, :])
                    env.ve.mark(v.drain())
                rec_tok = env.ve.n

                @block.tensor
                def _(te, rec_tok=rec_tok, prf=pr_free[0]):
                    te.wait_ge(env.ve.sem, rec_tok)
                    if prf:
                        te.wait_ge(env.ve.sem, prf)
                    inst = te.matmul(pR[:, :], onesc_f[0:1, 0:64], recl[0:1, :],
                                     start=True, stop=True)
                    env.te.mark(inst)
                rb_tok = env.te.n

                @block.vector
                def _(v, rb_tok=rb_tok, stok=store_tok[0]):
                    v.wait_ge(env.te.sem, rb_tok)
                    if stok:
                        v.wait_ge(env.qd.sem, stok)
                    v.tensor_copy(r_sb2[:, :], pR[:, :])
                    v.drain()
                    inst = v.tensor_mul(onrm[:, :], pO[0:64, :], r_sb2[:, :])
                    env.ve.mark(inst)
                pr_free[0] = env.ve.n
                po_attn[0] = env.ve.n
                on_tok = env.ve.n

                @block.sync
                def _(q, h=h, sp2=sp2, on_tok=on_tok):
                    q.wait_ge(env.ve.sem, on_tok)
                    env.qd.mark(q.dma_start(
                        out=o_d.ap()[h, :, sp2 * CHUNK:(sp2 + 1) * CHUNK],
                        in_=onrm[:, :]))
                store_tok[0] = env.qd.n

        @block.sync
        def _(q):
            q.wait_ge(env.qd.sem, env.qd.n)
    return nc


_S1A_NC = None


def s1a_inputs(x_full, w_attn, b_attn):
    w_attn = np.asarray(w_attn)
    b_attn = np.asarray(b_attn)
    eye = np.eye(128, dtype=np.float32)
    ones2 = np.zeros((128, 2), np.float32)
    ones2[0:64, 0] = 1.0
    ones2[64:128, 1] = 1.0
    masks = np.zeros((128, 4, 512), np.float32)
    r = np.arange(128)[:, None]
    cc = np.arange(512)[None, :]
    for m in range(4):
        masks[:, m, :] = np.where(m * 128 + r <= cc, 0.0, -1e9)
    in_maps = []
    for core in range(NCORES):
        b, hg = core // 2, core % 2
        rs = hg * 512
        wq = f32c(w_attn[rs:rs + 512, :].T)
        wk = f32c(w_attn[1024 + rs:1024 + rs + 512, :].T)
        wv = f32c(w_attn[2048 + rs:2048 + rs + 512, :].T)
        bq = f32c(b_attn[rs:rs + 512].reshape(4, 128).T)
        bk = f32c(b_attn[1024 + rs:1024 + rs + 512].reshape(4, 128).T)
        bv = f32c(b_attn[2048 + rs:2048 + rs + 512][None, :])
        in_maps.append(dict(x=f32c(x_full[b]), wq=wq, wk=wk, wv=wv,
                            bq=bq, bk=bk, bv=bv, masks=masks, eye=eye, ones2=ones2))
    return in_maps


def run_s1a(x_full, w_attn, b_attn):
    global _S1A_NC
    if _S1A_NC is None:
        _S1A_NC = build_s1a()
    res = run_bass_kernel_spmd(_S1A_NC, s1a_inputs(x_full, w_attn, b_attn),
                               core_ids=list(range(NCORES)))
    O = np.empty((B, 16, HD, T), BF)
    for core in range(NCORES):
        b, hg = core // 2, core % 2
        O[b, hg * 8:(hg + 1) * 8] = np.asarray(res.results[core]["o"], dtype=BF)
    return O


S1B_PHASES = 3
S1B_D1MASK = 15
S1B_DBGX1 = False
S1B_LOAD_WBIG = True
S1B_NSP = NCHOWN


def build_s1b():
    nc = bass.Bass("TRN2", target_bir_lowering=False)
    x_d = nc.dram_tensor("x", [HALF, C], F32, kind="ExternalInput")
    o_in = nc.dram_tensor("oin", [16, HD, HALF], BF16, kind="ExternalInput")
    wp_d = nc.dram_tensor("wp", [C, C], F32, kind="ExternalInput")
    bp_d = nc.dram_tensor("bp", [1, C], F32, kind="ExternalInput")
    wf_d = nc.dram_tensor("wf", [C, 4 * C], F32, kind="ExternalInput")
    bf_d = nc.dram_tensor("bfc", [128, 32], F32, kind="ExternalInput")
    wm_d = nc.dram_tensor("wm", [4 * C, C], F32, kind="ExternalInput")
    bm_d = nc.dram_tensor("bm", [1, C], F32, kind="ExternalInput")
    eye_d = nc.dram_tensor("eye", [128, 128], F32, kind="ExternalInput")
    y_d = nc.dram_tensor("y", [HALF, C], F32, kind="ExternalOutput")
    ln2sc = nc.dram_tensor("ln2sc", [NCHOWN, 128, 8, CHUNK], BF16)
    x1sc = nc.dram_tensor("x1sc", [NCHOWN, 4, 128, C], F32)
    hsc = nc.dram_tensor("hsc", [NCHOWN, 32, 128, CHUNK], BF16)

    with ExitStack() as ctx:
        env = Env(nc, ctx)
        sb, ps = env.sb, env.ps
        wp_b = sb("wp_b", [128, 8, C], BF16)
        bp_b = sb("bp_b", [1, C], BF16)
        bm_b = sb("bm_b", [1, C], BF16)
        bf_s = sb("bf_s", [128, 32])
        wbig = sb("wbig", [128, 32768], BF16)
        eye_b = sb("eye_b", [128, 128], BF16)
        onesr_b = sb("onesr_b", [1, 128], BF16)
        o_sb = sb("o_sb", [128, 8, CHUNK], BF16)
        xres = sb("xres", [128, C])
        x1_sb = sb("x1_sb", [128, C])
        t1_sb = sb("t1_sb", [128, C])
        junk_b = sb("junk_b", [128, C], BF16)
        st = sb("st", [128, 12])
        eps0 = sb("eps0", [128, 1])
        epsl = sb("epsl", [128, 1])
        ln2_b = sb("ln2_b", [128, C], BF16)
        ln2fm = sb("ln2fm", [128, 8, CHUNK], BF16)
        h_sb = sb("h_sb", [128, 32, CHUNK], BF16)
        hfc = sb("hfc", [128, 2, CHUNK], BF16)
        ynew = sb("ynew", [128, 2, C])
        pXt = [ps("pX0", [128, 512]), ps("pX1", [128, 512]),
               ps("pX2", [128, 512]), ps("pX3", [128, 512])]
        pX = [pXt[0:2], pXt[2:4]]
        pF = [ps("pF0", [128, 512]), ps("pF1", [128, 512])]
        pT = ps("pTb", [128, 1024], BF16)
        block = ctx.enter_context(nc.Block())
        wf_v = wbig[:, :].rearrange("p (f o) -> p f o", f=8)
        wm_v = wbig[:, :].rearrange("p (h o) -> p h o", h=32)

        @block.gpsimd
        def _(g):
            env.gd.mark(g.dma_start(out=wp_b[:, :, :],
                                    in_=wp_d.ap().rearrange("(f p) o -> p f o", p=128)))
            env.gd.mark(g.dma_start(out=bp_b[:, :], in_=bp_d[:, :]))
            env.gd.mark(g.dma_start(out=bm_b[:, :], in_=bm_d[:, :]))
            env.gd.mark(g.dma_start(out=eye_b[:, :], in_=eye_d[:, :]))
            if S1B_LOAD_WBIG:
                env.gd.mark(g.dma_start(out=wbig[:, :].rearrange("p (f o) -> p f o", f=8),
                                        in_=wf_d.ap().rearrange("(f p) o -> p f o", p=128)))
        wload_g = env.gd.n

        @block.sync
        def _(q):
            env.qd.mark(q.dma_start(out=bf_s[:, :], in_=bf_d[:, :]))
        wload_q = env.qd.n

        @block.vector
        def _(v):
            v.wait_ge(env.gd.sem, wload_g)
            v.wait_ge(env.qd.sem, wload_q)
            v.memset(eps0[:, :], float(LN0EPS))
            v.memset(epsl[:, :], float(LNEPS))
            v.memset(onesr_b[:, :], 1.0)
            env.ve.mark(v.drain())
        const_ve = env.ve.n

        evac_tok = [0]
        px_free = {}
        x1st_tok = [0]
        ln2st_tok = [0]
        for sp in range(S1B_NSP):
            @block.sync
            def _(q, sp=sp, ve_prev=env.ve.n):
                if sp > 0:
                    q.wait_ge(env.ve.sem, ve_prev)
                src = o_in.ap()[:, :, sp * CHUNK:(sp + 1) * CHUNK]
                env.qd.mark(q.dma_start(out=o_sb[:, :, :],
                                        in_=src.rearrange("(pr two) d t -> (two d) pr t", two=2)))
            oload_q = env.qd.n

            for tb in range(4):
                @block.sync
                def _(q, sp=sp, tb=tb, se_prev=env.se.n):
                    if se_prev:
                        q.wait_ge(env.se.sem, se_prev)
                    env.qd.mark(q.dma_start(
                        out=xres[:, :],
                        in_=x_d[sp * CHUNK + tb * 128: sp * CHUNK + (tb + 1) * 128, :]))
                xload_q = env.qd.n

                @block.tensor
                def _(te, sp=sp, tb=tb, oload_q=oload_q, cv=const_ve,
                      pxf=px_free.get(0), xl=xload_q):
                    if not (S1B_D1MASK & 1):
                        return
                    if tb == 0:
                        te.wait_ge(env.qd.sem, oload_q)
                        te.wait_ge(env.ve.sem, cv)
                    if pxf:
                        te.wait_ge(env.ve.sem, pxf)
                    for ot in range(2):
                        for p in range(8):
                            te.matmul(pX[0][ot][:, :],
                                      o_sb[:, p, tb * 128:(tb + 1) * 128],
                                      wp_b[:, p, ot * 512:(ot + 1) * 512],
                                      start=(p == 0), stop=False)
                        inst = te.matmul(pX[0][ot][:, :], onesr_b[0:1, :],
                                         bp_b[0:1, ot * 512:(ot + 1) * 512],
                                         start=False, stop=True)
                    env.te.mark(inst)
                mm_tok = env.te.n

                @block.vector
                def _(v, mm_tok=mm_tok, xl=xload_q, x1st=x1st_tok[0]):
                    if not (S1B_D1MASK & 2):
                        v.wait_ge(env.te.sem, mm_tok)
                        env.ve.mark(v.tensor_copy(x1_sb[:, :], xres[:, :]))
                        return
                    v.wait_ge(env.te.sem, mm_tok)
                    v.wait_ge(env.qd.sem, xl)
                    if x1st:
                        v.wait_ge(env.gd.sem, x1st)
                    v.tensor_add(x1_sb[:, 0:512], pX[0][0][:, :], xres[:, 0:512])
                    v.tensor_add(x1_sb[:, 512:1024], pX[0][1][:, :], xres[:, 512:1024])
                    env.ve.mark(v.drain())
                px_free[0] = env.ve.n
                sum_tok = env.ve.n

                if S1B_D1MASK & 2:
                    @block.scalar
                    def _(s, sum_tok=sum_tok):
                        s.wait_ge(env.ve.sem, sum_tok)
                        s.activation(junk_b[:, :], x1_sb[:, :], AF.Identity, accum_out=st[:, 0:1])
                        s.activation(junk_b[:, :], x1_sb[:, :], AF.Square, accum_out=st[:, 1:2])
                        env.se.mark(s.drain())
                    sq_tok = env.se.n

                    ln_done = emit_ln_stats_apply(env, block, x1_sb[:, :], st, ln2_b[:, :],
                                                  eps0[:, 0:1], sq_tok, "se")
                else:
                    @block.scalar
                    def _(s, sum_tok=sum_tok):
                        s.wait_ge(env.ve.sem, sum_tok)
                        env.se.mark(s.copy(ln2_b[:, :], x1_sb[:, :]))
                    ln_done = env.se.n

                if S1B_D1MASK & 4:
                    @block.gpsimd
                    def _(g, sp=sp, tb=tb, ln_done=ln_done):
                        g.wait_ge(env.se.sem, ln_done)
                        env.gd.mark(g.dma_start(out=x1sc.ap()[sp, tb, :, :], in_=x1_sb[:, :]))
                        if S1B_DBGX1:
                            env.gd.mark(g.dma_start(
                                out=y_d[sp * CHUNK + tb * 128: sp * CHUNK + (tb + 1) * 128, :],
                                in_=x1_sb[:, :]))
                    x1st_tok[0] = env.gd.n

                if not (S1B_D1MASK & 8):
                    continue
                @block.tensor
                def _(te, ln_done=ln_done, ev=evac_tok[0]):
                    te.wait_ge(env.se.sem, ln_done)
                    if ev:
                        te.wait_ge(env.ve.sem, ev)
                    for kf in range(8):
                        inst = te.transpose(pT[:, kf * 128:(kf + 1) * 128],
                                            ln2_b[:, kf * 128:(kf + 1) * 128], eye_b[:, :])
                    env.te.mark(inst)
                tr_done = env.te.n

                @block.vector
                def _(v, tb=tb, tr_done=tr_done, lst=ln2st_tok[0]):
                    v.wait_ge(env.te.sem, tr_done)
                    if tb == 0 and lst:
                        v.wait_ge(env.gd.sem, lst)
                    src = pT[:, :].rearrange("p (f t) -> p f t", f=8)
                    inst = v.tensor_copy(ln2fm[:, :, tb * 128:(tb + 1) * 128], src)
                    env.ve.mark(inst)
                evac_tok[0] = env.ve.n
            fm_done = env.ve.n

            if S1B_D1MASK & 8:
                @block.gpsimd
                def _(g, sp=sp, fm_done=fm_done):
                    g.wait_ge(env.ve.sem, fm_done)
                    env.gd.mark(g.dma_start(out=ln2sc.ap()[sp, :, :, :], in_=ln2fm[:, :, :]))
                ln2st_tok[0] = env.gd.n
        d1_g = env.gd.n
        d1_ve = env.ve.n

        if S1B_PHASES < 2:
            @block.sync
            def _(q):
                q.wait_ge(env.gd.sem, env.gd.n)
                q.wait_ge(env.qd.sem, env.qd.n)
            return nc
        pf_free = {}
        for sp in range(NCHOWN):
            @block.sync
            def _(q, sp=sp, d1_g=d1_g, te_prev=env.te.n):
                if sp == 0:
                    q.wait_ge(env.gd.sem, d1_g)
                else:
                    q.wait_ge(env.te.sem, te_prev)
                env.qd.mark(q.dma_start(out=ln2fm[:, :, :], in_=ln2sc.ap()[sp, :, :, :]))
            lload_q = env.qd.n

            for hc in range(32):
                @block.tensor
                def _(te, hc=hc, lload_q=lload_q, pff=pf_free.get(hc % 2)):
                    if hc == 0:
                        te.wait_ge(env.qd.sem, lload_q)
                    if pff:
                        te.wait_ge(env.se.sem, pff)
                    for kf in range(8):
                        inst = te.matmul(pF[hc % 2][:, :], wf_v[:, kf, hc * 128:(hc + 1) * 128],
                                         ln2fm[:, kf, :], start=(kf == 0), stop=(kf == 7))
                    env.te.mark(inst)
                mm_tok = env.te.n

                @block.scalar
                def _(s, hc=hc, mm_tok=mm_tok, gst=env.gd.n):
                    s.wait_ge(env.te.sem, mm_tok)
                    if gst >= 32:
                        s.wait_ge(env.gd.sem, gst - 16)
                    inst = s.activation(hfc[:, hc % 2, :], pF[hc % 2][:, :],
                                        AF.Gelu_apprx_tanh, bias=bf_s[:, hc:hc + 1])
                    env.se.mark(inst)
                pf_free[hc % 2] = env.se.n
                g_tok = env.se.n

                @block.gpsimd
                def _(g, sp=sp, hc=hc, g_tok=g_tok):
                    g.wait_ge(env.se.sem, g_tok)
                    env.gd.mark(g.dma_start(out=hsc.ap()[sp, hc, :, :],
                                            in_=hfc[:, hc % 2, :]))
        d2_g = env.gd.n

        if S1B_PHASES < 3:
            @block.sync
            def _(q):
                q.wait_ge(env.gd.sem, env.gd.n)
                q.wait_ge(env.qd.sem, env.qd.n)
            return nc
        @block.gpsimd
        def _(g, d2_g=d2_g, te_prev=env.te.n):
            g.wait_ge(env.gd.sem, d2_g)
            g.wait_ge(env.te.sem, te_prev)
            env.gd.mark(g.dma_start(out=wbig[:, :].rearrange("p (h o) -> p h o", h=32),
                                    in_=wm_d.ap().rearrange("(h p) o -> p h o", p=128)))
        wm_g = env.gd.n

        px_free2 = {}
        ylast = [0]
        for sp in range(NCHOWN):
            @block.sync
            def _(q, sp=sp, wm_g=wm_g, te_prev=env.te.n):
                if sp == 0:
                    q.wait_ge(env.gd.sem, wm_g)
                else:
                    q.wait_ge(env.te.sem, te_prev)
                env.qd.mark(q.dma_start(out=h_sb[:, :, :],
                                        in_=hsc.ap()[sp].rearrange("h p t -> p h t")))
            hload_q = env.qd.n

            for tb in range(4):
                @block.sync
                def _(q, sp=sp, tb=tb, ve_prev=env.ve.n):
                    if ve_prev:
                        q.wait_ge(env.ve.sem, ve_prev)
                    env.qd.mark(q.dma_start(
                        out=xres[:, :],
                        in_=x_d[sp * CHUNK + tb * 128: sp * CHUNK + (tb + 1) * 128, :]))
                    env.qd.mark(q.dma_start(out=x1_sb[:, :], in_=x1sc.ap()[sp, tb, :, :]))
                xl_q = env.qd.n

                @block.tensor
                def _(te, tb=tb, hload_q=hload_q, pxf=px_free2.get(tb % 2)):
                    if tb == 0:
                        te.wait_ge(env.qd.sem, hload_q)
                    if pxf:
                        te.wait_ge(env.ve.sem, pxf)
                    for ot in range(2):
                        for hc in range(32):
                            te.matmul(pX[tb % 2][ot][:, :],
                                      h_sb[:, hc, tb * 128:(tb + 1) * 128],
                                      wm_v[:, hc, ot * 512:(ot + 1) * 512],
                                      start=(hc == 0), stop=False)
                        inst = te.matmul(pX[tb % 2][ot][:, :], onesr_b[0:1, :],
                                         bm_b[0:1, ot * 512:(ot + 1) * 512],
                                         start=False, stop=True)
                    env.te.mark(inst)
                mm_tok = env.te.n

                @block.vector
                def _(v, tb=tb, mm_tok=mm_tok, xl=xl_q, yl=ylast[0]):
                    v.wait_ge(env.te.sem, mm_tok)
                    v.wait_ge(env.qd.sem, xl)
                    if yl:
                        v.wait_ge(env.se.sem, yl)
                    v.tensor_add(t1_sb[:, 0:512], pX[tb % 2][0][:, :], x1_sb[:, 0:512])
                    v.tensor_add(t1_sb[:, 512:1024], pX[tb % 2][1][:, :],
                                 x1_sb[:, 512:1024])
                    v.drain()
                    v.scalar_tensor_tensor(t1_sb[:, :], t1_sb[:, :], 1.0, xres[:, :],
                                           ALU.mult, ALU.add, accum_out=st[:, 0:1])
                    env.ve.mark(v.drain())
                px_free2[tb % 2] = env.ve.n
                sum_tok = env.ve.n

                @block.scalar
                def _(s, sum_tok=sum_tok):
                    s.wait_ge(env.ve.sem, sum_tok)
                    s.activation(junk_b[:, :], t1_sb[:, :], AF.Square, accum_out=st[:, 1:2])
                    env.se.mark(s.drain())
                sq_tok = env.se.n

                y_done = emit_ln_stats_apply(env, block, t1_sb[:, :], st,
                                             ynew[:, tb % 2, :], epsl[:, 0:1], sq_tok, "se")
                ylast[0] = y_done

                @block.sync
                def _(q, sp=sp, tb=tb, y_done=y_done):
                    q.wait_ge(env.se.sem, y_done)
                    env.qd.mark(q.dma_start(
                        out=y_d[sp * CHUNK + tb * 128: sp * CHUNK + (tb + 1) * 128, :],
                        in_=ynew[:, tb % 2, :]))
                yst = env.qd.n

                @block.scalar
                def _(s, yst=yst):
                    s.wait_ge(env.qd.sem, yst - 16)

        @block.sync
        def _(q):
            q.wait_ge(env.qd.sem, env.qd.n)
    return nc


_S1B_NC = None


def run_s1b(x_full, O_fm, w_proj, b_proj, w_fc, b_fc, w_mlp, b_mlp):
    global _S1B_NC
    if _S1B_NC is None:
        _S1B_NC = build_s1b()
    eye = np.eye(128, dtype=np.float32)
    wp = f32c(np.asarray(w_proj).T)
    bp = f32c(np.asarray(b_proj)[None, :])
    wf = f32c(np.asarray(w_fc).T)
    bfc = f32c(np.asarray(b_fc).reshape(32, 128).T)
    wm = f32c(np.asarray(w_mlp).T)
    bm = f32c(np.asarray(b_mlp)[None, :])
    in_maps = []
    for core in range(NCORES):
        b, hf = core // 2, core % 2
        in_maps.append(dict(
            x=f32c(x_full[b, hf * HALF:(hf + 1) * HALF]),
            oin=np.ascontiguousarray(O_fm[b][:, :, hf * HALF:(hf + 1) * HALF]),
            wp=wp, bp=bp, wf=wf, bfc=bfc, wm=wm, bm=bm, eye=eye))
    res = run_bass_kernel_spmd(_S1B_NC, in_maps, core_ids=list(range(NCORES)))
    y = np.empty((B, T, C), np.float32)
    for core in range(NCORES):
        b, hf = core // 2, core % 2
        y[b, hf * HALF:(hf + 1) * HALF] = res.results[core]["y"]
    return y


DD = 256
NSUB = 128
GS3 = 0.1
EPS3 = 1e-6


S3_PHASES = 4
S3_DEBUG = False


def build_s3():
    nc = bass.Bass("TRN2", target_bir_lowering=False)
    x_d = nc.dram_tensor("x", [T, C], F32, kind="ExternalInput")
    wkvq_d = nc.dram_tensor("wkvq", [C, 768], F32, kind="ExternalInput")
    waet_d = nc.dram_tensor("waet", [C, 768], F32, kind="ExternalInput")
    baet_d = nc.dram_tensor("baet", [128, 6], F32, kind="ExternalInput")
    dwc_d = nc.dram_tensor("dwc", [128, 6, 4], F32, kind="ExternalInput")
    dwb_d = nc.dram_tensor("dwb", [128, 6], F32, kind="ExternalInput")
    mw1_d = nc.dram_tensor("mw1", [DD, DD], F32, kind="ExternalInput")
    mw2_d = nc.dram_tensor("mw2", [DD, DD], F32, kind="ExternalInput")
    wo_d = nc.dram_tensor("wo", [DD, C], F32, kind="ExternalInput")
    gw_d = nc.dram_tensor("gw", [C, C], F32, kind="ExternalInput")
    gb_d = nc.dram_tensor("gb", [1, C], F32, kind="ExternalInput")
    ng_d = nc.dram_tensor("ng", [1, C], F32, kind="ExternalInput")
    eye_d = nc.dram_tensor("eye", [128, 128], F32, kind="ExternalInput")
    triu_d = nc.dram_tensor("triu", [128, 128], F32, kind="ExternalInput")
    tril_d = nc.dram_tensor("tril", [128, 128], F32, kind="ExternalInput")
    sel_d = nc.dram_tensor("sel", [128, 4], F32, kind="ExternalInput")
    y_d = nc.dram_tensor("y", [T, C], F32, kind="ExternalOutput")
    dbgkind = dict(kind="ExternalOutput") if S3_DEBUG else {}
    praw_d = nc.dram_tensor("praw", [6, 128, T], BF16, **dbgkind)
    kdw_d = nc.dram_tensor("kdw", [6, 128, T], BF16, **dbgkind)
    retr_d = nc.dram_tensor("retr", [32 * 128, DD], F32, **dbgkind)
    if S3_DEBUG:
        dbgst = {n: nc.dram_tensor("dbg_" + n, [128, 2, DD], F32, kind="ExternalOutput")
                 for n in ("w1", "w2", "s1", "s2")}
        dbgco = nc.dram_tensor("dbg_co", [128, 640], F32, kind="ExternalOutput")

    with ExitStack() as ctx:
        env = Env(nc, ctx)
        sb, ps = env.sb, env.ps
        wkvq_b = sb("wkvq_b", [128, 8, 768], BF16)
        waet_b = sb("waet_b", [128, 8, 768], BF16)
        baet_s = sb("baet_s", [128, 6])
        dwc_s = sb("dwc_s", [128, 6, 4])
        dwb_s = sb("dwb_s", [128, 6])
        wo_b = sb("wo_b", [128, 2, C], BF16)
        gw_b = sb("gw_b", [128, 8, C], BF16)
        gb_b = sb("gb_b", [1, C], BF16)
        ng_s = sb("ng_s", [1, C])
        eye_b = sb("eye_b", [128, 128], BF16)
        eye_f = sb("eye_f", [128, 128])
        triu_f = sb("triu_f", [128, 128])
        tril_f = sb("tril_f", [128, 128])
        sel_f = sb("sel_f", [128, 4])
        onesr_b = sb("onesr_b", [1, 128], BF16)
        onesr_f = sb("onesr_f", [1, 128])
        xres = sb("xres", [128, 4, C])
        xbf = sb("xbf", [128, 4, C], BF16)
        xfm = sb("xfm", [128, 8, CHUNK], BF16)
        junk_b = sb("junk_b", [128, C], BF16)
        st = sb("st", [128, 12])
        epsl = sb("epsl", [128, 1])
        eps3c = sb("eps3c", [128, 1])
        scal = sb("scal", [128, 32, 4])
        ubig = sb("ubig", [128, 2 * T])
        sig_s = ubig[:, 0:1024].rearrange("p (a b) -> p a b", a=2)
        sp_s = ubig[:, 1024:2048].rearrange("p (a b) -> p a b", a=2)
        cacc = [ubig[:, 0:T], ubig[:, T:2 * T]]
        crow = sb("crow", [128, T], BF16)
        w1t = sb("w1t", [128, 2, DD]); w2t = sb("w2t", [128, 2, DD])
        s1t = sb("s1t", [128, 2, DD]); s2t = sb("s2t", [128, 2, DD])
        w1b = sb("w1b", [128, 2, DD], BF16); w2b = sb("w2b", [128, 2, DD], BF16)
        s1b = sb("s1b", [128, 2, DD], BF16); s2b = sb("s2b", [128, 2, DD], BF16)
        w2pb = sb("w2pb", [128, 2, DD], BF16)
        kc_b = sb("kc_b", [128, 2, CHUNK], BF16)
        vc_b = sb("vc_b", [128, 2, CHUNK], BF16)
        qc_b = sb("qc_b", [128, 2, CHUNK], BF16)
        afm = sb("afm", [128, 2, CHUNK], BF16)
        dsfm = sb("dsfm", [128, 2, CHUNK])
        efm = sb("efm", [128, 2, CHUNK], BF16)
        ufm = sb("ufm", [128, 2, CHUNK], BF16)
        n2 = sb("n2", [128, 2, CHUNK], BF16)
        wthc = sb("wthc", [128, 4])
        gncol = sb("gncol", [128, 24])
        clarow = sb("clarow", [1, 128]); clerow = sb("clerow", [1, 128])
        lacol = sb("lacol", [128, 6])
        expin = sb("expin", [128, 128])
        m1t_f = sb("m1t_f", [128, 128]); m2t_f = sb("m2t_f", [128, 128])
        dt_sb = sb("dt_sb", [128, 128])
        acol = sb("acol", [128, 4])
        arow = sb("arow", [1, 128])
        lasts = sb("lasts", [128, 4])
        eyeA = sb("eyeA", [128, 128], BF16)
        eyeB = sb("eyeB", [128, 128], BF16)
        eyeE = sb("eyeE", [128, 128], BF16)
        ktm = sb("ktm", [128, DD], BF16)
        utm = sb("utm", [128, DD], BF16)
        etm = sb("etm", [128, DD], BF16)
        atm = sb("atm", [128, DD], BF16)
        kw_b = sb("kw_b", [128, DD], BF16)
        aw_b = sb("aw_b", [128, DD], BF16)
        kw2_b = sb("kw2_b", [128, DD], BF16)
        aw2_b = sb("aw2_b", [128, DD], BF16)
        hsb2 = sb("hsb2", [128, DD])
        onesc_f = sb("onesc_f", [128, 1])
        onesc_b = sb("onesc_b", [128, 1], BF16)
        negcle = sb("negcle", [1, 128])
        p1_b = sb("p1_b", [128, 128], BF16)
        p2_b = sb("p2_b", [128, 128], BF16)
        hsb = sb("hsb", [128, DD])
        hstm = sb("hstm", [128, DD], BF16)
        hsfm = sb("hsfm", [128, 2, 128], BF16)
        ott = sb("ott", [128, DD])
        retfm = sb("retfm", [128, 2, 128], BF16)
        ret_s = sb("ret_s", [128, C])
        gate_s = sb("gate_s", [128, C])
        g128 = sb("g128", [128, C])
        o3_s = sb("o3_s", [128, C])
        s_sb = sb("s_sb", [128, C])
        ynew = sb("ynew", [128, 2, C])
        dl_c = sb("dl_c", [128, 4])
        pA = ps("pA", [128, 512])
        pB = ps("pB", [128, 512])
        pC = ps("pC", [128, 512])
        pD = ps("pD", [128, 512])
        pE = ps("pE", [128, 512])
        pF2 = ps("pF2", [128, 512])
        pT = ps("pT3", [128, 1024], BF16)
        block = ctx.enter_context(nc.Block())

        @block.gpsimd
        def _(g):
            env.gd.mark(g.dma_start(out=wkvq_b[:, :, :],
                                    in_=wkvq_d.ap().rearrange("(f p) o -> p f o", p=128)))
            env.gd.mark(g.dma_start(out=waet_b[:, :, :],
                                    in_=waet_d.ap().rearrange("(f p) o -> p f o", p=128)))
            env.gd.mark(g.dma_start(out=wo_b[:, :, :],
                                    in_=wo_d.ap().rearrange("(f p) o -> p f o", p=128)))
            env.gd.mark(g.dma_start(out=gw_b[:, :, :],
                                    in_=gw_d.ap().rearrange("(f p) o -> p f o", p=128)))
            env.gd.mark(g.dma_start(out=gb_b[:, :], in_=gb_d[:, :]))
            env.gd.mark(g.dma_start(out=eye_b[:, :], in_=eye_d[:, :]))
            env.gd.mark(g.dma_start(out=w1t[:, :, :],
                                    in_=mw1_d.ap().rearrange("(c p) o -> p c o", p=128)))
            env.gd.mark(g.dma_start(out=w2t[:, :, :],
                                    in_=mw2_d.ap().rearrange("(c p) o -> p c o", p=128)))
        wload_g = env.gd.n

        @block.sync
        def _(q):
            env.qd.mark(q.dma_start(out=baet_s[:, :], in_=baet_d[:, :]))
            env.qd.mark(q.dma_start(out=dwc_s[:, :, :], in_=dwc_d[:, :, :]))
            env.qd.mark(q.dma_start(out=dwb_s[:, :], in_=dwb_d[:, :]))
            env.qd.mark(q.dma_start(out=eye_f[:, :], in_=eye_d[:, :]))
            env.qd.mark(q.dma_start(out=triu_f[:, :], in_=triu_d[:, :]))
            env.qd.mark(q.dma_start(out=tril_f[:, :], in_=tril_d[:, :]))
            env.qd.mark(q.dma_start(out=sel_f[:, :], in_=sel_d[:, :]))
            env.qd.mark(q.dma_start(out=ng_s[:, :], in_=ng_d[:, :]))
        wload_q = env.qd.n

        @block.vector
        def _(v):
            v.wait_ge(env.gd.sem, wload_g)
            v.wait_ge(env.qd.sem, wload_q)
            v.memset(epsl[:, :], float(LNEPS))
            v.memset(eps3c[:, :], float(EPS3))
            v.memset(onesr_b[:, :], 1.0)
            v.memset(onesr_f[:, :], 1.0)
            v.memset(onesc_f[:, :], 1.0)
            v.memset(onesc_b[:, :], 1.0)
            v.memset(s1t[:, :, :], 0.0)
            v.memset(s2t[:, :, :], 0.0)
            v.memset(s1b[:, :, :], 0.0)
            v.memset(s2b[:, :, :], 0.0)
            v.tensor_copy(w1b[:, :, :], w1t[:, :, :])
            inst = v.tensor_copy(w2b[:, :, :], w2t[:, :, :])
            env.ve.mark(v.drain())
        const_ve = env.ve.n

        @block.tensor
        def _(te):
            te.wait_ge(env.ve.sem, const_ve)
            ng_bf = junk_b
            te.wait_ge(env.ve.sem, const_ve)
            inst = te.matmul(pA[:, :], onesr_f[0:1, :], ng_s[0:1, 0:512], start=True, stop=True)
            inst = te.matmul(pB[:, :], onesr_f[0:1, :], ng_s[0:1, 512:1024], start=True, stop=True)
            env.te.mark(inst)
        ngb_tok = env.te.n

        @block.vector
        def _(v, ngb_tok=ngb_tok):
            v.wait_ge(env.te.sem, ngb_tok)
            v.tensor_copy(g128[:, 0:512], pA[:, :])
            inst = v.tensor_copy(g128[:, 512:1024], pB[:, :])
            env.ve.mark(v.drain())
        g128_tok = env.ve.n

        evac_tok = [0]
        pn_free = [None]
        for sp in range(NSP):
            t0 = sp * CHUNK

            @block.sync
            def _(q, t0=t0, sp=sp, ve_prev=env.ve.n):
                if sp > 0:
                    q.wait_ge(env.ve.sem, ve_prev)
                for tb in range(4):
                    env.qd.mark(q.dma_start(out=xres[:, tb, :],
                                            in_=x_d[t0 + tb * 128: t0 + (tb + 1) * 128, :]))
            loads_q = env.qd.n

            @block.vector
            def _(v, loads_q=loads_q, te_prev=env.te.n, cv=const_ve):
                v.wait_ge(env.qd.sem, loads_q)
                v.wait_ge(env.ve.sem, cv)
                if te_prev:
                    v.wait_ge(env.te.sem, te_prev)
                for tb in range(4):
                    inst = v.tensor_copy(xbf[:, tb, :], xres[:, tb, :])
                env.ve.mark(inst)
            cast_done = env.ve.n

            for tb in range(4):
                @block.tensor
                def _(te, tb=tb, cast_done=cast_done, ev=evac_tok[0]):
                    te.wait_ge(env.ve.sem, cast_done)
                    if ev:
                        te.wait_ge(env.ve.sem, ev)
                    for kf in range(8):
                        inst = te.transpose(pT[:, kf * 128:(kf + 1) * 128],
                                            xbf[:, tb, kf * 128:(kf + 1) * 128], eye_b[:, :])
                    env.te.mark(inst)
                tr_done = env.te.n

                @block.vector
                def _(v, tb=tb, tr_done=tr_done):
                    v.wait_ge(env.te.sem, tr_done)
                    src = pT[:, :].rearrange("p (f t) -> p f t", f=8)
                    inst = v.tensor_copy(xfm[:, :, tb * 128:(tb + 1) * 128], src)
                    env.ve.mark(inst)
                evac_tok[0] = env.ve.n
            fm_done = env.ve.n

            for oc in range(6):
                @block.tensor
                def _(te, oc=oc, fm_done=fm_done, pf=pn_free[0]):
                    te.wait_ge(env.ve.sem, fm_done)
                    if pf:
                        te.wait_ge(env.se.sem if pf[0] == "se" else env.ve.sem, pf[1])
                    for kf in range(8):
                        inst = te.matmul(pA[:, :], wkvq_b[:, kf, oc * 128:(oc + 1) * 128],
                                         xfm[:, kf, :], start=(kf == 0), stop=(kf == 7))
                    env.te.mark(inst)
                mm_tok = env.te.n

                @block.scalar
                def _(s, oc=oc, sp=sp, mm_tok=mm_tok, gprev=env.gd.n):
                    s.wait_ge(env.te.sem, mm_tok)
                    if gprev >= 16 * 10:
                        s.wait_ge(env.gd.sem, gprev - 16)
                    inst = s.activation(crow[:, sp * CHUNK:(sp + 1) * CHUNK], pA[:, :],
                                        AF.Identity)
                    env.se.mark(inst)
                pn_free[0] = ("se", env.se.n)
                ev_tok = env.se.n

                @block.gpsimd
                def _(g, oc=oc, sp=sp, ev_tok=ev_tok):
                    g.wait_ge(env.se.sem, ev_tok)
                    env.gd.mark(g.dma_start(out=praw_d.ap()[oc, :, sp * CHUNK:(sp + 1) * CHUNK],
                                            in_=crow[:, sp * CHUNK:(sp + 1) * CHUNK]))

            for pi in range(3):
                for h2 in range(2):
                    oc = pi * 2 + h2

                    @block.tensor
                    def _(te, oc=oc, fm_done=fm_done, pf=pn_free[0]):
                        te.wait_ge(env.ve.sem, fm_done)
                        if pf:
                            te.wait_ge(env.se.sem if pf[0] == "se" else env.ve.sem, pf[1])
                        for kf in range(8):
                            inst = te.matmul(pB[:, :], waet_b[:, kf, oc * 128:(oc + 1) * 128],
                                             xfm[:, kf, :], start=(kf == 0), stop=(kf == 7))
                        env.te.mark(inst)
                    mm_tok = env.te.n

                    @block.scalar
                    def _(s, pi=pi, oc=oc, h2=h2, mm_tok=mm_tok, vprev=env.ve.n):
                        s.wait_ge(env.te.sem, mm_tok)
                        if vprev:
                            s.wait_ge(env.ve.sem, vprev)
                        if pi < 2:
                            s.activation(sig_s[:, h2, :], pB[:, :], AF.Sigmoid,
                                         bias=baet_s[:, oc:oc + 1])
                        else:
                            s.activation(sp_s[:, h2, :], pB[:, :], AF.Abs,
                                         bias=baet_s[:, oc:oc + 1])
                            s.activation(sig_s[:, h2, :], pB[:, :], AF.Identity,
                                         bias=baet_s[:, oc:oc + 1])
                        env.se.mark(s.drain())
                    pn_free[0] = ("se", env.se.n)
                act_tok = env.se.n

                if pi == 2:
                    @block.scalar
                    def _(s, act_tok=act_tok):
                        s.activation(sp_s[:, 0, :], sp_s[:, 0, :], AF.Exp, scale=-1.0)
                        s.activation(sp_s[:, 1, :], sp_s[:, 1, :], AF.Exp, scale=-1.0)
                        s.drain()
                        s.activation(sp_s[:, 0, :], sp_s[:, 0, :], AF.Ln, bias=1.0)
                        s.activation(sp_s[:, 1, :], sp_s[:, 1, :], AF.Ln, bias=1.0)
                        env.se.mark(s.drain())
                    sp_tok = env.se.n

                    @block.vector
                    def _(v, sp_tok=sp_tok):
                        v.wait_ge(env.se.sem, sp_tok)
                        for h2 in range(2):
                            v.tensor_scalar(sig_s[:, h2, :], sig_s[:, h2, :], 0.0, None,
                                            ALU.max)
                        v.drain()
                        for h2 in range(2):
                            v.tensor_add(sig_s[:, h2, :], sig_s[:, h2, :], sp_s[:, h2, :])
                        env.ve.mark(v.drain())
                    act_tok = ("ve", env.ve.n)
                else:
                    act_tok = ("se", act_tok)

                @block.vector
                def _(v, act_tok=act_tok):
                    v.wait_ge(env.se.sem if act_tok[0] == "se" else env.ve.sem, act_tok[1])
                    v.tensor_copy(n2[:, 0, :], sig_s[:, 0, :])
                    inst = v.tensor_copy(n2[:, 1, :], sig_s[:, 1, :])
                    env.ve.mark(v.drain())
                cp_tok = env.ve.n

                @block.tensor
                def _(te, cp_tok=cp_tok, pi=pi):
                    te.wait_ge(env.ve.sem, cp_tok)
                    for blk in range(4):
                        for h2 in range(2):
                            inst = te.matmul(pC[:, pi * 4 + blk: pi * 4 + blk + 1],
                                             n2[:, h2, blk * 128:(blk + 1) * 128],
                                             onesc_b[:, 0:1],
                                             start=(h2 == 0), stop=(h2 == 1))
                        env.te.mark(inst)
            red_tok = env.te.n

            @block.vector
            def _(v, sp=sp, red_tok=red_tok):
                v.wait_ge(env.te.sem, red_tok)
                for blk in range(4):
                    bi = sp * 4 + blk
                    v.tensor_scalar(scal[:, bi, 0:1], pC[:, 8 + blk:9 + blk],
                                    0.1 / DD, None, ALU.mult)
                    v.tensor_scalar(scal[:, bi, 1:2], pC[:, 0 + blk:1 + blk],
                                    0.1 / DD, None, ALU.mult)
                    v.tensor_scalar(scal[:, bi, 2:3], pC[:, 4 + blk:5 + blk],
                                    0.9 / DD, None, ALU.mult)
                v.drain()
                for blk in range(4):
                    bi = sp * 4 + blk
                    inst = v.tensor_scalar(scal[:, bi, 2:3], scal[:, bi, 2:3],
                                           0.05, None, ALU.add)
                env.ve.mark(v.drain())
            mean_tok = env.ve.n

            @block.scalar
            def _(s, sp=sp, mean_tok=mean_tok):
                s.wait_ge(env.ve.sem, mean_tok)
                for blk in range(4):
                    bi = sp * 4 + blk
                    s.activation(scal[:, bi, 1:2], scal[:, bi, 1:2], AF.Ln,
                                 scale=-1.0, bias=1.0)
                    s.activation(scal[:, bi, 2:3], scal[:, bi, 2:3], AF.Ln)
                env.se.mark(s.drain())
            pn_free[0] = ("se", env.se.n)
        t1_se = env.se.n
        t1_gd = env.gd.n

        if S3_PHASES < 2:
            @block.sync
            def _(q):
                q.wait_ge(env.qd.sem, env.qd.n)
                q.wait_ge(env.gd.sem, env.gd.n)
            return nc
        conv_tok = [0]
        for oc in range(6):
            @block.sync
            def _(q, oc=oc, t1_gd=t1_gd, vp=conv_tok[0], gp=env.gd.n):
                if oc == 0:
                    q.wait_ge(env.gd.sem, t1_gd)
                if vp:
                    q.wait_ge(env.ve.sem, vp)
                q.wait_ge(env.gd.sem, gp)
                env.qd.mark(q.dma_start(out=crow[:, :], in_=praw_d.ap()[oc, :, :]))
            ld_tok = env.qd.n

            @block.vector
            def _(v, oc=oc, ld_tok=ld_tok, gprev=env.gd.n):
                v.wait_ge(env.qd.sem, ld_tok)
                if gprev > t1_gd:
                    v.wait_ge(env.gd.sem, gprev)
                v.tensor_scalar(cacc[0][:, :], crow[:, :], dwc_s[:, oc, 3:4], None, ALU.mult)
                v.drain()
                v.scalar_tensor_tensor(cacc[1][:, 1:T], crow[:, 0:T - 1],
                                       dwc_s[:, oc, 2:3], cacc[0][:, 1:T],
                                       ALU.mult, ALU.add)
                v.tensor_copy(cacc[1][:, 0:1], cacc[0][:, 0:1])
                v.drain()
                v.scalar_tensor_tensor(cacc[0][:, 2:T], crow[:, 0:T - 2],
                                       dwc_s[:, oc, 1:2], cacc[1][:, 2:T],
                                       ALU.mult, ALU.add)
                v.tensor_copy(cacc[0][:, 0:2], cacc[1][:, 0:2])
                v.drain()
                v.scalar_tensor_tensor(cacc[1][:, 3:T], crow[:, 0:T - 3],
                                       dwc_s[:, oc, 0:1], cacc[0][:, 3:T],
                                       ALU.mult, ALU.add)
                v.tensor_copy(cacc[1][:, 0:3], cacc[0][:, 0:3])
                v.drain()
                inst = v.tensor_scalar(crow[:, :], cacc[1][:, :], dwb_s[:, oc:oc + 1],
                                       None, ALU.add)
                env.ve.mark(v.drain())
            conv_tok[0] = env.ve.n

            @block.gpsimd
            def _(g, oc=oc, ct=conv_tok[0]):
                g.wait_ge(env.ve.sem, ct)
                env.gd.mark(g.dma_start(out=kdw_d.ap()[oc, :, :], in_=crow[:, :]))
        t2_gd = env.gd.n

        if S3_PHASES < 3:
            @block.sync
            def _(q):
                q.wait_ge(env.qd.sem, env.qd.n)
                q.wait_ge(env.gd.sem, env.gd.n)
            return nc
        psum_free = {}

        def mmseq(te, out, pairs, extra_wait=None):
            for i, (lh, rh) in enumerate(pairs):
                inst = te.matmul(out, lh, rh, start=(i == 0), stop=(i == len(pairs) - 1))
            return inst

        for cc in range(8):
            @block.sync
            def _(q, cc=cc, t2_gd=t2_gd, te_prev=env.te.n):
                if cc == 0:
                    q.wait_ge(env.gd.sem, t2_gd)
                else:
                    q.wait_ge(env.te.sem, te_prev)
                for dc in range(2):
                    env.qd.mark(q.dma_start(out=kc_b[:, dc, :],
                                            in_=kdw_d.ap()[0 + dc, :, cc * CHUNK:(cc + 1) * CHUNK]))
                    env.qd.mark(q.dma_start(out=vc_b[:, dc, :],
                                            in_=kdw_d.ap()[2 + dc, :, cc * CHUNK:(cc + 1) * CHUNK]))
                    env.qd.mark(q.dma_start(out=qc_b[:, dc, :],
                                            in_=kdw_d.ap()[4 + dc, :, cc * CHUNK:(cc + 1) * CHUNK]))
            kv_tok = env.qd.n

            for od in range(2):
                @block.tensor
                def _(te, od=od, cc=cc, kv_tok=kv_tok, vp=env.ve.n, sp_=env.se.n):
                    if od == 0:
                        te.wait_ge(env.qd.sem, kv_tok)
                        if vp:
                            te.wait_ge(env.ve.sem, vp)
                    if sp_:
                        te.wait_ge(env.se.sem, sp_)
                    inst = mmseq(te, pA[:, :],
                                 [(w1b[:, dc, od * 128:(od + 1) * 128], kc_b[:, dc, :])
                                  for dc in range(2)])
                    env.te.mark(inst)

                @block.scalar
                def _(s2, od=od, tok=env.te.n):
                    s2.wait_ge(env.te.sem, tok)
                    s2.activation(afm[:, od, :], pA[:, :], AF.Silu)
                    s2.activation(dsfm[:, od, :], pA[:, :], AF.Derivative_silu)
                    env.se.mark(s2.drain())
            a_tok = env.se.n

            for od in range(2):
                @block.tensor
                def _(te, od=od, a_tok=a_tok, vp=env.ve.n):
                    if od == 0:
                        te.wait_ge(env.se.sem, a_tok)
                    if vp:
                        te.wait_ge(env.ve.sem, vp)
                    inst = mmseq(te, pB[:, :],
                                 [(w2b[:, oc2, od * 128:(od + 1) * 128], afm[:, oc2, :])
                                  for oc2 in range(2)])
                    env.te.mark(inst)

                @block.vector
                def _(v2, od=od, tok=env.te.n):
                    v2.wait_ge(env.te.sem, tok)
                    inst2 = v2.tensor_sub(efm[:, od, :], pB[:, :], vc_b[:, od, :])
                    env.ve.mark(v2.drain())
            e_tok = env.ve.n

            @block.tensor
            def _(te, e_tok=e_tok):
                te.wait_ge(env.ve.sem, e_tok)
                for pc in range(2):
                    for oc2 in range(2):
                        inst = te.transpose(pT[:, (pc * 2 + oc2) * 128:(pc * 2 + oc2 + 1) * 128],
                                            w2b[:, oc2, pc * 128:(pc + 1) * 128], eye_b[:, :])
                env.te.mark(inst)
            w2p_tok = env.te.n

            @block.vector
            def _(v, w2p_tok=w2p_tok):
                v.wait_ge(env.te.sem, w2p_tok)
                src = pT[:, :].rearrange("p (a b) -> p a b", a=8)
                v.tensor_copy(w2pb[:, 0, 0:128], src[:, 0, :])
                v.tensor_copy(w2pb[:, 0, 128:256], src[:, 1, :])
                v.tensor_copy(w2pb[:, 1, 0:128], src[:, 2, :])
                inst = v.tensor_copy(w2pb[:, 1, 128:256], src[:, 3, :])
                env.ve.mark(v.drain())
            w2pb_tok = env.ve.n

            for od in range(2):
                @block.tensor
                def _(te, od=od, w2pb_tok=w2pb_tok, vp=env.ve.n):
                    if od == 0:
                        te.wait_ge(env.ve.sem, w2pb_tok)
                    else:
                        te.wait_ge(env.ve.sem, vp)
                    inst = mmseq(te, pC[:, :],
                                 [(w2pb[:, pc, od * 128:(od + 1) * 128], efm[:, pc, :])
                                  for pc in range(2)])
                    env.te.mark(inst)

                @block.vector
                def _(v2, od=od, tok=env.te.n):
                    v2.wait_ge(env.te.sem, tok)
                    inst2 = v2.tensor_mul(ufm[:, od, :], pC[:, :], dsfm[:, od, :])
                    env.ve.mark(v2.drain())
            u_tok = env.ve.n

            @block.scalar
            def _(s, u_tok=u_tok):
                s.wait_ge(env.ve.sem, u_tok)
                for od in range(2):
                    s.activation(n2[:, od, :], ufm[:, od, :], AF.Square)
                env.se.mark(s.drain())
            nu_tok = env.se.n

            def norm_reduce(src_tag, dst_col, wait_tok):
                @block.tensor
                def _(te, dst_col=dst_col, wait_tok=wait_tok):
                    te.wait_ge(env.se.sem, wait_tok)
                    for blk in range(4):
                        for od in range(2):
                            inst = te.matmul(pD[:, dst_col * 4 + blk: dst_col * 4 + blk + 1],
                                             n2[:, od, blk * 128:(blk + 1) * 128],
                                             onesc_b[:, 0:1], start=(od == 0), stop=(od == 1))
                    env.te.mark(inst)
                return env.te.n

            nmm = norm_reduce("u", 0, nu_tok)

            @block.scalar
            def _(s, nmm=nmm):
                s.wait_ge(env.te.sem, nmm)
                for od in range(2):
                    s.activation(n2[:, od, :], kc_b[:, od, :], AF.Square)
                env.se.mark(s.drain())
            nmm = norm_reduce("k", 1, env.se.n)

            @block.scalar
            def _(s, nmm=nmm):
                s.wait_ge(env.te.sem, nmm)
                for od in range(2):
                    s.activation(n2[:, od, :], efm[:, od, :], AF.Square)
                env.se.mark(s.drain())
            nmm = norm_reduce("e", 2, env.se.n)

            @block.scalar
            def _(s, nmm=nmm):
                s.wait_ge(env.te.sem, nmm)
                for od in range(2):
                    s.activation(n2[:, od, :], afm[:, od, :], AF.Square)
                env.se.mark(s.drain())
            nmm = norm_reduce("a", 3, env.se.n)

            @block.vector
            def _(v, cc=cc, nmm=nmm):
                v.wait_ge(env.te.sem, nmm)
                v.tensor_copy(gncol[:, 0:16], pD[:, 0:16])
                v.drain()
                v.tensor_mul(gncol[:, 16:20], gncol[:, 0:4], gncol[:, 4:8])
                v.tensor_mul(gncol[:, 20:24], gncol[:, 8:12], gncol[:, 12:16])
                v.drain()
                inst = v.tensor_add(gncol[:, 3:7], gncol[:, 16:20], gncol[:, 20:24])
                env.ve.mark(v.drain())
            gn_tok = env.ve.n

            @block.scalar
            def _(s, gn_tok=gn_tok):
                s.wait_ge(env.ve.sem, gn_tok)
                inst = s.activation(gncol[:, 3:7], gncol[:, 3:7], AF.Sqrt)
                env.se.mark(inst)
            sq2_tok = env.se.n

            @block.vector
            def _(v, cc=cc, sq2_tok=sq2_tok):
                v.wait_ge(env.se.sem, sq2_tok)
                v.tensor_scalar(gncol[:, 3:7], gncol[:, 3:7], 2.0 * GS3, None, ALU.mult)
                v.drain()
                v.reciprocal(gncol[:, 3:7], gncol[:, 3:7])
                v.drain()
                v.tensor_scalar(gncol[:, 3:7], gncol[:, 3:7], 1.0, None, ALU.min)
                v.drain()
                v.tensor_scalar(gncol[:, 3:7], gncol[:, 3:7], 2.0 * GS3, None, ALU.mult)
                v.drain()
                for blk in range(4):
                    inst = v.tensor_mul(wthc[:, blk:blk + 1], gncol[:, 3 + blk:4 + blk],
                                        scal[:, cc * 4 + blk, 0:1])
                env.ve.mark(v.drain())
            wthc_tok = env.ve.n

            for m in range(4):
                bi = cc * 4 + m

                @block.tensor
                def _(te, bi=bi, wthc_tok=wthc_tok, vp=env.ve.n):
                    te.wait_ge(env.ve.sem, max(wthc_tok, vp))
                    inst = te.matmul(pE[:, 0:2], triu_f[:, :], scal[:, bi, 1:3],
                                     start=True, stop=True)
                    env.te.mark(inst)
                cum_tok = env.te.n

                @block.vector
                def _(v, cum_tok=cum_tok):
                    v.wait_ge(env.te.sem, cum_tok)
                    inst = v.tensor_copy(lacol[:, 0:2], pE[:, 0:2])
                    env.ve.mark(v.drain())
                cumc_tok = env.ve.n

                @block.tensor
                def _(te, cumc_tok=cumc_tok):
                    te.wait_ge(env.ve.sem, cumc_tok)
                    inst = te.transpose(pE[0:2, 0:128] if False else pE[0:1, 0:128],
                                        lacol[:, 0:1], eye_f[:, :])
                    inst = te.transpose(pE[0:1, 128:256], lacol[:, 1:2], eye_f[:, :])
                    env.te.mark(inst)
                rowt_tok = env.te.n

                @block.vector
                def _(v, rowt_tok=rowt_tok):
                    v.wait_ge(env.te.sem, rowt_tok)
                    v.tensor_copy(clarow[:, :], pE[0:1, 0:128])
                    inst = v.tensor_copy(clerow[:, :], pE[0:1, 128:256])
                    v.drain()
                    v.tensor_sub(lacol[:, 2:3], lacol[:, 1:2], lacol[:, 0:1])
                    v.tensor_scalar(lacol[:, 3:4], lacol[:, 0:1], -1.0, None, ALU.mult)
                    v.tensor_scalar(lacol[:, 4:5], lacol[:, 1:2], -1.0, None, ALU.mult)
                    v.tensor_scalar(negcle[:, :], clerow[:, :], -1.0, None, ALU.mult)
                    env.ve.mark(v.drain())
                rows_tok = env.ve.n

                @block.tensor
                def _(te, rows_tok=rows_tok):
                    te.wait_ge(env.ve.sem, rows_tok)
                    te.matmul(pE[:, 0:128], onesr_f[0:1, :], clarow[0:1, :],
                              start=True, stop=True)
                    inst = te.matmul(pE[:, 128:256], onesr_f[0:1, :], negcle[0:1, :],
                                     start=True, stop=True)
                    env.te.mark(inst)
                bc_tok = env.te.n

                @block.scalar
                def _(s, bc_tok=bc_tok):
                    s.wait_ge(env.te.sem, bc_tok)
                    s.activation(m1t_f[:, :], pE[:, 0:128], AF.Exp,
                                 bias=lacol[:, 3:4], scale=1.0)
                    s.activation(m2t_f[:, :], pE[:, 128:256], AF.Exp,
                                 bias=lacol[:, 1:2], scale=1.0)
                    s.activation(expin[:, :], pE[:, 0:128], AF.Exp,
                                 bias=lacol[:, 2:3], scale=1.0)
                    s.activation(acol[:, 0:1], lacol[:, 0:1], AF.Exp)
                    s.activation(acol[:, 2:3], lacol[:, 1:2], AF.Exp)
                    env.se.mark(s.drain())
                exp_tok = env.se.n

                @block.vector
                def _(v, exp_tok=exp_tok):
                    v.wait_ge(env.se.sem, exp_tok)
                    v.tensor_mul(m1t_f[:, :], m1t_f[:, :], triu_f[:, :])
                    v.tensor_mul(expin[:, :], expin[:, :], triu_f[:, :])
                    env.ve.mark(v.drain())
                msk1_tok = env.ve.n

                @block.vector
                def _(v, msk1_tok=msk1_tok):
                    inst = v.tensor_mul(m2t_f[:, :], m2t_f[:, :], tril_f[:, :])
                    env.ve.mark(v.drain())
                msk_tok = env.ve.n

                @block.tensor
                def _(te, msk_tok=msk_tok):
                    te.wait_ge(env.ve.sem, msk_tok)
                    te.matmul(pF2[:, 0:128], m2t_f[:, :], m1t_f[:, :], start=True, stop=True)
                    inst = te.matmul(pF2[:, 128:129], expin[:, :], onesc_f[:, 0:1],
                                     start=True, stop=True)
                    env.te.mark(inst)
                dt_tok = env.te.n

                @block.vector
                def _(v, dt_tok=dt_tok):
                    v.wait_ge(env.te.sem, dt_tok)
                    v.tensor_copy(dt_sb[:, :], pF2[:, 0:128])
                    inst = v.tensor_copy(acol[:, 1:2], pF2[:, 128:129])
                    env.ve.mark(v.drain())
                dt_sb_tok = env.ve.n

                @block.vector
                def _(v, dt_sb_tok=dt_sb_tok):
                    v.wait_ge(env.ve.sem, dt_sb_tok)
                    inst = v.tensor_copy(acol[:, 3:4], lacol[:, 1:2])
                    env.ve.mark(v.drain())
                a4_tok = env.ve.n

                @block.tensor
                def _(te, a4_tok=a4_tok):
                    te.wait_ge(env.ve.sem, a4_tok)
                    inst = te.matmul(pE[0:1, 300:304], sel_f[:, 0:1], acol[:, 0:4],
                                     start=True, stop=True)
                    env.te.mark(inst)
                l1_tok = env.te.n

                @block.vector
                def _(v, l1_tok=l1_tok):
                    v.wait_ge(env.te.sem, l1_tok)
                    inst = v.tensor_copy(arow[0:1, 0:4], pE[0:1, 300:304])
                    env.ve.mark(v.drain())
                l2_tok = env.ve.n

                @block.tensor
                def _(te, l2_tok=l2_tok):
                    te.wait_ge(env.ve.sem, l2_tok)
                    inst = te.matmul(pE[:, 300:304], onesr_f[0:1, :], arow[0:1, 0:4],
                                     start=True, stop=True)
                    env.te.mark(inst)
                l3_tok = env.te.n

                @block.vector
                def _(v, bi=bi, l3_tok=l3_tok):
                    v.wait_ge(env.te.sem, l3_tok)
                    v.tensor_copy(lasts[:, 0:4], pE[:, 300:304])
                    v.drain()
                    v.tensor_mul(dl_c[:, 0:1], dt_sb[:, 127:128], wthc[:, bi % 4:bi % 4 + 1])
                    v.drain()
                    inst = v.tensor_scalar(dl_c[:, 0:1], dl_c[:, 0:1], -1.0, None, ALU.mult)
                    env.ve.mark(v.drain())
                la_tok = env.ve.n

                @block.scalar
                def _(s, la_tok=la_tok):
                    s.wait_ge(env.ve.sem, la_tok)
                    inst = s.activation(dl_c[:, 1:2], lacol[:, 4:5], AF.Exp,
                                        bias=lasts[:, 3:4], scale=1.0)
                    env.se.mark(inst)
                ds_tok = env.se.n

                @block.vector
                def _(v, bi=bi, ds_tok=ds_tok):
                    v.wait_ge(env.se.sem, ds_tok)
                    v.tensor_mul(dl_c[:, 1:2], dl_c[:, 1:2], wthc[:, bi % 4:bi % 4 + 1])
                    v.drain()
                    v.tensor_scalar(dl_c[:, 1:2], dl_c[:, 1:2], -1.0, None, ALU.mult)
                    v.tensor_scalar(eyeA[:, :], eye_f[:, :], lasts[:, 0:1], None, ALU.mult)
                    v.tensor_scalar(eyeB[:, :], eye_f[:, :], lasts[:, 1:2], None, ALU.mult)
                    v.tensor_scalar(eyeE[:, :], eye_f[:, :], lasts[:, 2:3], None, ALU.mult)
                    env.ve.mark(v.drain())
                eyes_tok = env.ve.n

                @block.tensor
                def _(te, m=m, eyes_tok=eyes_tok):
                    te.wait_ge(env.ve.sem, eyes_tok)
                    for mi, (fmb, tmb) in enumerate(((kc_b, ktm), (ufm, utm),
                                                     (efm, etm), (afm, atm))):
                        for dc in range(2):
                            inst = te.transpose(
                                pT[:, (mi * 2 + dc) * 128:(mi * 2 + dc + 1) * 128],
                                fmb[:, dc, m * 128:(m + 1) * 128], eye_b[:, :])
                    env.te.mark(inst)
                trs_tok = env.te.n

                @block.vector
                def _(v, trs_tok=trs_tok):
                    v.wait_ge(env.te.sem, trs_tok)
                    src = pT[:, :].rearrange("p (a b) -> p a b", a=8)
                    for mi, tmb in enumerate((ktm, utm, etm, atm)):
                        for dc in range(2):
                            inst = v.tensor_copy(tmb[:, dc * 128:(dc + 1) * 128],
                                                 src[:, mi * 2 + dc, :])
                    env.ve.mark(v.drain())
                tm_tok = env.ve.n

                @block.tensor
                def _(te, m=m, tm_tok=tm_tok):
                    te.wait_ge(env.ve.sem, tm_tok)
                    mmseq(te, pA[:, 0:256], [(qc_b[:, dc, m * 128:(m + 1) * 128],
                                              w1b[:, dc, :]) for dc in range(2)])
                    mmseq(te, pB[:, 0:256], [(qc_b[:, dc, m * 128:(m + 1) * 128],
                                              s1b[:, dc, :]) for dc in range(2)])
                    inst = mmseq(te, pC[:, 0:128],
                                 [(kc_b[:, dc, m * 128:(m + 1) * 128],
                                   qc_b[:, dc, m * 128:(m + 1) * 128]) for dc in range(2)])
                    env.te.mark(inst)
                app1_tok = env.te.n

                @block.vector
                def _(v, m=m, bi=bi, app1_tok=app1_tok):
                    v.wait_ge(env.te.sem, app1_tok)
                    v.tensor_mul(hsb2[:, 0:128], pC[:, 0:128], dt_sb[:, :])
                    v.drain()
                    v.tensor_scalar(p1_b[:, :], hsb2[:, 0:128], wthc[:, bi % 4:bi % 4 + 1],
                                    None, ALU.mult)
                    env.ve.mark(v.drain())
                p1_tok = env.ve.n

                @block.tensor
                def _(te, p1_tok=p1_tok):
                    te.wait_ge(env.ve.sem, p1_tok)
                    inst = te.matmul(pD[:, 0:256], p1_b[:, :], utm[:, :],
                                     start=True, stop=True)
                    env.te.mark(inst)
                ct1_tok = env.te.n

                @block.vector
                def _(v, ct1_tok=ct1_tok):
                    v.wait_ge(env.te.sem, ct1_tok)
                    v.tensor_scalar(hsb[:, :], pA[:, 0:256], acol[:, 0:1], None, ALU.mult)
                    v.drain()
                    v.scalar_tensor_tensor(hsb2[:, :], pB[:, 0:256], acol[:, 1:2],
                                           hsb[:, :], ALU.mult, ALU.add)
                    v.drain()
                    inst = v.tensor_sub(hsb[:, :], hsb2[:, :], pD[:, 0:256])
                    env.ve.mark(v.drain())
                hf_tok = env.ve.n

                @block.scalar
                def _(s, hf_tok=hf_tok):
                    s.wait_ge(env.ve.sem, hf_tok)
                    inst = s.activation(hstm[:, :], hsb[:, :], AF.Silu)
                    env.se.mark(inst)
                hs_tok = env.se.n

                @block.tensor
                def _(te, hs_tok=hs_tok):
                    te.wait_ge(env.se.sem, hs_tok)
                    for oc2 in range(2):
                        inst = te.transpose(pT[:, oc2 * 128:(oc2 + 1) * 128],
                                            hstm[:, oc2 * 128:(oc2 + 1) * 128], eye_b[:, :])
                    env.te.mark(inst)
                hst_tok = env.te.n

                @block.vector
                def _(v, hst_tok=hst_tok):
                    v.wait_ge(env.te.sem, hst_tok)
                    src = pT[:, :].rearrange("p (a b) -> p a b", a=8)
                    v.tensor_copy(hsfm[:, 0, :], src[:, 0, :])
                    inst = v.tensor_copy(hsfm[:, 1, :], src[:, 1, :])
                    env.ve.mark(v.drain())
                hsf_tok = env.ve.n

                @block.tensor
                def _(te, m=m, hsf_tok=hsf_tok):
                    te.wait_ge(env.ve.sem, hsf_tok)
                    mmseq(te, pA[:, 0:256], [(hsfm[:, oc2, :], w2b[:, oc2, :])
                                             for oc2 in range(2)])
                    mmseq(te, pB[:, 0:256], [(hsfm[:, oc2, :], s2b[:, oc2, :])
                                             for oc2 in range(2)])
                    inst = mmseq(te, pC[:, 0:128],
                                 [(afm[:, oc2, m * 128:(m + 1) * 128], hsfm[:, oc2, :])
                                  for oc2 in range(2)])
                    env.te.mark(inst)
                app2_tok = env.te.n

                @block.vector
                def _(v, bi=bi, app2_tok=app2_tok):
                    v.wait_ge(env.te.sem, app2_tok)
                    v.tensor_mul(hsb2[:, 0:128], pC[:, 0:128], dt_sb[:, :])
                    v.drain()
                    v.tensor_scalar(p2_b[:, :], hsb2[:, 0:128], wthc[:, bi % 4:bi % 4 + 1],
                                    None, ALU.mult)
                    env.ve.mark(v.drain())
                p2_tok = env.ve.n

                @block.tensor
                def _(te, p2_tok=p2_tok):
                    te.wait_ge(env.ve.sem, p2_tok)
                    inst = te.matmul(pD[:, 0:256], p2_b[:, :], etm[:, :],
                                     start=True, stop=True)
                    env.te.mark(inst)
                ct2_tok = env.te.n

                @block.vector
                def _(v, ct2_tok=ct2_tok):
                    v.wait_ge(env.te.sem, ct2_tok)
                    v.tensor_scalar(ott[:, :], pA[:, 0:256], acol[:, 0:1], None, ALU.mult)
                    v.drain()
                    v.scalar_tensor_tensor(hsb2[:, :], pB[:, 0:256], acol[:, 1:2],
                                           ott[:, :], ALU.mult, ALU.add)
                    v.drain()
                    inst = v.tensor_sub(ott[:, :], hsb2[:, :], pD[:, 0:256])
                    env.ve.mark(v.drain())
                ot_tok = env.ve.n

                @block.gpsimd
                def _(g, bi=bi, ot_tok=ot_tok):
                    g.wait_ge(env.ve.sem, ot_tok)
                    env.gd.mark(g.dma_start(out=retr_d[bi * 128:(bi + 1) * 128, :],
                                            in_=ott[:, :]))
                ret_st = env.gd.n

                @block.vector
                def _(v, tm_tok=tm_tok, ot_tok=ot_tok):
                    v.wait_ge(env.ve.sem, ot_tok)
                    v.tensor_scalar(kw_b[:, :], ktm[:, :], dl_c[:, 0:1], None, ALU.mult)
                    v.tensor_scalar(aw_b[:, :], atm[:, :], dl_c[:, 0:1], None, ALU.mult)
                    v.tensor_scalar(kw2_b[:, :], ktm[:, :], dl_c[:, 1:2], None, ALU.mult)
                    inst = v.tensor_scalar(aw2_b[:, :], atm[:, :], dl_c[:, 1:2],
                                           None, ALU.mult)
                    env.ve.mark(v.drain())
                kw_tok = env.ve.n

                @block.tensor
                def _(te, kw_tok=kw_tok):
                    te.wait_ge(env.ve.sem, kw_tok)
                    for dc in range(2):
                        inst = mmseq(te, (pA if dc == 0 else pB)[:, 0:256],
                                     [(eyeA[:, :], w1b[:, dc, :]),
                                      (eyeB[:, :], s1b[:, dc, :]),
                                      (kw_b[:, dc * 128:(dc + 1) * 128], utm[:, :])])
                    for oc2 in range(2):
                        inst = mmseq(te, (pC if oc2 == 0 else pD)[:, 0:256],
                                     [(eyeA[:, :], w2b[:, oc2, :]),
                                      (eyeB[:, :], s2b[:, oc2, :]),
                                      (aw_b[:, oc2 * 128:(oc2 + 1) * 128], etm[:, :])])
                    env.te.mark(inst)
                wup_tok = env.te.n

                @block.vector
                def _(v, wup_tok=wup_tok):
                    v.wait_ge(env.te.sem, wup_tok)
                    for dc, pp in ((0, pA), (1, pB)):
                        v.tensor_copy(w1t[:, dc, :], pp[:, 0:256])
                    for oc2, pp in ((0, pC), (1, pD)):
                        inst = v.tensor_copy(w2t[:, oc2, :], pp[:, 0:256])
                    env.ve.mark(v.drain())
                wev_tok = env.ve.n

                @block.tensor
                def _(te, wev_tok=wev_tok):
                    te.wait_ge(env.ve.sem, wev_tok)
                    for dc in range(2):
                        inst = mmseq(te, (pA if dc == 0 else pB)[:, 0:256],
                                     [(eyeE[:, :], s1b[:, dc, :]),
                                      (kw2_b[:, dc * 128:(dc + 1) * 128], utm[:, :])])
                    for oc2 in range(2):
                        inst = mmseq(te, (pC if oc2 == 0 else pD)[:, 0:256],
                                     [(eyeE[:, :], s2b[:, oc2, :]),
                                      (aw2_b[:, oc2 * 128:(oc2 + 1) * 128], etm[:, :])])
                    env.te.mark(inst)
                sup_tok = env.te.n

                @block.vector
                def _(v, sup_tok=sup_tok):
                    v.wait_ge(env.te.sem, sup_tok)
                    for dc, pp in ((0, pA), (1, pB)):
                        v.tensor_copy(s1t[:, dc, :], pp[:, 0:256])
                    for oc2, pp in ((0, pC), (1, pD)):
                        v.tensor_copy(s2t[:, oc2, :], pp[:, 0:256])
                    v.drain()
                    v.tensor_copy(w1b[:, :, :], w1t[:, :, :])
                    v.tensor_copy(w2b[:, :, :], w2t[:, :, :])
                    v.tensor_copy(s1b[:, :, :], s1t[:, :, :])
                    inst = v.tensor_copy(s2b[:, :, :], s2t[:, :, :])
                    env.ve.mark(v.drain())
                st_tok2 = env.ve.n

                if S3_DEBUG and cc == 0 and m == 0:
                    @block.gpsimd
                    def _(g, st_tok2=st_tok2):
                        g.wait_ge(env.ve.sem, st_tok2)
                        env.gd.mark(g.dma_start(out=dbgst["w1"].ap(), in_=w1t[:, :, :]))
                        env.gd.mark(g.dma_start(out=dbgst["w2"].ap(), in_=w2t[:, :, :]))
                        env.gd.mark(g.dma_start(out=dbgst["s1"].ap(), in_=s1t[:, :, :]))
                        env.gd.mark(g.dma_start(out=dbgst["s2"].ap(), in_=s2t[:, :, :]))
                        env.gd.mark(g.dma_start(out=dbgco.ap()[:, 0:128], in_=dt_sb[:, :]))
                        env.gd.mark(g.dma_start(out=dbgco.ap()[:, 128:132], in_=acol[:, :]))
                        env.gd.mark(g.dma_start(out=dbgco.ap()[:, 132:136], in_=lasts[:, :]))
                        env.gd.mark(g.dma_start(out=dbgco.ap()[:, 136:140], in_=dl_c[:, :]))
                        env.gd.mark(g.dma_start(out=dbgco.ap()[:, 140:144], in_=wthc[:, :]))
                        env.gd.mark(g.dma_start(out=dbgco.ap()[:, 144:150], in_=lacol[:, :]))
                    @block.vector
                    def _(v):
                        v.wait_ge(env.gd.sem, env.gd.n)
                        env.ve.mark(v.engine_nop())
        chain_ve = env.ve.n
        chain_gd = env.gd.n

        if S3_PHASES < 4:
            @block.sync
            def _(q):
                q.wait_ge(env.qd.sem, env.qd.n)
                q.wait_ge(env.gd.sem, env.gd.n)
            return nc
        @block.sync
        def _(q, chain_ve=chain_ve, chain_gd=chain_gd):
            q.wait_ge(env.ve.sem, chain_ve)
            q.wait_ge(env.gd.sem, chain_gd)

        rtm = ret_s
        for spl in range(8):
            for tb in range(4):
                bil = spl * 4 + tb

                @block.sync
                def _(q, spl=spl, tb=tb, ve_prev=env.ve.n, se_prev=env.se.n):
                    if ve_prev:
                        q.wait_ge(env.ve.sem, ve_prev)
                    if se_prev:
                        q.wait_ge(env.se.sem, se_prev)
                    bi2 = spl * 4 + tb
                    goff = spl * CHUNK + tb * 128
                    env.qd.mark(q.dma_start(out=xres[:, 0, :],
                                            in_=x_d[goff: goff + 128, :]))
                    env.qd.mark(q.dma_start(
                        out=ott[:, :], in_=retr_d[bi2 * 128:(bi2 + 1) * 128, :]))
                ld_tok = env.qd.n

                @block.vector
                def _(v, ld_tok=ld_tok, te_prev=env.te.n):
                    v.wait_ge(env.qd.sem, ld_tok)
                    if te_prev:
                        v.wait_ge(env.te.sem, te_prev)
                    v.tensor_copy(xbf[:, 0, :], xres[:, 0, :])
                    inst = v.tensor_copy(junk_b[:, 0:DD], ott[:, :])
                    env.ve.mark(v.drain())
                cast_tok = env.ve.n

                @block.tensor
                def _(te, cast_tok=cast_tok):
                    te.wait_ge(env.ve.sem, cast_tok)
                    for kf in range(8):
                        inst = te.transpose(pT[:, kf * 128:(kf + 1) * 128],
                                            xbf[:, 0, kf * 128:(kf + 1) * 128], eye_b[:, :])
                    env.te.mark(inst)
                xtr_tok = env.te.n

                @block.vector
                def _(v, xtr_tok=xtr_tok):
                    v.wait_ge(env.te.sem, xtr_tok)
                    src = pT[:, :].rearrange("p (f t) -> p f t", f=8)
                    inst = v.tensor_copy(xfm[:, :, 0:128], src)
                    env.ve.mark(v.drain())
                xfm_tok = env.ve.n

                @block.tensor
                def _(te, xfm_tok=xfm_tok):
                    te.wait_ge(env.ve.sem, xfm_tok)
                    for dc in range(2):
                        inst = te.transpose(pT[:, dc * 128:(dc + 1) * 128],
                                            junk_b[:, dc * 128:(dc + 1) * 128], eye_b[:, :])
                    env.te.mark(inst)
                rtr_tok = env.te.n

                @block.vector
                def _(v, rtr_tok=rtr_tok):
                    v.wait_ge(env.te.sem, rtr_tok)
                    src = pT[:, :].rearrange("p (f t) -> p f t", f=8)
                    v.tensor_copy(retfm[:, 0, :], src[:, 0, :])
                    inst = v.tensor_copy(retfm[:, 1, :], src[:, 1, :])
                    env.ve.mark(v.drain())
                rfm_tok = env.ve.n

                @block.tensor
                def _(te, rfm_tok=rfm_tok):
                    te.wait_ge(env.ve.sem, rfm_tok)
                    for oh, pp in ((0, pA), (1, pB)):
                        inst = mmseq(te, pp[:, :],
                                     [(retfm[:, dc, :], wo_b[:, dc, oh * 512:(oh + 1) * 512])
                                      for dc in range(2)])
                    for oh, pp in ((0, pC), (1, pD)):
                        pairs = [(xfm[:, kf, 0:128], gw_b[:, kf, oh * 512:(oh + 1) * 512])
                                 for kf in range(8)]
                        pairs.append((onesr_b[0:1, :], gb_b[0:1, oh * 512:(oh + 1) * 512]))
                        inst = mmseq(te, pp[:, :], pairs)
                    env.te.mark(inst)
                wog_tok = env.te.n

                @block.scalar
                def _(s, wog_tok=wog_tok):
                    s.wait_ge(env.te.sem, wog_tok)
                    s.activation(ret_s[:, 0:512], pA[:, :], AF.Identity)
                    s.activation(ret_s[:, 512:1024], pB[:, :], AF.Identity)
                    s.drain()
                    s.activation(junk_b[:, :], ret_s[:, 0:1024], AF.Square,
                                 accum_out=st[:, 0:1])
                    env.se.mark(s.drain())
                ms_tok = env.se.n

                @block.vector
                def _(v, ms_tok=ms_tok):
                    v.wait_ge(env.se.sem, ms_tok)
                    inst = v.tensor_scalar(st[:, 2:3], st[:, 0:1], 1.0 / C, None, ALU.mult)
                    env.ve.mark(v.drain())
                ms2_tok = env.ve.n

                @block.scalar
                def _(s, ms2_tok=ms2_tok):
                    s.wait_ge(env.ve.sem, ms2_tok)
                    inst = s.activation(st[:, 3:4], st[:, 2:3], AF.Sqrt, bias=eps3c[:, 0:1])
                    env.se.mark(inst)
                sd3_tok = env.se.n

                @block.vector
                def _(v, sd3_tok=sd3_tok):
                    v.wait_ge(env.se.sem, sd3_tok)
                    v.reciprocal(st[:, 4:5], st[:, 3:4])
                    env.ve.mark(v.drain())
                rc_tok = env.ve.n

                @block.scalar
                def _(s, rc_tok=rc_tok):
                    s.wait_ge(env.ve.sem, rc_tok)
                    inst = s.activation(o3_s[:, :], ret_s[:, 0:1024], AF.Identity,
                                        scale=st[:, 4:5])
                    env.se.mark(inst)
                o1_tok = env.se.n

                @block.vector
                def _(v, o1_tok=o1_tok, g128_tok=g128_tok):
                    v.wait_ge(env.se.sem, o1_tok)
                    v.tensor_mul(o3_s[:, :], o3_s[:, :], g128[:, :])
                    v.drain()
                    v.tensor_mul(o3_s[:, 0:512], o3_s[:, 0:512], pC[:, :])
                    v.tensor_mul(o3_s[:, 512:1024], o3_s[:, 512:1024], pD[:, :])
                    v.drain()
                    v.scalar_tensor_tensor(s_sb[:, :], o3_s[:, :], 1.0, xres[:, 0, :],
                                           ALU.mult, ALU.add, accum_out=st[:, 0:1])
                    env.ve.mark(v.drain())
                sum3_tok = env.ve.n

                @block.scalar
                def _(s, sum3_tok=sum3_tok):
                    s.wait_ge(env.ve.sem, sum3_tok)
                    s.activation(junk_b[:, :], s_sb[:, :], AF.Square, accum_out=st[:, 1:2])
                    env.se.mark(s.drain())
                sq3_tok = env.se.n

                y_done = emit_ln_stats_apply(env, block, s_sb[:, :], st,
                                             ynew[:, bil % 2, :], epsl[:, 0:1],
                                             sq3_tok, "se")

                @block.sync
                def _(q, spl=spl, tb=tb, bil=bil, y_done=y_done):
                    q.wait_ge(env.se.sem, y_done)
                    env.qd.mark(q.dma_start(
                        out=y_d[spl * CHUNK + tb * 128: spl * CHUNK + (tb + 1) * 128, :],
                        in_=ynew[:, bil % 2, :]))
                yst3 = env.qd.n

                @block.scalar
                def _(s, yst3=yst3):
                    if yst3 >= 32:
                        s.wait_ge(env.qd.sem, yst3 - 16)

        @block.sync
        def _(q):
            q.wait_ge(env.qd.sem, env.qd.n)
            q.wait_ge(env.gd.sem, env.gd.n)
    return nc


_S3_NC = None


def run_s3(x_full, Wk, Wv, Wq, Wo, ckw, ckb, cvw, cvb, cqw, cqb, mW1, mW2,
           aW, ab, eW, eb, tW, tb_, norm_g, gW, gb):
    global _S3_NC
    if _S3_NC is None:
        _S3_NC = build_s3()
    eye = np.eye(128, dtype=np.float32)
    rr = np.arange(128)
    triu = (rr[:, None] <= rr[None, :]).astype(np.float32)
    tril = (rr[:, None] >= rr[None, :]).astype(np.float32)
    sel = np.zeros((128, 4), np.float32)
    sel[127, 0] = 1.0
    sel[:, 1] = 1.0
    wkvq = f32c(np.concatenate([np.asarray(Wk).T, np.asarray(Wv).T, np.asarray(Wq).T], 1))
    waet = f32c(np.concatenate([np.asarray(aW).T, np.asarray(eW).T, np.asarray(tW).T], 1))
    baet = f32c(np.stack([np.asarray(ab)[0:128], np.asarray(ab)[128:256],
                          np.asarray(eb)[0:128], np.asarray(eb)[128:256],
                          np.asarray(tb_)[0:128], np.asarray(tb_)[128:256]], 1))
    dwc = f32c(np.stack([np.asarray(ckw)[0:128], np.asarray(ckw)[128:256],
                         np.asarray(cvw)[0:128], np.asarray(cvw)[128:256],
                         np.asarray(cqw)[0:128], np.asarray(cqw)[128:256]], 1))
    dwb = f32c(np.stack([np.asarray(ckb)[0:128], np.asarray(ckb)[128:256],
                         np.asarray(cvb)[0:128], np.asarray(cvb)[128:256],
                         np.asarray(cqb)[0:128], np.asarray(cqb)[128:256]], 1))
    in_common = dict(wkvq=wkvq, waet=waet, baet=baet, dwc=dwc, dwb=dwb,
                     mw1=f32c(np.asarray(mW1).T), mw2=f32c(np.asarray(mW2).T),
                     wo=f32c(np.asarray(Wo).T), gw=f32c(np.asarray(gW).T),
                     gb=f32c(np.asarray(gb)[None, :]), ng=f32c(np.asarray(norm_g)[None, :]),
                     eye=eye, triu=triu, tril=tril, sel=sel)
    in_maps = []
    for core in range(NCORES):
        b, hf = core // 2, core % 2
        im = dict(in_common)
        im["x"] = f32c(x_full[b])
        in_maps.append(im)
    res = run_bass_kernel_spmd(_S3_NC, in_maps, core_ids=list(range(NCORES)))
    y = np.empty((B, T, C), np.float32)
    for b in range(B):
        y[b] = res.results[2 * b]["y"]
    return y
